# version 31
# baseline (speedup 1.0000x reference)
import numpy as np
import concourse.bass as bass
import concourse.mybir as mybir
from concourse.bass_utils import run_bass_kernel_spmd

F32 = mybir.dt.float32
BF16 = mybir.dt.bfloat16
U8 = mybir.dt.uint8
AF = mybir.ActivationFunctionType
ALU = mybir.AluOpType
AX = mybir.AxisListType

DEBUG_TAPS = False
B, L, D, HID = 8, 2048, 1024, 4096
KC = D // 128
HC = HID // 128
NT = 512
NNT = L // NT
EPS_LN = 1e-5
EPS_Q = 1e-5
QB = 127.0

ENGINES = ("sync", "vector", "scalar", "tensor", "gpsimd")


class Prog:

    def __init__(self):
        self.ops = {e: [] for e in ENGINES}
        self.cnt = {}
        self.sems = None

    def sem(self, name):
        self.cnt[name] = 0

    def inc(self, name, n=1):
        self.cnt[name] += n
        return self.cnt[name]

    def on(self, eng, fn, wait=(), inc=None, dma=False):
        sems = self.sems
        serialized = (eng in ("vector", "scalar", "gpsimd")) and not dma
        for (s, v) in wait:
            if v > 0:
                self.ops[eng].append(lambda E, s=s, v=v: E.wait_ge(sems[s], v))
        if inc is not None:
            sname, amt = inc
            val = self.inc(sname, amt)
            if fn is None:
                self.ops[eng].append(
                    lambda E, s=sname, a=amt: E.sem_inc(sems[s], a))
            elif serialized:
                def run(E, f=fn, s=sname, a=amt):
                    f(E)
                    E.ser_flush_wait()
                    E.sem_inc(sems[s], a)
                self.ops[eng].append(run)
            else:
                self.ops[eng].append(
                    lambda E, f=fn, s=sname, a=amt: f(E).then_inc(sems[s], a))
            return val
        assert fn is not None
        self.ops[eng].append(fn)
        return None

    def barrier(self):
        for e in ENGINES:
            self.inc("sBR", 1)
        target = self.cnt["sBR"]
        sems = self.sems
        for e in ENGINES:
            self.ops[e].append(lambda E: E.sem_inc(sems["sBR"], 1))
            self.ops[e].append(lambda E: E.wait_ge(sems["sBR"], target))


SER_METHS = {"tensor_copy", "tensor_tensor", "tensor_scalar",
              "tensor_single_scalar", "scalar_tensor_tensor",
              "tensor_tensor_scan", "tensor_reduce", "select", "reciprocal",
              "memset", "memzero", "activation", "sign", "square", "sqrt",
              "mul", "add", "copy", "tensor_add", "tensor_sub", "tensor_mul",
              "tensor_max", "tensor_relu", "tensor_scalar_add",
              "tensor_scalar_mul", "tensor_scalar_max", "tensor_scalar_min"}


class Ser:

    def __init__(self, eng, sem):
        self._e, self._s, self._n = eng, sem, 0

    def ser_flush_wait(self):
        if self._n:
            self._e.wait_ge(self._s, self._n)

    def __getattr__(self, m):
        f = getattr(self._e, m)
        if m in SER_METHS:
            def w(*a, **k):
                if self._n:
                    self._e.wait_ge(self._s, self._n)
                ins = f(*a, **k)
                self._n += 1
                return ins.then_inc(self._s, 1)
            return w
        return f


def build_nc():
    nc = bass.Bass("TRN2", target_bir_lowering=False, debug=False)
    p = Prog()

    def din(name, shape, dt=F32):
        return nc.dram_tensor(name, shape, dt, kind="ExternalInput").ap()

    xT = din("xT", [KC, 128, L])
    cpack_f = din("cpack_f", [128, 84 + 128])
    cpack_b = din("cpack_b", [128, 256], BF16)
    wfT = din("wfT", [128, KC * D])
    wiT = din("wiT", [128, KC * D])
    whT = din("whT", [128, KC * D])
    w1T = din("w1T", [HC, 128, KC * 128])
    w2T = din("w2T", [KC, 128, HC * 128])
    wmodT = din("wmodT", [12, 128, 4096])
    idnf = din("idnf", [128, 128])

    out_img = nc.dram_tensor("out_img", [KC, 128, L], F32,
                             kind="ExternalOutput").ap()
    if DEBUG_TAPS:
        tap_hln = nc.dram_tensor("tap_hln", [128, KC * L], BF16,
                                 kind="ExternalOutput").ap()
        tap_x2 = nc.dram_tensor("tap_x2", [128, KC * L], BF16,
                                kind="ExternalOutput").ap()
        tap_xq = nc.dram_tensor("tap_xq", [128, KC * L], BF16,
                                kind="ExternalOutput").ap()
        tap_stm = nc.dram_tensor("tap_stm", [128, 64], F32,
                                 kind="ExternalOutput").ap()
        tap_mod = nc.dram_tensor("tap_mod", [128, 48], F32,
                                 kind="ExternalOutput").ap()
        tap_bs = nc.dram_tensor("tap_bs", [128, 3 * L], BF16,
                                kind="ExternalOutput").ap()
        tap_h1 = nc.dram_tensor("tap_h1", [128, L], BF16,
                                kind="ExternalOutput").ap()
        tap_f = nc.dram_tensor("tap_f", [128, L], BF16,
                               kind="ExternalOutput").ap()
        tap_v = nc.dram_tensor("tap_v", [128, L], BF16,
                               kind="ExternalOutput").ap()
        tap_hm = nc.dram_tensor("tap_hm", [128, L], F32,
                                kind="ExternalOutput").ap()
    h1sp = nc.dram_tensor("h1sp", [HC, 128, L], BF16, kind="Internal").ap()
    h1qsp = nc.dram_tensor("h1qsp", [HC, 128, L], BF16, kind="Internal").ap()
    w2qsp = nc.dram_tensor("w2qsp", [HC, 128, KC * 128], BF16,
                           kind="Internal").ap()

    def sb(name, shape, dt):
        return nc.alloc_sbuf_tensor(name, shape, dt).ap()

    X = sb("X", [128, KC * L], BF16)
    Hh = sb("Hh", [128, KC * L], BF16)
    X2 = sb("X2", [128, KC * L], BF16)
    SC = [sb(f"SC{i}", [128, L], F32) for i in range(5)]
    MKm = sb("MKm", [128, L], U8)
    MB = sb("MB", [128, L], BF16)
    RB = sb("RB", [128, L], BF16)
    M1 = sb("M1", [128, L], BF16)
    WSTG = sb("WSTG", [128, 4096], F32)
    WSB = sb("WSB", [128, 4096], BF16)
    W1Q = sb("W1Q", [128, 2048], BF16)
    B127 = sb("B127", [128, L], BF16)
    BS1 = sb("BS1", [128, L], BF16)
    BS2 = sb("BS2", [128, L], BF16)
    ROW1 = sb("ROW1", [1, L], BF16)
    ROW2 = sb("ROW2", [1, L], BF16)
    SCf = sb("SCf", [128, KC], F32)
    TMPa = sb("TMPa", [128, KC], F32)
    MODC = sb("MODC", [128, 48], F32)
    GA = sb("GAt", [128, KC], F32)
    GB = sb("GBt", [128, KC], F32)
    CA1 = sb("CA1", [128, KC], F32)
    CB1 = sb("CB1", [128, KC], F32)
    NBF = sb("NBF", [128, KC], F32)
    NBI = sb("NBI", [128, KC], F32)
    B2G = sb("B2G", [128, KC], F32)
    IDNF = sb("IDNF", [128, 128], F32)
    STM = sb("STM", [128, 16], F32)
    STMH = sb("STMH", [128, 16], F32)
    VB127 = sb("VB127", [128, 16], F32)
    VBS1 = sb("VBS1", [128, 16], F32)
    VBS2 = sb("VBS2", [128, 16], F32)
    WS1i = sb("WS1i", [128, 1], F32)
    WS1n = sb("WS1n", [128, 1], F32)
    WS2i = sb("WS2i", [128, 1], F32)
    WS2n = sb("WS2n", [128, 1], F32)
    WS1v = sb("WS1v", [128, 1], F32)
    WS2v = sb("WS2v", [128, 1], F32)
    WSrow = sb("WSrow", [1, 8], F32)
    CPF = sb("CPF", [128, 84 + 128], F32)
    CPB = sb("CPB", [128, 256], BF16)

    XS = lambda a, b: X[:, a:b]
    QUU, QAU = XS(0, 2048), XS(2048, 4096)
    QY, QR0 = XS(4096, 6144), XS(6144, 8192)
    QRA, QRB = XS(8192, 10240), XS(10240, 12288)
    QT, QNA = XS(12288, 14336), XS(14336, 16384)
    QSG = RB[:, 0:2048]
    H1Tp = [XS(0, 2048), XS(2048, 4096)]
    TRN2A = XS(4096, 6144)
    TRN2B = XS(6144, 8192)
    HP = [XS(6144, 6656), XS(6656, 7168)]
    H1Qp = [XS(7168, 7680), XS(7680, 8192)]
    W2QS = [XS(8192, 9216), XS(9216, 10240)]
    R0t = [XS(10240, 10752), XS(10752, 11264)]
    Rta = [XS(11264, 11776), XS(11776, 12288)]
    Rtb = [XS(12288, 12800), XS(12800, 13312)]
    Tt = [XS(13312, 13824), XS(13824, 14336)]
    UUt = [XS(14336, 14848), XS(14848, 15360)]
    AUt = [XS(15360, 15872), XS(15872, 16384)]
    NAt = [RB[:, 0:512], RB[:, 512:1024]]
    SGt = [RB[:, 1024:1536], RB[:, 1536:2048]]
    OUTP = [SC[2][:, 0:512], SC[2][:, 512:1024]]

    CVt = CPF[:, 0:8]
    BFVt = CPF[:, 8:16]
    BIVt = CPF[:, 16:24]
    BHVt = CPF[:, 24:32]
    B1Vt = CPF[:, 32:64]
    B2Vt = CPF[:, 64:72]
    CONSTV = CPF[:, 72:84]
    ONESRf = CPF[0:1, 84:212]
    _cvals = [EPS_LN, 1.0, 0.5, -1.0, -0.5, 2.0, 128.0, -127.5, -64.0,
              0.0, 0.0, 0.0]
    for _j, _v in enumerate(_cvals):
        nc.const_aps.register(F32, _v, CONSTV[:, 72 - 72 + _j:_j + 1]
                              if False else CPF[:, 72 + _j:72 + _j + 1])
    ONES128 = CPB[:, 0:128]
    IDN = CPB[:, 128:256]
    ONESRb = CPB[0:1, 0:128]
    ONESCb = CPB[:, 0:1]
    ONE1b = CPB[0:1, 0:1]

    PB = [nc.alloc_psum_tensor(f"PB{i}", [128, NT], F32).ap() for i in range(8)]

    semnames = ["sBR", "sMD", "sM1", "sM2", "sM3", "sLXa", "sLXb", "sCN", "sWMa", "sWMb", "sLWa", "sLWb",
                "sMW", "sW1a", "sW1b", "sGRa", "sGRb", "sW2", "sW2b", "sW2c", "sW2d", "sLWc", "sLWd", "sSP", "sSPa", "sSPb", "sRDa", "sRDb",
                "sFINa", "sFINb", "sMM", "sDV", "sAC", "sPL",
                "sVq", "sSq", "sPq", "sTAP", "sH1a", "sH1b", "sW1c", "sW1d"]
    sems = {s: nc.alloc_semaphore(s) for s in semnames}
    p.sems = sems
    for s in semnames:
        p.sem(s)

    chunk = lambda t, k: t[:, k * L:(k + 1) * L]
    ntile = lambda t, k, j: t[:, k * L + j * NT:k * L + (j + 1) * NT]
    slot = lambda s: MODC[:, s * 8:(s + 1) * 8]

    p.on("scalar", lambda E: E.dma_start(out=CPF[:], in_=cpack_f),
         inc=("sCN", 16), dma=True)
    p.on("scalar", lambda E: E.dma_start(out=CPB[:], in_=cpack_b),
         inc=("sCN", 16), dma=True)
    p.on("scalar", lambda E: E.dma_start(out=IDNF[:], in_=idnf),
         inc=("sCN", 16), dma=True)
    cn_all = p.cnt["sCN"]

    xstg = [SC[0], SC[1]]
    for k in range(KC):
        w = [("sDV", p.cnt["sDV"])] if k >= 2 else []
        sx = "sLXa" if k % 2 == 0 else "sLXb"
        p.on("sync", lambda E, k=k: E.dma_start(out=xstg[k % 2], in_=xT[k]),
             wait=w, inc=(sx, 16))
        p.on("vector", lambda E, k=k: E.tensor_copy(chunk(X, k), xstg[k % 2]),
             wait=[(sx, p.cnt[sx])], inc=("sDV", 1))
    xcast_done = p.cnt["sDV"]

    p.on("scalar", lambda E: E.activation(SCf[:], CVt[:], AF.Silu),
         wait=[("sCN", cn_all)], inc=("sAC", 1))
    ac_silu = p.cnt["sAC"]

    mod_dv_base = p.cnt["sDV"]
    mod_ev = {}
    modstg = [WSTG[:, 0:4096], None]

    def mod_piece(j, k):
        if j % 2 == 0:
            return WSTG[:, k * 512:(k + 1) * 512]
        t = SC[2] if k < 4 else SC[3]
        return t[:, (k % 4) * 512:(k % 4 + 1) * 512]

    def emit_mod_block(j):
        pp = j % 2
        if j >= 2:
            p.on("sync", lambda E, v=j - 1: E.wait_ge(sems["sMD"], v))
        swm = "sWMa" if pp == 0 else "sWMb"
        if pp == 0:
            p.on("sync", lambda E, j=j: E.dma_start(
                out=WSTG[:, 0:4096], in_=wmodT[j]), inc=(swm, 16))
        else:
            p.on("sync", lambda E, j=j: E.dma_start(
                out=SC[2][:], in_=wmodT[j][:, 0:2048]), inc=(swm, 16))
            p.on("sync", lambda E, j=j: E.dma_start(
                out=SC[3][:], in_=wmodT[j][:, 2048:4096]), inc=(swm, 16))
        wm_now = p.cnt[swm]
        for k in range(KC):
            w = []
            if k == 0:
                w = [(swm, wm_now), ("sAC", ac_silu)]
                if j >= 2:
                    w.append(("sDV", mod_ev[j - 1]))
            p.on("tensor", lambda E, j=j, k=k: E.matmul(
                PB[j % 2][0:1, :], SCf[:, k:k + 1], mod_piece(j, k),
                start=(k == 0), stop=(k == KC - 1)),
                wait=w, inc=("sMD", 1) if k == KC - 1 else None)
        mm_now = p.cnt["sMD"]
        if j < 4:
            dstrow = MB[0:1, j * 512:(j + 1) * 512]
        elif j < 8:
            dstrow = RB[0:1, (j - 4) * 512:(j - 4 + 1) * 512]
        else:
            dstrow = M1[0:1, (j - 8) * 512:(j - 8 + 1) * 512]
        p.on("vector", lambda E, d=dstrow, j=j: E.tensor_copy(
            d, PB[j % 2][0:1, :]), wait=[("sMD", mm_now)], inc=("sDV", 1))
        mod_ev[j] = p.cnt["sDV"]


    wabs_stage = [SC[4][:, 0:1024], SC[4][:, 1024:2048],
                  SC[3][:, 0:1024], SC[3][:, 1024:2048]]
    wabs_b = [WSB[:, i * 1024:(i + 1) * 1024] for i in range(4)]
    w2abs_stage = [SC[0], SC[1], WSTG[:, 0:2048], WSTG[:, 2048:4096]]
    w2abs_b = [X2[:, 6144 + i * 1024:6144 + (i + 1) * 1024] for i in range(4)]
    w2actn = {}
    w1actn = {}
    w1mm = {}
    w2mm_abs = {}

    def emit_w1_slab(s):
        if s < 10:
            st = s % 2
            w = [("sAC", w1actn[s - 2])] if s >= 2 else []
        elif s < 12:
            st = s % 4
            w = [("sMD", 12)]
        else:
            st = s % 4
            w = [("sAC", w1actn[s - 4])]
        slw = ["sLWa", "sLWb", "sLWc", "sLWd"][s % 4]
        p.on("gpsimd", lambda E, s=s, st=st: E.dma_start(
            out=wabs_stage[st], in_=w1T[s]),
             wait=w, inc=(slw, 16), dma=True)
        ob = s % 4
        wsMM = [("sM1", w1mm[s - 4])] if s >= 4 else []
        p.on("scalar", lambda E, s=s, st=st, ob=ob: E.activation(
            wabs_b[ob], wabs_stage[st], AF.Abs),
            wait=[(slw, p.cnt[slw])] + wsMM, inc=("sAC", 1))
        acn = p.cnt["sAC"]
        w1actn[s] = acn
        for h in range(2):
            p.on("tensor", lambda E, s=s, h=h, ob=ob: E.matmul(
                PB[4 + h][0:1, :], ONESCb[:],
                wabs_b[ob][:, h * 512:(h + 1) * 512],
                start=(s == 0), stop=(s == HC - 1)),
                wait=[("sAC", acn)] if h == 0 else [],
                inc=("sM1", 1))
        w1mm[s] = p.cnt["sM1"]

    def emit_w2_unit(t):
        if t < 2:
            st = t % 2
            w2w = [("sDV", xcast_done)]
        elif t < 6:
            st = t % 2
            w2w = [("sAC", w2actn[t - 2])]
        elif t < 8:
            st = t % 4
            w2w = [("sMD", 11)]
        else:
            st = t % 4
            w2w = [("sAC", w2actn[t - 4])]
        sw2 = ["sW2", "sW2b", "sW2c", "sW2d"][t % 4]
        p.on("scalar", lambda E, t=t, st=st: E.dma_start(
            out=w2abs_stage[st],
            in_=w2T[t // 2][:, (t % 2) * 2048:(t % 2) * 2048 + 2048]),
            wait=w2w, inc=(sw2, 16), dma=True)
        w2n = p.cnt[sw2]
        for h in range(2):
            a = 2 * t + h
            ob = a % 4
            wv2 = [(sw2, w2n)] if h == 0 else []
            if a >= 4:
                wv2 += [("sM2", w2mm_abs[a - 4])]
            p.on("scalar", lambda E, t=t, h=h, st=st, ob=ob: E.activation(
                w2abs_b[ob], w2abs_stage[st][:, h * 1024:(h + 1) * 1024],
                AF.Abs), wait=wv2, inc=("sAC", 1))
            a2 = p.cnt["sAC"]
            for q in range(2):
                p.on("tensor", lambda E, t=t, h=h, q=q, ob=ob: E.matmul(
                    PB[6][0:1, :], ONESCb[:],
                    w2abs_b[ob][:, q * 512:(q + 1) * 512],
                    start=(t == 0 and h == 0 and q == 0),
                    stop=(t == 15 and h == 1 and q == 1)),
                    wait=[("sAC", a2)] if q == 0 else [],
                    inc=("sM2", 1))
            w2mm_abs[a] = p.cnt["sM2"]
        w2actn[t] = p.cnt["sAC"]

    for j in range(12):
        emit_mod_block(j)
        if j < 10:
            emit_w1_slab(j)
        if j < 6:
            emit_w2_unit(j)
    mod_rows_done = p.cnt["sDV"]

    def modr_piece(t):
        if t < 16:
            return MB[0:1, t * 128:(t + 1) * 128]
        if t < 32:
            return RB[0:1, (t - 16) * 128:(t - 16) * 128 + 128]
        return M1[0:1, (t - 32) * 128:(t - 32) * 128 + 128]

    for t in range(48):
        w = [("sDV", mod_rows_done), ("sCN", cn_all)] if t == 0 else []
        p.on("tensor", lambda E, t=t: E.matmul(
            PB[2][:, t:t + 1], modr_piece(t), ONE1b[:],
            start=True, stop=True), wait=w,
            inc=("sMD", 1) if t == 47 else None)
    mm_modc = p.cnt["sMD"]
    p.on("vector", lambda E: E.tensor_copy(MODC[:], PB[2][:, 0:48]),
         wait=[("sMD", mm_modc)], inc=("sDV", 1))
    modc_done = p.cnt["sDV"]

    def derive(E):
        E.tensor_scalar(CA1[:], slot(1), 1.0, None, ALU.add)
        E.tensor_scalar(CB1[:], slot(4), 1.0, None, ALU.add)
        E.tensor_scalar(GA[:], slot(2), 0.0, 2.0, ALU.is_ge, ALU.mult)
        E.tensor_scalar(GA[:], GA[:], -1.0, None, ALU.add)
        E.tensor_scalar(TMPa[:], slot(2), -1.0, None, ALU.mult)
        E.tensor_tensor(TMPa[:], TMPa[:], slot(2), ALU.max)
        E.tensor_scalar(TMPa[:], TMPa[:], 0.25, None, ALU.max)
        E.tensor_tensor(GA[:], GA[:], TMPa[:], ALU.mult)
        E.tensor_scalar(GB[:], slot(5), 0.0, 2.0, ALU.is_ge, ALU.mult)
        E.tensor_scalar(GB[:], GB[:], -1.0, None, ALU.add)
        E.tensor_scalar(TMPa[:], slot(5), -1.0, None, ALU.mult)
        E.tensor_tensor(TMPa[:], TMPa[:], slot(5), ALU.max)
        E.tensor_scalar(TMPa[:], TMPa[:], 0.25, None, ALU.max)
        E.tensor_tensor(GB[:], GB[:], TMPa[:], ALU.mult)
        E.tensor_scalar(NBF[:], BFVt[:], -1.0, None, ALU.mult)
        E.tensor_scalar(NBI[:], BIVt[:], -1.0, None, ALU.mult)
        return E.tensor_tensor(B2G[:], GB[:], B2Vt[:], ALU.mult)
    p.on("vector", derive, wait=[("sDV", modc_done)], inc=("sDV", 1))

    for r in range(11):
        emit_w1_slab(10 + 2 * r)
        emit_w1_slab(11 + 2 * r)
        if r < 10:
            emit_w2_unit(6 + r)
    mm_w1abs = ("sM1", p.cnt["sM1"])
    mm_w2abs = ("sM2", p.cnt["sM2"])


    def ws1_fin(E):
        E.tensor_reduce(WSrow[0:1, 0:1], PB[4][0:1, :], AX.X, ALU.add)
        E.tensor_reduce(WSrow[0:1, 4:5], PB[5][0:1, :], AX.X, ALU.add)
        E.tensor_tensor(WSrow[0:1, 0:1], WSrow[0:1, 0:1], WSrow[0:1, 4:5],
                        ALU.add)
        E.tensor_scalar(WSrow[0:1, 2:3], WSrow[0:1, 0:1],
                        1.0 / (HID * D), None, ALU.mult)
        return E.tensor_scalar(WSrow[0:1, 2:3], WSrow[0:1, 2:3], EPS_Q,
                               None, ALU.max)
    p.on("vector", ws1_fin, wait=[mm_w1abs], inc=("sDV", 1))
    ws1_done = p.cnt["sDV"]

    def ws2_fin(E):
        E.tensor_reduce(WSrow[0:1, 1:2], PB[6][0:1, :], AX.X, ALU.add)
        E.tensor_scalar(WSrow[0:1, 3:4], WSrow[0:1, 1:2],
                        1.0 / (HID * D), None, ALU.mult)
        return E.tensor_scalar(WSrow[0:1, 3:4], WSrow[0:1, 3:4], EPS_Q,
                               None, ALU.max)
    p.on("vector", ws2_fin, wait=[mm_w2abs], inc=("sDV", 1))
    ws2_done = p.cnt["sDV"]

    p.on("tensor", lambda E: E.matmul(PB[7][:, 0:1], ONESRf[:],
                                      WSrow[0:1, 2:3], start=True, stop=True),
         wait=[("sDV", ws2_done)])
    p.on("tensor", lambda E: E.matmul(PB[7][:, 1:2], ONESRf[:],
                                      WSrow[0:1, 3:4], start=True, stop=True),
         inc=("sMM", 1))
    mm_wsb = p.cnt["sMM"]

    def ws_vecs(E):
        E.tensor_copy(WS1v[:], PB[7][:, 0:1])
        E.tensor_copy(WS2v[:], PB[7][:, 1:2])
        E.reciprocal(WS1i[:], WS1v[:])
        E.reciprocal(WS2i[:], WS2v[:])
        E.tensor_scalar(WS1n[:], WS1i[:], -1.0, None, ALU.mult)
        return E.tensor_scalar(WS2n[:], WS2i[:], -1.0, None, ALU.mult)
    p.on("vector", ws_vecs, wait=[("sMM", mm_wsb)], inc=("sDV", 1))

    for k in range(KC):
        wv = [("sDV", modc_done)] if k == 0 else []
        for j in range(NNT):
            p.on("tensor", lambda E, k=k, j=j: E.matmul(
                PB[j][0:1, :], ONESCb[:], ntile(X, k, j),
                start=(k == 0), stop=(k == KC - 1)),
                wait=wv if j == 0 else [],
                inc=("sM3", 1) if (k == KC - 1 and j == NNT - 1) else None)
    mm_sum1 = p.cnt["sM3"]

    def ln1_mean(E):
        for j in range(NNT):
            E.tensor_scalar(ROW1[0:1, j * 512:(j + 1) * 512],
                            PB[j][0:1, :], 1.0 / D, None, ALU.mult)
        last = None
        for j in range(NNT):
            last = E.tensor_tensor(ROW2[0:1, j * 512:(j + 1) * 512],
                                   ROW1[0:1, j * 512:(j + 1) * 512],
                                   ROW1[0:1, j * 512:(j + 1) * 512],
                                   ALU.mult)
        return last
    p.on("vector", ln1_mean, wait=[("sM3", mm_sum1)], inc=("sDV", 1))

    sq1 = [X2[:, 2048:4096], X2[:, 4096:6144]]
    st1_mm = {}
    for k in range(KC):
        wv = [("sM3", st1_mm[k - 2])] if k >= 2 else []
        p.on("vector", lambda E, k=k: E.tensor_tensor(
            sq1[k % 2], chunk(X, k), chunk(X, k), ALU.mult),
            wait=wv, inc=("sDV", 1))
        sqdv = p.cnt["sDV"]
        for j in range(NNT):
            p.on("tensor", lambda E, k=k, j=j: E.matmul(
                PB[j][0:1, :], ONESCb[:],
                sq1[k % 2][:, j * 512:(j + 1) * 512],
                start=(k == 0), stop=(k == KC - 1)),
                wait=[("sDV", sqdv)] if j == 0 else [],
                inc=("sM3", 1) if j == NNT - 1 else None)
        st1_mm[k] = p.cnt["sM3"]
    mm_sq1 = p.cnt["sM3"]

    def ln1_var(E):
        for j in range(NNT):
            E.tensor_scalar(PB[j][0:1, :], PB[j][0:1, :], 1.0 / D,
                            None, ALU.mult)
        last = None
        for j in range(NNT):
            last = E.tensor_tensor(PB[j][0:1, :], PB[j][0:1, :],
                                   ROW2[0:1, j * 512:(j + 1) * 512],
                                   ALU.subtract)
        return last
    p.on("vector", ln1_var, wait=[("sM3", mm_sq1)], inc=("sDV", 1))

    def ln1_rstd(E):
        for j in range(NNT):
            E.activation(SC[2][0:1, j * 512:(j + 1) * 512], PB[j][0:1, :],
                         AF.Ln, bias=EPS_LN)
        last = None
        for j in range(NNT):
            last = E.activation(ROW2[0:1, j * 512:(j + 1) * 512],
                                SC[2][0:1, j * 512:(j + 1) * 512],
                                AF.Exp, scale=-0.5)
        return last
    p.on("scalar", ln1_rstd, wait=[("sDV", p.cnt["sDV"])], inc=("sAC", 1))
    rstd1_ac = p.cnt["sAC"]

    for j in range(NNT):
        p.on("tensor", lambda E, j=j: E.matmul(
            PB[j][:, :], ONESRb[:], ROW1[0:1, j * 512:(j + 1) * 512],
            start=True, stop=True),
            wait=[("sAC", rstd1_ac)] if j == 0 else [],
            inc=("sM3", 1) if j == NNT - 1 else None)
    bcw1 = p.cnt["sM3"]

    def bc1_evac(E):
        last = None
        for j in range(NNT):
            last = E.activation(MB[:, j * 512:(j + 1) * 512], PB[j][:, :],
                                AF.Identity)
        return last
    p.on("scalar", bc1_evac, wait=[("sM3", bcw1)], inc=("sAC", 1))
    bc1e_ac = p.cnt["sAC"]
    for j in range(NNT):
        p.on("tensor", lambda E, j=j: E.matmul(
            PB[j][:, :], ONESRb[:], ROW2[0:1, j * 512:(j + 1) * 512],
            start=True, stop=True),
            wait=[("sAC", bc1e_ac)] if j == 0 else [],
            inc=("sM3", 1) if j == NNT - 1 else None)
    bcw2 = p.cnt["sM3"]

    def bc2_evac(E):
        last = None
        for j in range(NNT):
            last = E.activation(RB[:, j * 512:(j + 1) * 512], PB[j][:, :],
                                AF.Identity)
        return last
    p.on("scalar", bc2_evac, wait=[("sM3", bcw2)], inc=("sAC", 1))
    p.on("vector", None, wait=[("sAC", p.cnt["sAC"])], inc=("sDV", 1))
    bc1_done = p.cnt["sDV"]

    for k in range(KC):
        wv = [("sDV", bc1_done)] if k == 0 else [("sAC", p.cnt["sAC"])]

        def ln_apply_v(E, k=k, src=X):
            E.tensor_tensor(M1[:], chunk(src, k), MB[:], ALU.subtract)
            return E.tensor_tensor(X2[:, 0:L], M1[:], RB[:], ALU.mult)
        p.on("vector", ln_apply_v, wait=wv, inc=("sDV", 1))
        p.on("scalar", lambda E, k=k: E.activation(
            chunk(Hh, k), X2[:, 0:L], AF.Identity,
            bias=slot(0)[:, k:k + 1], scale=CA1[:, k:k + 1]),
            wait=[("sDV", p.cnt["sDV"])], inc=("sAC", 1))

    sq_slab = [WSB[:, 0:2048], WSB[:, 2048:4096]]

    def ln_stats(src_tile):
        mm_chunk_vals = []
        for k in range(KC):
            wv = []
            if k >= 2:
                wv = [("sMM", mm_chunk_vals[k - 2])]
            p.on("scalar", lambda E, k=k, s=src_tile: E.activation(
                sq_slab[k % 2], chunk(s, k), AF.Square),
                wait=wv, inc=("sAC", 1))
            acn = p.cnt["sAC"]
            for j in range(NNT):
                p.on("tensor", lambda E, k=k, j=j, s=src_tile: E.matmul(
                    PB[j][0:1, :], ONESCb[:], ntile(s, k, j),
                    start=(k == 0), stop=(k == KC - 1)))
            for j in range(NNT):
                p.on("tensor", lambda E, k=k, j=j: E.matmul(
                    PB[4 + j][0:1, :], ONESCb[:],
                    sq_slab[k % 2][:, j * 512:(j + 1) * 512],
                    start=(k == 0), stop=(k == KC - 1)),
                    wait=[("sAC", acn)] if j == 0 else [],
                    inc=("sMM", 1) if j == NNT - 1 else None)
            mm_chunk_vals.append(p.cnt["sMM"])
        return p.cnt["sMM"]

    def ln_rows(E):
        for j in range(NNT):
            E.tensor_scalar(ROW1[0:1, j * 512:(j + 1) * 512],
                            PB[j][0:1, :], 1.0 / D, None, ALU.mult)
        for j in range(NNT):
            E.tensor_scalar(PB[4 + j][0:1, :], PB[4 + j][0:1, :],
                            1.0 / D, None, ALU.mult)
            E.tensor_tensor(ROW2[0:1, j * 512:(j + 1) * 512],
                            ROW1[0:1, j * 512:(j + 1) * 512],
                            ROW1[0:1, j * 512:(j + 1) * 512], ALU.mult)
        last = None
        for j in range(NNT):
            last = E.tensor_tensor(PB[4 + j][0:1, :], PB[4 + j][0:1, :],
                                   ROW2[0:1, j * 512:(j + 1) * 512],
                                   ALU.subtract)
        return last

    def ln_rstd(E):
        for j in range(NNT):
            E.activation(PB[j][0:1, :], PB[4 + j][0:1, :], AF.Ln, bias=EPS_LN)
        last = None
        for j in range(NNT):
            last = E.activation(ROW2[0:1, j * 512:(j + 1) * 512],
                                PB[j][0:1, :], AF.Exp, scale=-0.5)
        return last

    def ln_bcast():
        ac_r = p.cnt["sAC"]
        for j in range(NNT):
            p.on("tensor", lambda E, j=j: E.matmul(
                PB[j][:, :], ONESRb[:], ROW1[0:1, j * 512:(j + 1) * 512],
                start=True, stop=True),
                wait=[("sAC", ac_r)] if j == 0 else [])
            p.on("tensor", lambda E, j=j: E.matmul(
                PB[4 + j][:, :], ONESRb[:], ROW2[0:1, j * 512:(j + 1) * 512],
                start=True, stop=True),
                inc=("sMM", 1) if j == NNT - 1 else None)
        mm_bc = p.cnt["sMM"]

        def bc_evac(E):
            for j in range(NNT):
                E.activation(MB[:, j * 512:(j + 1) * 512], PB[j][:, :],
                             AF.Identity)
            last = None
            for j in range(NNT):
                last = E.activation(RB[:, j * 512:(j + 1) * 512],
                                    PB[4 + j][:, :], AF.Identity)
            return last
        p.on("scalar", bc_evac, wait=[("sMM", mm_bc)], inc=("sAC", 1))
        p.on("vector", None, wait=[("sAC", p.cnt["sAC"])], inc=("sDV", 1))
        return p.cnt["sDV"]

    if DEBUG_TAPS:
        p.on("gpsimd", lambda E: E.dma_start(out=tap_hln, in_=Hh[:]),
             wait=[("sAC", p.cnt["sAC"])], inc=("sTAP", 16), dma=True)
        p.on("gpsimd", lambda E, v=p.cnt["sTAP"]: E.wait_ge(sems["sTAP"], v))
        p.on("gpsimd", lambda E: E.dma_start(out=tap_mod, in_=MODC[:]),
             inc=("sTAP", 16), dma=True)
        p.on("gpsimd", lambda E, v=p.cnt["sTAP"]: E.wait_ge(sems["sTAP"], v))
    p.barrier()

    BH1p = [M1, B127]
    BH2p = [MB, BS1]
    RLp = [SC[3][:], BS2[:]]
    MNp = [SC[4][:], W1Q[:, 0:L]]
    evac_vals = {}

    chain_ends = {}

    def emit_mixer_mm_evac(e):
        par = e % 2
        for gi, wimg in enumerate((wfT, wiT, whT)):
            wv = []
            if gi == 0 and e >= 1:
                wv = [("sMM", p.cnt["sMM"])]
            p.on("sync", lambda E, gi=gi, w_=wimg, e=e: E.dma_start(
                out=WSTG[:, gi * 1024:(gi + 1) * 1024],
                in_=w_[:, e * 1024:(e + 1) * 1024]),
                wait=wv, inc=("sMW", 16))
        mw_now = p.cnt["sMW"]

        def slab_cast(E, e=e):
            E.tensor_copy(WSB[:, 0:1024], WSTG[:, 0:1024])
            E.tensor_copy(WSB[:, 1024:2048], WSTG[:, 1024:2048])
            return E.tensor_copy(WSB[:, 2048:3072], WSTG[:, 2048:3072])
        p.on("gpsimd", slab_cast, wait=[("sMW", mw_now)] +
             ([("sAC", p.cnt["sAC"])] if e == 0 else []), inc=("sPL", 1))
        cast_done = ("sPL", p.cnt["sPL"])

        for j in range(NNT):
            pp = 3 * (j % 2)
            wv = [cast_done]
            if j >= 2:
                wv += [("sDV", evac_vals[(e, j - 2, "dv")]),
                       ("sAC", evac_vals[(e, j - 2, "ac")])]
            elif e >= 1:
                wv += [("sDV", evac_vals[(e - 1, j + 2, "dv")]),
                       ("sAC", evac_vals[(e - 1, j + 2, "ac")])]
            for gi in range(3):
                for k in range(KC):
                    p.on("tensor", lambda E, e=e, j=j, gi=gi, k=k, pp=pp:
                         E.matmul(PB[pp + gi][:, :],
                                  WSB[:, gi * 1024 + k * 128:
                                      gi * 1024 + (k + 1) * 128],
                                  ntile(Hh, k, j),
                                  start=(k == 0), stop=(k == KC - 1)),
                         wait=wv if (gi == 0 and k == 0) else [],
                         inc=("sMM", 1) if k == KC - 1 else None)
            mm_gate = p.cnt["sMM"]

            def evac_act(E, e=e, j=j, pp=pp, par=par):
                E.activation(BH1p[par][:, j * 512:(j + 1) * 512],
                             PB[pp][:, :],
                             AF.Exp, scale=-1.0, bias=NBF[:, e:e + 1])
                return E.activation(BH2p[par][:, j * 512:(j + 1) * 512],
                                    PB[pp + 1][:, :],
                                    AF.Exp, scale=-1.0, bias=NBI[:, e:e + 1])
            ga_g = [("sDV", g_dvs[e - 2])] if (j == 0 and
                                               (e - 2) in g_dvs) else []
            p.on("scalar", evac_act,
                 wait=[("sMM", mm_gate)] + ga_g, inc=("sAC", 1))
            evac_vals[(e, j, "ac")] = p.cnt["sAC"]

            def evac_dve(E, e=e, j=j, pp=pp, par=par):
                E.tensor_scalar(RLp[par][:, j * 512:(j + 1) * 512],
                                PB[pp + 2][:, :], BHVt[:, e:e + 1], 0.0,
                                ALU.add, ALU.max)
                return E.tensor_scalar(MNp[par][:, j * 512:(j + 1) * 512],
                                       PB[pp + 2][:, :], BHVt[:, e:e + 1],
                                       0.0, ALU.add, ALU.min)
            p.on("vector", evac_dve,
                 wait=[("sMM", mm_gate)], inc=("sDV", 1))
            evac_vals[(e, j, "dv")] = p.cnt["sDV"]

    Ap = [SC[1][:], SC[0][:]]
    Bp = [SC[2][:], RB[:]]

    g_dvs = {}

    def emit_chain_pair(e0):
        es = (e0, e0 + 1)

        def arrs(e):
            par = e % 2
            return (Ap[par], Bp[par], BH1p[par][:], BH2p[par][:],
                    RLp[par], MNp[par])

        t_ac = {}
        t_dv = {}
        for e in es:
            A, B, BH1, BH2, RL, MN = arrs(e)
            w0 = [("sAC", evac_vals[(e, NNT - 1, "ac")]),
                  ("sDV", evac_vals[(e, NNT - 1, "dv")])]
            if (e - 2) in chain_ends:
                w0 += [("sDV", chain_ends[e - 2])]
            p.on("scalar", lambda E, A=A, BH2=BH2: E.activation(
                A, BH2, AF.Ln, bias=1.0), wait=w0, inc=("sAC", 1))
        for e in es:
            A, B, BH1, BH2, RL, MN = arrs(e)
            p.on("scalar", lambda E, B=B, BH1=BH1: E.activation(
                B, BH1, AF.Ln, bias=1.0), inc=("sAC", 1))
            t_ac[e] = p.cnt["sAC"]
        for e in es:
            A, B, BH1, BH2, RL, MN = arrs(e)
            p.on("vector", lambda E, A=A, B=B: E.tensor_tensor(
                A, B, A, ALU.subtract),
                wait=[("sAC", t_ac[e])], inc=("sDV", 1))
            t_dv[e] = p.cnt["sDV"]
        for e in es:
            A, B, BH1, BH2, RL, MN = arrs(e)
            p.on("scalar", lambda E, BH1=BH1, MN=MN: E.activation(
                BH1, MN, AF.Sigmoid), inc=("sAC", 1))
            t_ac[e] = p.cnt["sAC"]
        for e in es:
            A, B, BH1, BH2, RL, MN = arrs(e)
            p.on("scalar", lambda E, A=A, B=B: E.activation(
                B, A, AF.Sigmoid, scale=-1.0),
                wait=[("sDV", t_dv[e])], inc=("sAC", 1))
            t_dv[e] = p.cnt["sAC"]
        for e in es:
            A, B, BH1, BH2, RL, MN = arrs(e)
            p.on("vector", lambda E, BH1=BH1, RL=RL: E.tensor_tensor(
                RL, RL, BH1, ALU.add),
                wait=[("sAC", t_ac[e])], inc=("sDV", 1))
            g_dvs[e] = p.cnt["sDV"]
        for e in es:
            A, B, BH1, BH2, RL, MN = arrs(e)
            p.on("vector", lambda E, A=A, B=B: E.tensor_scalar(
                A, B, -1.0, 1.0, ALU.mult, ALU.add),
                wait=[("sAC", t_dv[e])], inc=("sDV", 1))
        for e in es:
            A, B, BH1, BH2, RL, MN = arrs(e)
            p.on("vector", lambda E, A=A, MN=MN, RL=RL: E.tensor_tensor(
                MN, RL, A, ALU.mult), inc=("sDV", 1))
        for e in es:
            A, B, BH1, BH2, RL, MN = arrs(e)

            def scan_epi(E, e=e, B=B, MN=MN, RL=RL):
                E.tensor_tensor_scan(RL, B, MN, 0.0, ALU.mult, ALU.add)
                return E.scalar_tensor_tensor(
                    chunk(X2, e), RL, GA[:, e:e + 1], chunk(X, e),
                    ALU.mult, ALU.add)
            p.on("vector", scan_epi, inc=("sDV", 1))
            chain_ends[e] = p.cnt["sDV"]

    for ep in range(KC // 2):
        if ep >= 1:
            emit_chain_pair(2 * ep - 2)
        emit_mixer_mm_evac(2 * ep)
        emit_mixer_mm_evac(2 * ep + 1)
    emit_chain_pair(KC - 2)
    p.barrier()

    mm_st2 = ln_stats(X2)
    p.on("vector", ln_rows, wait=[("sMM", mm_st2)], inc=("sDV", 1))
    p.on("scalar", ln_rstd, wait=[("sDV", p.cnt["sDV"])], inc=("sAC", 1))
    bc2_done = ln_bcast()

    for k in range(KC):
        wv = [("sDV", bc2_done)] if k == 0 else []

        def h2_v(E, k=k):
            E.tensor_tensor(QUU, chunk(X2, k), MB[:], ALU.subtract)
            return E.tensor_tensor(QAU, QUU, RB[:], ALU.mult)
        p.on("vector", h2_v, wait=wv, inc=("sDV", 1))
        p.on("scalar", lambda E, k=k: E.activation(
            chunk(Hh, k), QAU, AF.Identity,
            bias=slot(3)[:, k:k + 1], scale=CB1[:, k:k + 1]),
            wait=[("sDV", p.cnt["sDV"])], inc=("sAC", 1))
        acn = p.cnt["sAC"]
        M1n = XS(4096, 6144)
        if k == 0:
            p.on("vector", lambda E, M1n=M1n: (
                E.tensor_copy(M1[:], chunk(Hh, 0)),
                E.tensor_copy(M1n, chunk(Hh, 0)))[-1],
                wait=[("sAC", acn)], inc=("sDV", 1))
        else:
            p.on("vector", lambda E, k=k, M1n=M1n: (
                E.tensor_tensor(M1[:], M1[:], chunk(Hh, k), ALU.max),
                E.tensor_tensor(M1n, M1n, chunk(Hh, k), ALU.min))[-1],
                wait=[("sAC", acn)], inc=("sDV", 1))
    M1n = XS(4096, 6144)
    p.on("vector", lambda E, M1n=M1n: (
        E.tensor_scalar(M1n, M1n, -1.0, None, ALU.mult),
        E.tensor_tensor(M1[:], M1[:], M1n, ALU.max))[-1],
        inc=("sDV", 1))
    m1_done = p.cnt["sDV"]

    def make_scales(stm_tile, first_waits):
        p.on("vector", lambda E: E.tensor_copy(SC[0][:], M1[:]),
             wait=list(first_waits), inc=("sDV", 1))
        m1f_done = p.cnt["sDV"]
        for g4 in range(4):
            tgt = PB[6] if g4 % 2 == 0 else PB[7]
            wv = [("sDV", m1f_done)] if g4 == 0 else (
                [("sDV", p.cnt["sDV"])] if g4 >= 2 else [])
            for b4 in range(4):
                blk = g4 * 4 + b4
                p.on("tensor", lambda E, blk=blk, b4=b4, tgt=tgt: E.transpose(
                    tgt[:, b4 * 128:(b4 + 1) * 128],
                    SC[0][:, blk * 128:(blk + 1) * 128], IDNF[:]),
                    wait=wv if b4 == 0 else [],
                    inc=("sMM", 1) if b4 == 3 else None)
            mmn = p.cnt["sMM"]
            p.on("vector", lambda E, g4=g4, tgt=tgt, stm=stm_tile:
                 E.tensor_reduce(
                     stm[:, g4 * 4:(g4 + 1) * 4],
                     tgt[:, 0:512].rearrange("p (b c) -> p b c", b=4),
                     AX.X, ALU.max, apply_absolute_value=True),
                 wait=[("sMM", mmn)], inc=("sDV", 1))
        p.on("vector", lambda E, stm=stm_tile: E.tensor_scalar(
            stm[:], stm[:], EPS_Q, None, ALU.max), inc=("sDV", 1))
        return p.cnt["sDV"]

    stm_done = make_scales(STM, [("sDV", m1_done)])

    def xscale_vecs(E):
        E.reciprocal(STMH[:], STM[:])
        E.tensor_scalar(VB127[:], STMH[:], QB, None, ALU.mult)
        E.tensor_scalar(STMH[:], STM[:], 1.0 / QB, None, ALU.mult)
        return E.tensor_scalar(VBS1[:], STMH[:], WS1v[:, 0:1], None, ALU.mult)
    p.on("vector", xscale_vecs, wait=[("sDV", stm_done)], inc=("sDV", 1))
    xs_done = p.cnt["sDV"]

    def build_bcast(vec_tile, out_tile, after_dv):
        for g4 in range(4):
            pp = g4 % 2
            wv = [("sDV", after_dv)] if g4 == 0 else []
            if g4 >= 2:
                wv += [("sAC", p.cnt["sAC"])]
            for b4 in range(4):
                blk = g4 * 4 + b4
                p.on("vector", lambda E, blk=blk, b4=b4, v=vec_tile, pp=pp:
                     E.tensor_scalar(
                         WSB[:, (pp * 4 + b4) * 128:(pp * 4 + b4) * 128 + 128],
                         IDN[:], v[:, blk:blk + 1], None, ALU.mult),
                     wait=wv if b4 == 0 else [], inc=("sDV", 1))
                dvn = p.cnt["sDV"]
                p.on("tensor", lambda E, blk=blk, b4=b4, pp=pp: E.matmul(
                    PB[pp][:, b4 * 128:(b4 + 1) * 128], ONES128[:],
                    WSB[:, (pp * 4 + b4) * 128:(pp * 4 + b4) * 128 + 128],
                    start=True, stop=True),
                    wait=[("sDV", dvn)],
                    inc=("sMM", 1) if b4 == 3 else None)
            mmn = p.cnt["sMM"]
            p.on("scalar", lambda E, g4=g4, o=out_tile, pp=pp:
                 E.activation(o[:, g4 * 512:(g4 + 1) * 512],
                              PB[pp][:, 0:512], AF.Identity),
                 wait=[("sMM", mmn)], inc=("sAC", 1))
        p.on("vector", None, wait=[("sAC", p.cnt["sAC"])], inc=("sDV", 1))
        return p.cnt["sDV"]

    b127_done = build_bcast(VB127, B127, xs_done)
    bs1_done = build_bcast(VBS1, BS1, b127_done)

    QSET = [(QUU, QY, QRA, QT, QSG), (QAU, QR0, QRB, QNA, MB[:])]
    xact = {}

    def x_u(k):
        U, Y, RA, T, SG = QSET[k % 2]
        p.on("vector", lambda E, k=k, U=U: E.tensor_tensor(
            U, chunk(Hh, k), B127[:], ALU.mult),
            wait=([("sDV", bs1_done)] if k == 0 else []), inc=("sDV", 1))
        p.on("scalar", lambda E, U=U, Y=Y, SG=SG: (
            E.activation(Y, U, AF.Abs),
            E.activation(SG, U, AF.Sign))[-1],
            wait=[("sDV", p.cnt["sDV"])], inc=("sAC", 1))
        xact[k] = p.cnt["sAC"]

    x_u(0)
    for k in range(KC):
        if k + 1 < KC:
            x_u(k + 1)
        U, Y, RA, T, SG = QSET[k % 2]

        def xdve(E, k=k, Y=Y, RA=RA, T=T, SG=SG):
            E.tensor_scalar(Y, Y, 128.0, None, ALU.add)
            E.tensor_scalar(RA, Y, -127.5, None, ALU.add)
            for bit in (64.0, 32.0, 16.0, 8.0, 4.0, 2.0, 1.0):
                E.tensor_scalar(T, RA, bit, bit, ALU.is_ge, ALU.mult)
                E.tensor_tensor(RA, RA, T, ALU.subtract)
            E.scalar_tensor_tensor(T, RA, -1.0, Y, ALU.mult, ALU.add)
            return E.scalar_tensor_tensor(chunk(Hh, k), T, -127.5, SG,
                                          ALU.add, ALU.mult)
        p.on("vector", xdve, wait=[("sAC", xact[k])], inc=("sDV", 1))
    xq_done = p.cnt["sDV"]
    if DEBUG_TAPS:
        p.on("gpsimd", lambda E: E.dma_start(out=tap_xq, in_=Hh[:]),
             wait=[("sDV", p.cnt["sDV"])], inc=("sTAP", 16), dma=True)
        def tapstm(E):
            return E.dma_start(out=tap_stm[:, 0:16], in_=STM[:])
        p.on("gpsimd", tapstm, inc=("sTAP", 16), dma=True)
        p.on("gpsimd", lambda E, v=p.cnt["sTAP"]: E.wait_ge(sems["sTAP"], v))
        p.on("gpsimd", lambda E: E.dma_start(out=tap_bs[:, 0:L], in_=B127[:]),
             inc=("sTAP", 16), dma=True)
        p.on("gpsimd", lambda E, v=p.cnt["sTAP"]: E.wait_ge(sems["sTAP"], v))
        p.on("gpsimd", lambda E: E.dma_start(out=tap_bs[:, L:2 * L], in_=BS1[:]),
             inc=("sTAP", 16), dma=True)
        p.on("gpsimd", lambda E, v=p.cnt["sTAP"]: E.wait_ge(sems["sTAP"], v))

    w1stage = [WSTG[:, i * 1024:(i + 1) * 1024] for i in range(4)]
    w1q = [W1Q[:, 0:1024], W1Q[:, 1024:2048], RB[:, 0:1024], RB[:, 1024:2048]]
    tern_st = [(SC[1][:, 0:1024], SC[1][:, 1024:2048]),
               (SC[3][:, 0:1024], SC[3][:, 1024:2048])]
    tern_dvs = {}
    tern_pl = {}
    hc_mm = {}
    gelu_acs = []
    ffn1_evacs = []
    spill_sps = []

    def emit_tern(hc):
        sp4 = hc % 4
        wv = []
        if hc >= 4:
            wv = [tern_dvs[hc - 4]]
        sw1 = ["sW1a", "sW1b", "sW1c", "sW1d"][sp4]
        p.on("sync", lambda E, hc=hc, sp4=sp4: E.dma_start(
            out=w1stage[sp4], in_=w1T[hc]), wait=wv, inc=(sw1, 16))
        w1n = p.cnt[sw1]
        tp_, tn_ = tern_st[hc % 2]

        def tern_ts(E, sp4=sp4, tp_=tp_, tn_=tn_):
            E.tensor_scalar(tp_, w1stage[sp4], WS1i[:, 0:1], 0.5,
                            ALU.mult, ALU.is_ge)
            return E.tensor_scalar(tn_, w1stage[sp4], WS1n[:, 0:1], 0.5,
                                   ALU.mult, ALU.is_ge)
        tw = [(sw1, w1n)]
        if hc >= 2:
            tw += [("sPL", tern_pl[hc - 2])]
        p.on("vector", tern_ts, wait=tw, inc=("sDV", 1))
        subw = [("sDV", p.cnt["sDV"])]
        if hc >= 4:
            subw += [("sMM", hc_mm[hc - 4])]
        elif hc >= 2:
            subw += [("sDV", xq_done)]
        p.on("gpsimd", lambda E, hc=hc, tp_=tp_, tn_=tn_: E.tensor_tensor(
            w1q[hc % 4], tp_, tn_, ALU.subtract),
            wait=subw, inc=("sPL", 1))
        tern_pl[hc] = p.cnt["sPL"]
        tern_dvs[hc] = ("sPL", p.cnt["sPL"])

    emit_tern(0)
    emit_tern(1)
    for hc in range(HC):
        if hc + 2 < HC:
            emit_tern(hc + 2)
        for j in range(NNT):
            jp = (hc * NNT + j) % 4
            gidx = hc * NNT + j
            for k in range(KC):
                wv = []
                if k == 0:
                    wv = [tern_dvs[hc]]
                    if gidx == 0:
                        wv += [("sDV", xq_done)]
                    if gidx >= 4:
                        wv += [("sDV", ffn1_evacs[gidx - 4])]
                p.on("tensor", lambda E, hc=hc, j=j, k=k, jp=jp:
                     E.matmul(PB[jp][:, :],
                              w1q[hc % 4][:, k * 128:(k + 1) * 128],
                              ntile(Hh, k, j),
                              start=(k == 0), stop=(k == KC - 1)),
                     wait=wv, inc=("sMM", 1) if k == KC - 1 else None)
            mmn = p.cnt["sMM"]
            hc_mm[hc] = mmn
            p.on("vector", lambda E, j=j, jp=jp: E.tensor_tensor(
                SC[2][:, j * 512:(j + 1) * 512], PB[jp][:, :],
                BS1[:, j * 512:(j + 1) * 512], ALU.mult),
                wait=[("sMM", mmn)], inc=("sDV", 1))
            dvn = p.cnt["sDV"]
            ffn1_evacs.append(dvn)
            p.on("scalar", lambda E, hc=hc, j=j: E.activation(
                H1Tp[hc % 2][:, j * 512:(j + 1) * 512],
                SC[2][:, j * 512:(j + 1) * 512], AF.Gelu,
                bias=B1Vt[:, hc:hc + 1]),
                wait=[("sDV", dvn)], inc=("sAC", 1))
        gelu_acs.append(p.cnt["sAC"])
        H1T = H1Tp[hc % 2]
        if hc == 0:
            p.on("vector", lambda E, H1T=H1T: (
                E.tensor_copy(M1[:], H1T),
                E.tensor_copy(MB[:], H1T))[-1],
                 wait=[("sAC", gelu_acs[-1])], inc=("sDV", 1))
        else:
            p.on("vector", lambda E, H1T=H1T: (
                E.tensor_tensor(M1[:], M1[:], H1T, ALU.max),
                E.tensor_tensor(MB[:], MB[:], H1T, ALU.min))[-1],
                 wait=[("sAC", gelu_acs[-1])], inc=("sDV", 1))
        ssp = "sSPa" if hc % 2 == 0 else "sSPb"
        p.on("gpsimd", lambda E, hc=hc, H1T=H1T: E.dma_start(
            out=h1sp[hc], in_=H1T),
             wait=[("sAC", gelu_acs[-1])], inc=(ssp, 16), dma=True)
        spill_sps.append((ssp, p.cnt[ssp]))
        if DEBUG_TAPS and hc == 0:
            p.on("gpsimd", lambda E: E.dma_start(out=tap_h1, in_=H1Tp[0]),
                 inc=("sTAP", 16), dma=True)
            p.on("gpsimd", lambda E, v=p.cnt["sTAP"]: E.wait_ge(sems["sTAP"], v))
        if hc >= 1:
            sn, v = spill_sps[hc - 1]
            p.on("scalar", lambda E, sn=sn, v=v: E.wait_ge(sems[sn], v))
        if DEBUG_TAPS and hc == 0:
            p.on("scalar", lambda E, v=p.cnt["sTAP"]: E.wait_ge(sems["sTAP"], v))
    p.on("vector", lambda E: (
        E.tensor_scalar(MB[:], MB[:], -1.0, None, ALU.mult),
        E.tensor_tensor(M1[:], M1[:], MB[:], ALU.max))[-1],
        inc=("sDV", 1))
    pl_m1h = ("sDV", p.cnt["sDV"])

    w2stg4 = [SC[1][:], SC[2][:], SC[3][:], WSTG[:, 0:2048]]
    w2sem4 = ["sW2", "sW2b", "sW2c", "sW2d"]
    w2u_dv = {}
    w2spl = {}
    for u in range(2 * KC):
        s, h = u // 2, u % 2
        st = u % 4
        if u < 4:
            wv = [("sSPa", p.cnt["sSPa"]), ("sSPb", p.cnt["sSPb"])]
        else:
            wv = [("sDV", w2u_dv[u - 4])]
        sw = w2sem4[st]
        p.on("sync", lambda E, s=s, h=h, st=st: E.dma_start(
            out=w2stg4[st], in_=w2T[s][:, h * 2048:(h + 1) * 2048]),
            wait=wv, inc=(sw, 16))
        w2n = p.cnt[sw]

        ioff = 4096 + (s % 2) * 4096

        def tern2(E, st=st, h=h, ioff=ioff):
            E.tensor_scalar(W1Q[:], w2stg4[st], WS2i[:, 0:1], 0.5,
                            ALU.mult, ALU.is_ge)
            E.tensor_scalar(RB[:], w2stg4[st], WS2n[:, 0:1], 0.5,
                            ALU.mult, ALU.is_ge)
            return E.tensor_tensor(
                X[:, ioff + h * 2048:ioff + (h + 1) * 2048],
                W1Q[:], RB[:], ALU.subtract)
        tv = [(sw, w2n)]
        if h == 0 and s >= 2:
            tv += [w2spl[s - 2]]
        p.on("vector", tern2, wait=tv, inc=("sDV", 1))
        w2u_dv[u] = p.cnt["sDV"]
        if h == 1:
            ssp_s = "sSP" if s % 2 == 0 else "sSPa"
            p.on("gpsimd", lambda E, s=s, ioff=ioff: E.dma_start(
                out=w2qsp[:, :, s * 128:(s + 1) * 128].rearrange(
                    "k p c -> p k c"),
                in_=X[:, ioff:ioff + 4096].rearrange("p (k c) -> p k c",
                                                     k=HC)),
                wait=[("sDV", w2u_dv[u])], inc=(ssp_s, 16), dma=True)
            w2spl[s] = (ssp_s, p.cnt[ssp_s])
    sp_all = p.cnt["sSP"]

    stmh_done = make_scales(STMH, [pl_m1h])

    def hscale_vecs(E):
        E.reciprocal(STM[:], STMH[:])
        E.tensor_scalar(VB127[:], STM[:], QB, None, ALU.mult)
        E.tensor_scalar(STM[:], STMH[:], 1.0 / QB, None, ALU.mult)
        return E.tensor_scalar(VBS2[:], STM[:], WS2v[:, 0:1], None, ALU.mult)
    p.on("vector", hscale_vecs, wait=[("sDV", stmh_done)], inc=("sDV", 1))
    hs_done = p.cnt["sDV"]
    b127h_done = build_bcast(VB127, B127, hs_done)
    bs2_done = build_bcast(VBS2, BS2, b127h_done)
    if DEBUG_TAPS:
        p.on("gpsimd", lambda E: E.dma_start(out=tap_bs[:, 2 * L:3 * L],
                                             in_=BS2[:]),
             wait=[("sDV", bs2_done)], inc=("sTAP", 16), dma=True)
        p.on("gpsimd", lambda E, v=p.cnt["sTAP"]: E.wait_ge(sems["sTAP"], v))
        p.on("gpsimd", lambda E: E.dma_start(out=tap_stm[:, 16:32],
                                             in_=STMH[:]),
             inc=("sTAP", 16), dma=True)
        p.on("gpsimd", lambda E, v=p.cnt["sTAP"]: E.wait_ge(sems["sTAP"], v))
    p.barrier()

    HHS = lambda a, b: Hh[:, a:b]
    FIN = [HHS(0, 4096), XS(0, 4096)]
    FUU = [HHS(4096, 8192), XS(4096, 8192)]
    FRHO = [HHS(8192, 12288), XS(8192, 12288)]
    FSG = [HHS(12288, 16384), XS(12288, 16384)]
    H1Q2 = [XS(0, 2048), XS(2048, 4096), XS(4096, 6144), XS(6144, 8192)]
    W2R = [M1[:, 0:1024], M1[:, 1024:2048], RB[:, 0:1024], RB[:, 1024:2048]]

    def lad_dve2(E, pf):
        E.tensor_scalar(FIN[pf], FIN[pf], 128.0, None, ALU.add)
        E.tensor_scalar(FRHO[pf], FIN[pf], -127.5, None, ALU.add)
        for bit in (64.0, 32.0, 16.0, 8.0, 4.0, 2.0, 1.0):
            E.tensor_scalar(FUU[pf], FRHO[pf], bit, bit,
                            ALU.is_ge, ALU.mult)
            E.tensor_tensor(FRHO[pf], FRHO[pf], FUU[pf], ALU.subtract)
        E.scalar_tensor_tensor(FUU[pf], FRHO[pf], -1.0, FIN[pf],
                               ALU.mult, ALU.add)
        return E.scalar_tensor_tensor(FIN[pf], FUU[pf], -127.5,
                                      FSG[pf], ALU.add, ALU.mult)

    prev_evac_dv = 0
    bank_free_dv = 0
    last_mm = 0
    mm_vals = {}
    lad_vals = {}
    spl_vals = {}
    LBITS = (32.0, 16.0, 8.0, 4.0, 2.0, 1.0)

    for gi, (d0, d1) in enumerate([(0, 1), (2, 3), (4, 5), (6, 7)]):
        if gi == 0:
            uacts = {}
            udma = {}
            NPAIR = HC // 2

            def emit_uact(pr):
                pf = pr % 2
                srd = "sRDa" if pf == 0 else "sRDb"
                if pr <= 1:
                    wv = [("sSP", sp_all),
                          ("sSPa", p.cnt["sSPa"]), ("sSPb", p.cnt["sSPb"])]
                    if pr == 0:
                        wv += [("sDV", bs2_done)]
                else:
                    wv = [("sMM", mm_vals[(0, pr - 2)]),
                          ("sDV", lad_vals[(0, pr - 2)]),
                          (("sH1a" if pf == 0 else "sH1b"),
                           spl_vals[(0, pr - 2)])]
                p.on("sync", lambda E, pr=pr, pf=pf: E.dma_start(
                    out=FIN[pf].rearrange("p (k c) -> p k c", k=2),
                    in_=h1sp[2 * pr:2 * pr + 2].rearrange("k p c -> p k c")),
                    wait=wv, inc=(srd, 16))
                p.on("sync", lambda E, pr=pr, pf=pf: E.dma_start(
                    out=W2R[2 * pf], in_=w2qsp[2 * pr]), inc=(srd, 16))
                p.on("sync", lambda E, pr=pr, pf=pf: E.dma_start(
                    out=W2R[2 * pf + 1], in_=w2qsp[2 * pr + 1]),
                    inc=(srd, 16))
                udma[pr] = (srd, p.cnt[srd])

            def emit_ulad(pr):
                pf = pr % 2
                p.on("vector", lambda E, pf=pf: (
                    E.tensor_tensor(FUU[pf][:, 0:2048], FIN[pf][:, 0:2048],
                                    B127[:], ALU.mult),
                    E.tensor_tensor(FUU[pf][:, 2048:4096],
                                    FIN[pf][:, 2048:4096],
                                    B127[:], ALU.mult))[-1],
                     wait=[udma[pr]], inc=("sDV", 1))
                p.on("scalar", lambda E, pf=pf: (
                    E.activation(FIN[pf], FUU[pf], AF.Abs),
                    E.activation(FSG[pf], FUU[pf], AF.Sign))[-1],
                     wait=[("sDV", p.cnt["sDV"])], inc=("sAC", 1))
                uacts[pr] = p.cnt["sAC"]

            emit_uact(0)
            emit_ulad(0)
            for pr in range(NPAIR):
                pf = pr % 2
                if pr + 1 < NPAIR:
                    emit_uact(pr + 1)
                p.on("vector", lambda E, pf=pf: (
                    E.tensor_scalar(FIN[pf], FIN[pf], 128.0, None, ALU.add),
                    E.tensor_scalar(FRHO[pf], FIN[pf], -127.5, None,
                                    ALU.add),
                    E.tensor_scalar(FUU[pf], FRHO[pf], 64.0, 64.0,
                                    ALU.is_ge, ALU.mult),
                    E.tensor_tensor(FRHO[pf], FRHO[pf], FUU[pf],
                                    ALU.subtract),
                    E.tensor_scalar(FUU[pf], FRHO[pf], 32.0, 32.0,
                                    ALU.is_ge, ALU.mult),
                    E.tensor_tensor(FRHO[pf], FRHO[pf], FUU[pf],
                                    ALU.subtract),
                    E.tensor_scalar(FUU[pf], FRHO[pf], 16.0, 16.0,
                                    ALU.is_ge, ALU.mult),
                    E.tensor_tensor(FRHO[pf], FRHO[pf], FUU[pf],
                                    ALU.subtract))[-1],
                     wait=[("sAC", uacts[pr])], inc=("sDV", 1))
                if pr + 1 < NPAIR:
                    emit_ulad(pr + 1)

                def dve_tail(E, pf=pf):
                    for bit in (8.0, 4.0, 2.0, 1.0):
                        E.tensor_scalar(FUU[pf], FRHO[pf], bit, bit,
                                        ALU.is_ge, ALU.mult)
                        E.tensor_tensor(FRHO[pf], FRHO[pf], FUU[pf],
                                        ALU.subtract)
                    E.scalar_tensor_tensor(FUU[pf], FRHO[pf], -1.0, FIN[pf],
                                           ALU.mult, ALU.add)
                    return E.scalar_tensor_tensor(FIN[pf], FUU[pf], -127.5,
                                                  FSG[pf], ALU.add, ALU.mult)
                p.on("vector", dve_tail, inc=("sDV", 1))
                lad_vals[(0, pr)] = p.cnt["sDV"]
                sh1 = "sH1a" if pf == 0 else "sH1b"
                p.on("gpsimd", lambda E, pr=pr, pf=pf: E.dma_start(
                    out=h1qsp[2 * pr:2 * pr + 2].rearrange("k p c -> p k c"),
                    in_=FIN[pf].rearrange("p (k c) -> p k c", k=2)),
                    wait=[("sDV", lad_vals[(0, pr)])], inc=(sh1, 16),
                    dma=True)
                spl_vals[(0, pr)] = p.cnt[sh1]
                mmw = [("sDV", lad_vals[(0, pr)])]
                for kch in range(2):
                    for nt in range(NNT):
                        for di, d in enumerate((d0, d1)):
                            p.on("tensor",
                                 lambda E, pf=pf, kch=kch, nt=nt, di=di, d=d,
                                 pr=pr:
                                 E.matmul(
                                     PB[4 * di + nt][:, :],
                                     W2R[2 * pf + kch][:,
                                                       d * 128:(d + 1) * 128],
                                     FIN[pf][:, kch * 2048 + nt * 512:
                                             kch * 2048 + (nt + 1) * 512],
                                     start=(pr == 0 and kch == 0),
                                     stop=(pr == NPAIR - 1 and kch == 1)),
                                 wait=mmw
                                 if (kch == 0 and nt == 0 and di == 0)
                                 else [],
                                 inc=("sMM", 1)
                                 if (kch == 1 and nt == NNT - 1 and di == 1)
                                 else None)
                mm_vals[(0, pr)] = p.cnt["sMM"]
        else:
            for kc in range(HC):
                pp = kc % 4
                srd = ["sRDa", "sGRa", "sW1a", "sGRb"][pp]
                wv = []
                if kc < 4:
                    wv = [("sMM", last_mm)]
                    if gi == 1:
                        wv += [("sH1a", p.cnt["sH1a"]),
                               ("sH1b", p.cnt["sH1b"])]
                elif (gi, kc - 4) in mm_vals:
                    wv = [("sMM", mm_vals[(gi, kc - 4)])]
                heng = "sync" if kc % 2 == 0 else "gpsimd"
                weng = "scalar" if kc % 2 == 0 else "gpsimd"
                p.on(heng, lambda E, kc=kc, pp=pp: E.dma_start(
                    out=H1Q2[pp], in_=h1qsp[kc]), wait=wv, inc=(srd, 16),
                    dma=True)
                p.on(weng, lambda E, kc=kc, pp=pp: E.dma_start(
                    out=W2R[pp], in_=w2qsp[kc]), wait=wv, inc=(srd, 16),
                    dma=True)
                rdn = p.cnt[srd]
                mmw = [(srd, rdn)]
                if kc == 0:
                    mmw += [("sDV", bank_free_dv)]
                for nt in range(NNT):
                    for di, d in enumerate((d0, d1)):
                        p.on("tensor",
                             lambda E, kc=kc, pp=pp, nt=nt, di=di, d=d:
                             E.matmul(PB[4 * di + nt][:, :],
                                      W2R[pp][:, d * 128:(d + 1) * 128],
                                      H1Q2[pp][:, nt * 512:(nt + 1) * 512],
                                      start=(kc == 0), stop=(kc == HC - 1)),
                             wait=mmw if (nt == 0 and di == 0) else [],
                             inc=("sMM", 1) if (nt == NNT - 1 and di == 1)
                             else None)
                mm_vals[(gi, kc)] = p.cnt["sMM"]
        mm_g = p.cnt["sMM"]
        last_mm = mm_g
        bank_free_trk = None
        for ei in range(8):
            di, nt = ei // 4, ei % 4
            d = (d0, d1)[di]
            pp = ei % 2
            sfin = "sFINa" if pp == 0 else "sFINb"
            wv = [("sMM", mm_g)] if ei == 0 else []
            if ei >= 2 or gi >= 1:
                wv += [(sfin, p.cnt[sfin])]
            p.on("vector", lambda E, di=di, nt=nt, pp=pp: E.tensor_tensor(
                OUTP[pp], PB[4 * di + nt][:, :],
                BS2[:, nt * 512:(nt + 1) * 512], ALU.mult),
                wait=wv, inc=("sDV", 1))
            dvn = p.cnt["sDV"]
            if ei == 7:
                bank_free_trk = dvn
            p.on("scalar", lambda E, d=d, pp=pp: E.activation(
                OUTP[pp], OUTP[pp], AF.Identity,
                bias=B2G[:, d:d + 1], scale=GB[:, d:d + 1]),
                wait=[("sDV", dvn)], inc=("sAC", 1))
            acn = p.cnt["sAC"]
            p.on("vector", lambda E, d=d, nt=nt, pp=pp: E.tensor_tensor(
                OUTP[pp], OUTP[pp],
                X2[:, d * L + nt * 512:d * L + (nt + 1) * 512], ALU.add),
                wait=[("sAC", acn)], inc=("sDV", 1))
            dvn2 = p.cnt["sDV"]
            p.on("gpsimd", lambda E, d=d, nt=nt, pp=pp: E.dma_start(
                out=out_img[d][:, nt * 512:(nt + 1) * 512], in_=OUTP[pp]),
                wait=[("sDV", dvn2)], inc=(sfin, 16), dma=True)
        prev_evac_dv = p.cnt["sDV"]
        bank_free_dv = bank_free_trk

    fa, fb = p.cnt["sFINa"], p.cnt["sFINb"]
    p.on("sync", lambda E: E.wait_ge(sems["sFINa"], fa))
    p.on("sync", lambda E: E.wait_ge(sems["sFINb"], fb))

    with nc.Block() as block:
        @block.sync
        def _(E):
            for f in p.ops["sync"]:
                f(E)

        @block.vector
        def _(E):
            E2 = Ser(E, sems["sVq"])
            for f in p.ops["vector"]:
                f(E2)

        @block.scalar
        def _(E):
            E2 = Ser(E, sems["sSq"])
            for f in p.ops["scalar"]:
                f(E2)

        @block.tensor
        def _(E):
            for f in p.ops["tensor"]:
                f(E)

        @block.gpsimd
        def _(E):
            E2 = Ser(E, sems["sPq"])
            for f in p.ops["gpsimd"]:
                f(E2)

    return nc


_NC_CACHE = None


def kernel(L0_attn_x, L0_attn_W_f, L0_attn_b_f, L0_attn_W_i, L0_attn_b_i,
           L0_attn_W_h, L0_attn_b_h, L0_ffn_W1, L0_ffn_b1, L0_ffn_W2,
           L0_ffn_b2, L0_c, L0_wmod):
    global _NC_CACHE
    import ml_dtypes
    x = np.asarray(L0_attn_x, dtype=np.float32)
    W_f = np.asarray(L0_attn_W_f, dtype=np.float32)
    W_i = np.asarray(L0_attn_W_i, dtype=np.float32)
    W_h = np.asarray(L0_attn_W_h, dtype=np.float32)
    b_f = np.asarray(L0_attn_b_f, dtype=np.float32)
    b_i = np.asarray(L0_attn_b_i, dtype=np.float32)
    b_h = np.asarray(L0_attn_b_h, dtype=np.float32)
    W1 = np.asarray(L0_ffn_W1, dtype=np.float32)
    b1 = np.asarray(L0_ffn_b1, dtype=np.float32)
    W2 = np.asarray(L0_ffn_W2, dtype=np.float32)
    b2 = np.asarray(L0_ffn_b2, dtype=np.float32)
    cmat = np.asarray(L0_c, dtype=np.float32)
    wmod = np.asarray(L0_wmod, dtype=np.float32)

    def lay_lhsT(W, mout, kin):
        return np.ascontiguousarray(
            W.reshape(mout, 128, kin, 128).transpose(3, 0, 2, 1)
            .reshape(128, mout * kin * 128))

    shared = {
        "wfT": lay_lhsT(W_f, KC, KC),
        "wiT": lay_lhsT(W_i, KC, KC),
        "whT": lay_lhsT(W_h, KC, KC),
        "w1T": np.ascontiguousarray(
            W1.reshape(HC, 128, KC, 128).transpose(0, 3, 2, 1)
            .reshape(HC, 128, KC * 128)),
        "w2T": np.ascontiguousarray(
            W2.reshape(KC, 128, HC, 128).transpose(0, 3, 2, 1)
            .reshape(KC, 128, HC * 128)),
        "wmodT": np.ascontiguousarray(
            wmod.reshape(KC, 128, 12, 512).transpose(2, 1, 0, 3)
            .reshape(12, 128, KC * 512)),
        "idnf": np.eye(128, dtype=np.float32),
        "cpack_b": np.concatenate([
            np.ones((128, 128), ml_dtypes.bfloat16),
            np.eye(128, dtype=ml_dtypes.bfloat16)], axis=1),

    }
    constv = np.broadcast_to(np.array(
        [1e-5, 1.0, 0.5, -1.0, -0.5, 2.0, 128.0, -127.5, -64.0,
         0.0, 0.0, 0.0], np.float32), (128, 12))
    base_pack = np.concatenate([
        np.zeros((128, 8), np.float32),
        np.ascontiguousarray(b_f.reshape(KC, 128).T),
        np.ascontiguousarray(b_i.reshape(KC, 128).T),
        np.ascontiguousarray(b_h.reshape(KC, 128).T),
        np.ascontiguousarray(b1.reshape(HC, 128).T),
        np.ascontiguousarray(b2.reshape(KC, 128).T),
        constv,
        np.ones((128, 128), np.float32)], axis=1)
    in_maps = []
    for b in range(B):
        m = dict(shared)
        m["xT"] = np.ascontiguousarray(x[b].T).reshape(KC, 128, L)
        pk = base_pack.copy()
        pk[:, 0:8] = cmat[b].reshape(KC, 128).T
        m["cpack_f"] = pk
        in_maps.append(m)

    if _NC_CACHE is None:
        _NC_CACHE = build_nc()
    res = run_bass_kernel_spmd(_NC_CACHE, in_maps, core_ids=list(range(B)))
    outs = []
    for b in range(B):
        oi = np.asarray(res.results[b]["out_img"], dtype=np.float32)
        outs.append(np.ascontiguousarray(oi.reshape(D, L).T))
    return np.stack(outs, axis=0)


# version 32
# speedup vs baseline: 1.0365x; 1.0365x over previous
import numpy as np
import concourse.bass as bass
import concourse.mybir as mybir
from concourse.bass_utils import run_bass_kernel_spmd

F32 = mybir.dt.float32
BF16 = mybir.dt.bfloat16
U8 = mybir.dt.uint8
AF = mybir.ActivationFunctionType
ALU = mybir.AluOpType
AX = mybir.AxisListType

DEBUG_TAPS = False
B, L, D, HID = 8, 2048, 1024, 4096
KC = D // 128
HC = HID // 128
NT = 512
NNT = L // NT
EPS_LN = 1e-5
EPS_Q = 1e-5
QB = 127.0

ENGINES = ("sync", "vector", "scalar", "tensor", "gpsimd")


class Prog:

    def __init__(self):
        self.ops = {e: [] for e in ENGINES}
        self.cnt = {}
        self.sems = None

    def sem(self, name):
        self.cnt[name] = 0

    def inc(self, name, n=1):
        self.cnt[name] += n
        return self.cnt[name]

    def on(self, eng, fn, wait=(), inc=None, dma=False):
        sems = self.sems
        serialized = (eng in ("vector", "scalar", "gpsimd")) and not dma
        for (s, v) in wait:
            if v > 0:
                self.ops[eng].append(lambda E, s=s, v=v: E.wait_ge(sems[s], v))
        if inc is not None:
            sname, amt = inc
            val = self.inc(sname, amt)
            if fn is None:
                self.ops[eng].append(
                    lambda E, s=sname, a=amt: E.sem_inc(sems[s], a))
            elif serialized:
                def run(E, f=fn, s=sname, a=amt):
                    f(E)
                    E.ser_flush_wait()
                    E.sem_inc(sems[s], a)
                self.ops[eng].append(run)
            else:
                self.ops[eng].append(
                    lambda E, f=fn, s=sname, a=amt: f(E).then_inc(sems[s], a))
            return val
        assert fn is not None
        self.ops[eng].append(fn)
        return None

    def barrier(self):
        for e in ENGINES:
            self.inc("sBR", 1)
        target = self.cnt["sBR"]
        sems = self.sems
        for e in ENGINES:
            self.ops[e].append(lambda E: E.sem_inc(sems["sBR"], 1))
            self.ops[e].append(lambda E: E.wait_ge(sems["sBR"], target))


SER_METHS = {"tensor_copy", "tensor_tensor", "tensor_scalar",
              "tensor_single_scalar", "scalar_tensor_tensor",
              "tensor_tensor_scan", "tensor_reduce", "select", "reciprocal",
              "memset", "memzero", "activation", "sign", "square", "sqrt",
              "mul", "add", "copy", "tensor_add", "tensor_sub", "tensor_mul",
              "tensor_max", "tensor_relu", "tensor_scalar_add",
              "tensor_scalar_mul", "tensor_scalar_max", "tensor_scalar_min"}


class Ser:

    def __init__(self, eng, sem):
        self._e, self._s, self._n = eng, sem, 0

    def ser_flush_wait(self):
        if self._n:
            self._e.wait_ge(self._s, self._n)

    def __getattr__(self, m):
        f = getattr(self._e, m)
        if m in SER_METHS:
            def w(*a, **k):
                if self._n:
                    self._e.wait_ge(self._s, self._n)
                ins = f(*a, **k)
                self._n += 1
                return ins.then_inc(self._s, 1)
            return w
        return f


def build_nc():
    nc = bass.Bass("TRN2", target_bir_lowering=False, debug=False)
    p = Prog()

    def din(name, shape, dt=F32):
        return nc.dram_tensor(name, shape, dt, kind="ExternalInput").ap()

    xT = din("xT", [KC, 128, L])
    cpack_f = din("cpack_f", [128, 84 + 128])
    cpack_b = din("cpack_b", [128, 256], BF16)
    wfT = din("wfT", [128, KC * D])
    wiT = din("wiT", [128, KC * D])
    whT = din("whT", [128, KC * D])
    w1T = din("w1T", [HC, 128, KC * 128])
    w2T = din("w2T", [KC, 128, HC * 128])
    wmodT = din("wmodT", [12, 128, 4096])
    idnf = din("idnf", [128, 128])

    out_img = nc.dram_tensor("out_img", [KC, 128, L], F32,
                             kind="ExternalOutput").ap()
    if DEBUG_TAPS:
        tap_hln = nc.dram_tensor("tap_hln", [128, KC * L], BF16,
                                 kind="ExternalOutput").ap()
        tap_x2 = nc.dram_tensor("tap_x2", [128, KC * L], BF16,
                                kind="ExternalOutput").ap()
        tap_xq = nc.dram_tensor("tap_xq", [128, KC * L], BF16,
                                kind="ExternalOutput").ap()
        tap_stm = nc.dram_tensor("tap_stm", [128, 64], F32,
                                 kind="ExternalOutput").ap()
        tap_mod = nc.dram_tensor("tap_mod", [128, 48], F32,
                                 kind="ExternalOutput").ap()
        tap_bs = nc.dram_tensor("tap_bs", [128, 3 * L], BF16,
                                kind="ExternalOutput").ap()
        tap_h1 = nc.dram_tensor("tap_h1", [128, L], BF16,
                                kind="ExternalOutput").ap()
        tap_f = nc.dram_tensor("tap_f", [128, L], BF16,
                               kind="ExternalOutput").ap()
        tap_v = nc.dram_tensor("tap_v", [128, L], BF16,
                               kind="ExternalOutput").ap()
        tap_hm = nc.dram_tensor("tap_hm", [128, L], F32,
                                kind="ExternalOutput").ap()
    h1sp = nc.dram_tensor("h1sp", [HC, 128, L], BF16, kind="Internal").ap()
    h1qsp = nc.dram_tensor("h1qsp", [HC, 128, L], BF16, kind="Internal").ap()
    w2qsp = nc.dram_tensor("w2qsp", [HC, 128, KC * 128], BF16,
                           kind="Internal").ap()

    def sb(name, shape, dt):
        return nc.alloc_sbuf_tensor(name, shape, dt).ap()

    X = sb("X", [128, KC * L], BF16)
    Hh = sb("Hh", [128, KC * L], BF16)
    X2 = sb("X2", [128, KC * L], BF16)
    SC = [sb(f"SC{i}", [128, L], F32) for i in range(5)]
    MKm = sb("MKm", [128, L], U8)
    MB = sb("MB", [128, L], BF16)
    RB = sb("RB", [128, L], BF16)
    M1 = sb("M1", [128, L], BF16)
    WSTG = sb("WSTG", [128, 4096], F32)
    WSB = sb("WSB", [128, 4096], BF16)
    W1Q = sb("W1Q", [128, 2048], BF16)
    B127 = sb("B127", [128, L], BF16)
    BS1 = sb("BS1", [128, L], BF16)
    BS2 = sb("BS2", [128, L], BF16)
    ROW1 = sb("ROW1", [1, L], BF16)
    ROW2 = sb("ROW2", [1, L], BF16)
    SCf = sb("SCf", [128, KC], F32)
    TMPa = sb("TMPa", [128, KC], F32)
    MODC = sb("MODC", [128, 48], F32)
    GA = sb("GAt", [128, KC], F32)
    GB = sb("GBt", [128, KC], F32)
    CA1 = sb("CA1", [128, KC], F32)
    CB1 = sb("CB1", [128, KC], F32)
    NBF = sb("NBF", [128, KC], F32)
    NBI = sb("NBI", [128, KC], F32)
    B2G = sb("B2G", [128, KC], F32)
    IDNF = sb("IDNF", [128, 128], F32)
    STM = sb("STM", [128, 16], F32)
    STMH = sb("STMH", [128, 16], F32)
    VB127 = sb("VB127", [128, 16], F32)
    VBS1 = sb("VBS1", [128, 16], F32)
    VBS2 = sb("VBS2", [128, 16], F32)
    WS1i = sb("WS1i", [128, 1], F32)
    WS1n = sb("WS1n", [128, 1], F32)
    WS2i = sb("WS2i", [128, 1], F32)
    WS2n = sb("WS2n", [128, 1], F32)
    WS1v = sb("WS1v", [128, 1], F32)
    WS2v = sb("WS2v", [128, 1], F32)
    WSrow = sb("WSrow", [1, 8], F32)
    CPF = sb("CPF", [128, 84 + 128], F32)
    CPB = sb("CPB", [128, 256], BF16)

    XS = lambda a, b: X[:, a:b]
    QUU, QAU = XS(0, 2048), XS(2048, 4096)
    QY, QR0 = XS(4096, 6144), XS(6144, 8192)
    QRA, QRB = XS(8192, 10240), XS(10240, 12288)
    QT, QNA = XS(12288, 14336), XS(14336, 16384)
    QSG = RB[:, 0:2048]
    H1Tp = [XS(0, 2048), XS(2048, 4096)]
    TRN2A = XS(4096, 6144)
    TRN2B = XS(6144, 8192)
    HP = [XS(6144, 6656), XS(6656, 7168)]
    H1Qp = [XS(7168, 7680), XS(7680, 8192)]
    W2QS = [XS(8192, 9216), XS(9216, 10240)]
    R0t = [XS(10240, 10752), XS(10752, 11264)]
    Rta = [XS(11264, 11776), XS(11776, 12288)]
    Rtb = [XS(12288, 12800), XS(12800, 13312)]
    Tt = [XS(13312, 13824), XS(13824, 14336)]
    UUt = [XS(14336, 14848), XS(14848, 15360)]
    AUt = [XS(15360, 15872), XS(15872, 16384)]
    NAt = [RB[:, 0:512], RB[:, 512:1024]]
    SGt = [RB[:, 1024:1536], RB[:, 1536:2048]]
    OUTP = [SC[2][:, 0:512], SC[2][:, 512:1024]]

    CVt = CPF[:, 0:8]
    BFVt = CPF[:, 8:16]
    BIVt = CPF[:, 16:24]
    BHVt = CPF[:, 24:32]
    B1Vt = CPF[:, 32:64]
    B2Vt = CPF[:, 64:72]
    CONSTV = CPF[:, 72:84]
    ONESRf = CPF[0:1, 84:212]
    _cvals = [EPS_LN, 1.0, 0.5, -1.0, -0.5, 2.0, 128.0, -127.5, -64.0,
              0.0, 0.0, 0.0]
    for _j, _v in enumerate(_cvals):
        nc.const_aps.register(F32, _v, CONSTV[:, 72 - 72 + _j:_j + 1]
                              if False else CPF[:, 72 + _j:72 + _j + 1])
    ONES128 = CPB[:, 0:128]
    IDN = CPB[:, 128:256]
    ONESRb = CPB[0:1, 0:128]
    ONESCb = CPB[:, 0:1]
    ONE1b = CPB[0:1, 0:1]

    PB = [nc.alloc_psum_tensor(f"PB{i}", [128, NT], F32).ap() for i in range(8)]

    semnames = ["sBR", "sMD", "sM1", "sM2", "sM3", "sLXa", "sLXb", "sCN", "sWMa", "sWMb", "sLWa", "sLWb",
                "sMW", "sW1a", "sW1b", "sW2", "sW2b", "sW2c", "sW2d", "sLWc", "sLWd", "sSP", "sSPa", "sSPb", "sRDa", "sRDb",
                "sFINa", "sFINb", "sMM", "sDV", "sAC", "sPL",
                "sVq", "sSq", "sPq", "sTAP", "sH1a", "sH1b", "sW1c", "sW1d"]
    sems = {s: nc.alloc_semaphore(s) for s in semnames}
    p.sems = sems
    for s in semnames:
        p.sem(s)

    chunk = lambda t, k: t[:, k * L:(k + 1) * L]
    ntile = lambda t, k, j: t[:, k * L + j * NT:k * L + (j + 1) * NT]
    slot = lambda s: MODC[:, s * 8:(s + 1) * 8]

    p.on("scalar", lambda E: E.dma_start(out=CPF[:], in_=cpack_f),
         inc=("sCN", 16), dma=True)
    p.on("scalar", lambda E: E.dma_start(out=CPB[:], in_=cpack_b),
         inc=("sCN", 16), dma=True)
    p.on("scalar", lambda E: E.dma_start(out=IDNF[:], in_=idnf),
         inc=("sCN", 16), dma=True)
    cn_all = p.cnt["sCN"]

    xstg = [SC[0], SC[1]]
    for k in range(KC):
        w = [("sDV", p.cnt["sDV"])] if k >= 2 else []
        sx = "sLXa" if k % 2 == 0 else "sLXb"
        p.on("sync", lambda E, k=k: E.dma_start(out=xstg[k % 2], in_=xT[k]),
             wait=w, inc=(sx, 16))
        p.on("vector", lambda E, k=k: E.tensor_copy(chunk(X, k), xstg[k % 2]),
             wait=[(sx, p.cnt[sx])], inc=("sDV", 1))
    xcast_done = p.cnt["sDV"]

    p.on("scalar", lambda E: E.activation(SCf[:], CVt[:], AF.Silu),
         wait=[("sCN", cn_all)], inc=("sAC", 1))
    ac_silu = p.cnt["sAC"]

    mod_dv_base = p.cnt["sDV"]
    mod_ev = {}
    modstg = [WSTG[:, 0:4096], None]

    def mod_piece(j, k):
        if j % 2 == 0:
            return WSTG[:, k * 512:(k + 1) * 512]
        t = SC[2] if k < 4 else SC[3]
        return t[:, (k % 4) * 512:(k % 4 + 1) * 512]

    def emit_mod_block(j):
        pp = j % 2
        if j >= 2:
            p.on("sync", lambda E, v=j - 1: E.wait_ge(sems["sMD"], v))
        swm = "sWMa" if pp == 0 else "sWMb"
        if pp == 0:
            p.on("sync", lambda E, j=j: E.dma_start(
                out=WSTG[:, 0:4096], in_=wmodT[j]), inc=(swm, 16))
        else:
            p.on("sync", lambda E, j=j: E.dma_start(
                out=SC[2][:], in_=wmodT[j][:, 0:2048]), inc=(swm, 16))
            p.on("sync", lambda E, j=j: E.dma_start(
                out=SC[3][:], in_=wmodT[j][:, 2048:4096]), inc=(swm, 16))
        wm_now = p.cnt[swm]
        for k in range(KC):
            w = []
            if k == 0:
                w = [(swm, wm_now), ("sAC", ac_silu)]
                if j >= 2:
                    w.append(("sDV", mod_ev[j - 1]))
            p.on("tensor", lambda E, j=j, k=k: E.matmul(
                PB[j % 2][0:1, :], SCf[:, k:k + 1], mod_piece(j, k),
                start=(k == 0), stop=(k == KC - 1)),
                wait=w, inc=("sMD", 1) if k == KC - 1 else None)
        mm_now = p.cnt["sMD"]
        if j < 4:
            dstrow = MB[0:1, j * 512:(j + 1) * 512]
        elif j < 8:
            dstrow = RB[0:1, (j - 4) * 512:(j - 4 + 1) * 512]
        else:
            dstrow = M1[0:1, (j - 8) * 512:(j - 8 + 1) * 512]
        p.on("vector", lambda E, d=dstrow, j=j: E.tensor_copy(
            d, PB[j % 2][0:1, :]), wait=[("sMD", mm_now)], inc=("sDV", 1))
        mod_ev[j] = p.cnt["sDV"]


    wabs_stage = [SC[4][:, 0:1024], SC[4][:, 1024:2048],
                  SC[3][:, 0:1024], SC[3][:, 1024:2048]]
    wabs_b = [WSB[:, i * 1024:(i + 1) * 1024] for i in range(4)]
    w2abs_stage = [SC[0], SC[1], WSTG[:, 0:2048], WSTG[:, 2048:4096]]
    w2abs_b = [X2[:, 6144 + i * 1024:6144 + (i + 1) * 1024] for i in range(4)]
    w2actn = {}
    w1actn = {}
    w1mm = {}
    w2mm_abs = {}

    def emit_w1_slab(s):
        if s < 10:
            st = s % 2
            w = [("sAC", w1actn[s - 2])] if s >= 2 else []
        elif s < 12:
            st = s % 4
            w = [("sMD", 12)]
        else:
            st = s % 4
            w = [("sAC", w1actn[s - 4])]
        slw = ["sLWa", "sLWb", "sLWc", "sLWd"][s % 4]
        p.on("gpsimd", lambda E, s=s, st=st: E.dma_start(
            out=wabs_stage[st], in_=w1T[s]),
             wait=w, inc=(slw, 16), dma=True)
        ob = s % 4
        wsMM = [("sM1", w1mm[s - 4])] if s >= 4 else []
        p.on("scalar", lambda E, s=s, st=st, ob=ob: E.activation(
            wabs_b[ob], wabs_stage[st], AF.Abs),
            wait=[(slw, p.cnt[slw])] + wsMM, inc=("sAC", 1))
        acn = p.cnt["sAC"]
        w1actn[s] = acn
        for h in range(2):
            p.on("tensor", lambda E, s=s, h=h, ob=ob: E.matmul(
                PB[4 + h][0:1, :], ONESCb[:],
                wabs_b[ob][:, h * 512:(h + 1) * 512],
                start=(s == 0), stop=(s == HC - 1)),
                wait=[("sAC", acn)] if h == 0 else [],
                inc=("sM1", 1))
        w1mm[s] = p.cnt["sM1"]

    def emit_w2_unit(t):
        if t < 2:
            st = t % 2
            w2w = [("sDV", xcast_done)]
        elif t < 6:
            st = t % 2
            w2w = [("sAC", w2actn[t - 2])]
        elif t < 8:
            st = t % 4
            w2w = [("sMD", 11)]
        else:
            st = t % 4
            w2w = [("sAC", w2actn[t - 4])]
        sw2 = ["sW2", "sW2b", "sW2c", "sW2d"][t % 4]
        p.on("scalar", lambda E, t=t, st=st: E.dma_start(
            out=w2abs_stage[st],
            in_=w2T[t // 2][:, (t % 2) * 2048:(t % 2) * 2048 + 2048]),
            wait=w2w, inc=(sw2, 16), dma=True)
        w2n = p.cnt[sw2]
        for h in range(2):
            a = 2 * t + h
            ob = a % 4
            wv2 = [(sw2, w2n)] if h == 0 else []
            if a >= 4:
                wv2 += [("sM2", w2mm_abs[a - 4])]
            p.on("scalar", lambda E, t=t, h=h, st=st, ob=ob: E.activation(
                w2abs_b[ob], w2abs_stage[st][:, h * 1024:(h + 1) * 1024],
                AF.Abs), wait=wv2, inc=("sAC", 1))
            a2 = p.cnt["sAC"]
            for q in range(2):
                p.on("tensor", lambda E, t=t, h=h, q=q, ob=ob: E.matmul(
                    PB[6][0:1, :], ONESCb[:],
                    w2abs_b[ob][:, q * 512:(q + 1) * 512],
                    start=(t == 0 and h == 0 and q == 0),
                    stop=(t == 15 and h == 1 and q == 1)),
                    wait=[("sAC", a2)] if q == 0 else [],
                    inc=("sM2", 1))
            w2mm_abs[a] = p.cnt["sM2"]
        w2actn[t] = p.cnt["sAC"]

    for j in range(12):
        emit_mod_block(j)
        if j < 10:
            emit_w1_slab(j)
        if j < 6:
            emit_w2_unit(j)
    mod_rows_done = p.cnt["sDV"]

    def modr_piece(t):
        if t < 16:
            return MB[0:1, t * 128:(t + 1) * 128]
        if t < 32:
            return RB[0:1, (t - 16) * 128:(t - 16) * 128 + 128]
        return M1[0:1, (t - 32) * 128:(t - 32) * 128 + 128]

    for t in range(48):
        w = [("sDV", mod_rows_done), ("sCN", cn_all)] if t == 0 else []
        p.on("tensor", lambda E, t=t: E.matmul(
            PB[2][:, t:t + 1], modr_piece(t), ONE1b[:],
            start=True, stop=True), wait=w,
            inc=("sMD", 1) if t == 47 else None)
    mm_modc = p.cnt["sMD"]
    p.on("vector", lambda E: E.tensor_copy(MODC[:], PB[2][:, 0:48]),
         wait=[("sMD", mm_modc)], inc=("sDV", 1))
    modc_done = p.cnt["sDV"]

    def derive(E):
        E.tensor_scalar(CA1[:], slot(1), 1.0, None, ALU.add)
        E.tensor_scalar(CB1[:], slot(4), 1.0, None, ALU.add)
        E.tensor_scalar(GA[:], slot(2), 0.0, 2.0, ALU.is_ge, ALU.mult)
        E.tensor_scalar(GA[:], GA[:], -1.0, None, ALU.add)
        E.tensor_scalar(TMPa[:], slot(2), -1.0, None, ALU.mult)
        E.tensor_tensor(TMPa[:], TMPa[:], slot(2), ALU.max)
        E.tensor_scalar(TMPa[:], TMPa[:], 0.25, None, ALU.max)
        E.tensor_tensor(GA[:], GA[:], TMPa[:], ALU.mult)
        E.tensor_scalar(GB[:], slot(5), 0.0, 2.0, ALU.is_ge, ALU.mult)
        E.tensor_scalar(GB[:], GB[:], -1.0, None, ALU.add)
        E.tensor_scalar(TMPa[:], slot(5), -1.0, None, ALU.mult)
        E.tensor_tensor(TMPa[:], TMPa[:], slot(5), ALU.max)
        E.tensor_scalar(TMPa[:], TMPa[:], 0.25, None, ALU.max)
        E.tensor_tensor(GB[:], GB[:], TMPa[:], ALU.mult)
        E.tensor_scalar(NBF[:], BFVt[:], -1.0, None, ALU.mult)
        E.tensor_scalar(NBI[:], BIVt[:], -1.0, None, ALU.mult)
        return E.tensor_tensor(B2G[:], GB[:], B2Vt[:], ALU.mult)
    p.on("vector", derive, wait=[("sDV", modc_done)], inc=("sDV", 1))

    for r in range(11):
        emit_w1_slab(10 + 2 * r)
        emit_w1_slab(11 + 2 * r)
        if r < 10:
            emit_w2_unit(6 + r)
    mm_w1abs = ("sM1", p.cnt["sM1"])
    mm_w2abs = ("sM2", p.cnt["sM2"])


    def ws1_fin(E):
        E.tensor_reduce(WSrow[0:1, 0:1], PB[4][0:1, :], AX.X, ALU.add)
        E.tensor_reduce(WSrow[0:1, 4:5], PB[5][0:1, :], AX.X, ALU.add)
        E.tensor_tensor(WSrow[0:1, 0:1], WSrow[0:1, 0:1], WSrow[0:1, 4:5],
                        ALU.add)
        E.tensor_scalar(WSrow[0:1, 2:3], WSrow[0:1, 0:1],
                        1.0 / (HID * D), None, ALU.mult)
        return E.tensor_scalar(WSrow[0:1, 2:3], WSrow[0:1, 2:3], EPS_Q,
                               None, ALU.max)
    p.on("vector", ws1_fin, wait=[mm_w1abs], inc=("sDV", 1))
    ws1_done = p.cnt["sDV"]

    def ws2_fin(E):
        E.tensor_reduce(WSrow[0:1, 1:2], PB[6][0:1, :], AX.X, ALU.add)
        E.tensor_scalar(WSrow[0:1, 3:4], WSrow[0:1, 1:2],
                        1.0 / (HID * D), None, ALU.mult)
        return E.tensor_scalar(WSrow[0:1, 3:4], WSrow[0:1, 3:4], EPS_Q,
                               None, ALU.max)
    p.on("vector", ws2_fin, wait=[mm_w2abs], inc=("sDV", 1))
    ws2_done = p.cnt["sDV"]

    p.on("tensor", lambda E: E.matmul(PB[7][:, 0:1], ONESRf[:],
                                      WSrow[0:1, 2:3], start=True, stop=True),
         wait=[("sDV", ws2_done)])
    p.on("tensor", lambda E: E.matmul(PB[7][:, 1:2], ONESRf[:],
                                      WSrow[0:1, 3:4], start=True, stop=True),
         inc=("sMM", 1))
    mm_wsb = p.cnt["sMM"]

    def ws_vecs(E):
        E.tensor_copy(WS1v[:], PB[7][:, 0:1])
        E.tensor_copy(WS2v[:], PB[7][:, 1:2])
        E.reciprocal(WS1i[:], WS1v[:])
        E.reciprocal(WS2i[:], WS2v[:])
        E.tensor_scalar(WS1n[:], WS1i[:], -1.0, None, ALU.mult)
        return E.tensor_scalar(WS2n[:], WS2i[:], -1.0, None, ALU.mult)
    p.on("vector", ws_vecs, wait=[("sMM", mm_wsb)], inc=("sDV", 1))

    for k in range(KC):
        wv = [("sDV", modc_done)] if k == 0 else []
        for j in range(NNT):
            p.on("tensor", lambda E, k=k, j=j: E.matmul(
                PB[j][0:1, :], ONESCb[:], ntile(X, k, j),
                start=(k == 0), stop=(k == KC - 1)),
                wait=wv if j == 0 else [],
                inc=("sM3", 1) if (k == KC - 1 and j == NNT - 1) else None)
    mm_sum1 = p.cnt["sM3"]

    def ln1_mean(E):
        for j in range(NNT):
            E.tensor_scalar(ROW1[0:1, j * 512:(j + 1) * 512],
                            PB[j][0:1, :], 1.0 / D, None, ALU.mult)
        last = None
        for j in range(NNT):
            last = E.tensor_tensor(ROW2[0:1, j * 512:(j + 1) * 512],
                                   ROW1[0:1, j * 512:(j + 1) * 512],
                                   ROW1[0:1, j * 512:(j + 1) * 512],
                                   ALU.mult)
        return last
    p.on("vector", ln1_mean, wait=[("sM3", mm_sum1)], inc=("sDV", 1))

    sq1 = [X2[:, 2048:4096], X2[:, 4096:6144]]
    st1_mm = {}
    for k in range(KC):
        wv = [("sM3", st1_mm[k - 2])] if k >= 2 else []
        p.on("vector", lambda E, k=k: E.tensor_tensor(
            sq1[k % 2], chunk(X, k), chunk(X, k), ALU.mult),
            wait=wv, inc=("sDV", 1))
        sqdv = p.cnt["sDV"]
        for j in range(NNT):
            p.on("tensor", lambda E, k=k, j=j: E.matmul(
                PB[j][0:1, :], ONESCb[:],
                sq1[k % 2][:, j * 512:(j + 1) * 512],
                start=(k == 0), stop=(k == KC - 1)),
                wait=[("sDV", sqdv)] if j == 0 else [],
                inc=("sM3", 1) if j == NNT - 1 else None)
        st1_mm[k] = p.cnt["sM3"]
    mm_sq1 = p.cnt["sM3"]

    def ln1_var(E):
        for j in range(NNT):
            E.tensor_scalar(PB[j][0:1, :], PB[j][0:1, :], 1.0 / D,
                            None, ALU.mult)
        last = None
        for j in range(NNT):
            last = E.tensor_tensor(PB[j][0:1, :], PB[j][0:1, :],
                                   ROW2[0:1, j * 512:(j + 1) * 512],
                                   ALU.subtract)
        return last
    p.on("vector", ln1_var, wait=[("sM3", mm_sq1)], inc=("sDV", 1))

    def ln1_rstd(E):
        for j in range(NNT):
            E.activation(SC[2][0:1, j * 512:(j + 1) * 512], PB[j][0:1, :],
                         AF.Ln, bias=EPS_LN)
        last = None
        for j in range(NNT):
            last = E.activation(ROW2[0:1, j * 512:(j + 1) * 512],
                                SC[2][0:1, j * 512:(j + 1) * 512],
                                AF.Exp, scale=-0.5)
        return last
    p.on("scalar", ln1_rstd, wait=[("sDV", p.cnt["sDV"])], inc=("sAC", 1))
    rstd1_ac = p.cnt["sAC"]

    for j in range(NNT):
        p.on("tensor", lambda E, j=j: E.matmul(
            PB[j][:, :], ONESRb[:], ROW1[0:1, j * 512:(j + 1) * 512],
            start=True, stop=True),
            wait=[("sAC", rstd1_ac)] if j == 0 else [],
            inc=("sM3", 1) if j == NNT - 1 else None)
    bcw1 = p.cnt["sM3"]

    def bc1_evac(E):
        last = None
        for j in range(NNT):
            last = E.activation(MB[:, j * 512:(j + 1) * 512], PB[j][:, :],
                                AF.Identity)
        return last
    p.on("scalar", bc1_evac, wait=[("sM3", bcw1)], inc=("sAC", 1))
    bc1e_ac = p.cnt["sAC"]
    for j in range(NNT):
        p.on("tensor", lambda E, j=j: E.matmul(
            PB[j][:, :], ONESRb[:], ROW2[0:1, j * 512:(j + 1) * 512],
            start=True, stop=True),
            wait=[("sAC", bc1e_ac)] if j == 0 else [],
            inc=("sM3", 1) if j == NNT - 1 else None)
    bcw2 = p.cnt["sM3"]

    def bc2_evac(E):
        last = None
        for j in range(NNT):
            last = E.activation(RB[:, j * 512:(j + 1) * 512], PB[j][:, :],
                                AF.Identity)
        return last
    p.on("scalar", bc2_evac, wait=[("sM3", bcw2)], inc=("sAC", 1))
    p.on("vector", None, wait=[("sAC", p.cnt["sAC"])], inc=("sDV", 1))
    bc1_done = p.cnt["sDV"]

    for k in range(KC):
        wv = [("sDV", bc1_done)] if k == 0 else [("sAC", p.cnt["sAC"])]

        def ln_apply_v(E, k=k, src=X):
            E.tensor_tensor(M1[:], chunk(src, k), MB[:], ALU.subtract)
            return E.tensor_tensor(X2[:, 0:L], M1[:], RB[:], ALU.mult)
        p.on("vector", ln_apply_v, wait=wv, inc=("sDV", 1))
        p.on("scalar", lambda E, k=k: E.activation(
            chunk(Hh, k), X2[:, 0:L], AF.Identity,
            bias=slot(0)[:, k:k + 1], scale=CA1[:, k:k + 1]),
            wait=[("sDV", p.cnt["sDV"])], inc=("sAC", 1))

    sq_slab = [WSB[:, 0:2048], WSB[:, 2048:4096]]

    def ln_stats(src_tile):
        mm_chunk_vals = []
        for k in range(KC):
            wv = []
            if k >= 2:
                wv = [("sMM", mm_chunk_vals[k - 2])]
            p.on("scalar", lambda E, k=k, s=src_tile: E.activation(
                sq_slab[k % 2], chunk(s, k), AF.Square),
                wait=wv, inc=("sAC", 1))
            acn = p.cnt["sAC"]
            for j in range(NNT):
                p.on("tensor", lambda E, k=k, j=j, s=src_tile: E.matmul(
                    PB[j][0:1, :], ONESCb[:], ntile(s, k, j),
                    start=(k == 0), stop=(k == KC - 1)))
            for j in range(NNT):
                p.on("tensor", lambda E, k=k, j=j: E.matmul(
                    PB[4 + j][0:1, :], ONESCb[:],
                    sq_slab[k % 2][:, j * 512:(j + 1) * 512],
                    start=(k == 0), stop=(k == KC - 1)),
                    wait=[("sAC", acn)] if j == 0 else [],
                    inc=("sMM", 1) if j == NNT - 1 else None)
            mm_chunk_vals.append(p.cnt["sMM"])
        return p.cnt["sMM"]

    def ln_rows(E):
        for j in range(NNT):
            E.tensor_scalar(ROW1[0:1, j * 512:(j + 1) * 512],
                            PB[j][0:1, :], 1.0 / D, None, ALU.mult)
        for j in range(NNT):
            E.tensor_scalar(PB[4 + j][0:1, :], PB[4 + j][0:1, :],
                            1.0 / D, None, ALU.mult)
            E.tensor_tensor(ROW2[0:1, j * 512:(j + 1) * 512],
                            ROW1[0:1, j * 512:(j + 1) * 512],
                            ROW1[0:1, j * 512:(j + 1) * 512], ALU.mult)
        last = None
        for j in range(NNT):
            last = E.tensor_tensor(PB[4 + j][0:1, :], PB[4 + j][0:1, :],
                                   ROW2[0:1, j * 512:(j + 1) * 512],
                                   ALU.subtract)
        return last

    def ln_rstd(E):
        for j in range(NNT):
            E.activation(PB[j][0:1, :], PB[4 + j][0:1, :], AF.Ln, bias=EPS_LN)
        last = None
        for j in range(NNT):
            last = E.activation(ROW2[0:1, j * 512:(j + 1) * 512],
                                PB[j][0:1, :], AF.Exp, scale=-0.5)
        return last

    def ln_bcast():
        ac_r = p.cnt["sAC"]
        for j in range(NNT):
            p.on("tensor", lambda E, j=j: E.matmul(
                PB[j][:, :], ONESRb[:], ROW1[0:1, j * 512:(j + 1) * 512],
                start=True, stop=True),
                wait=[("sAC", ac_r)] if j == 0 else [])
            p.on("tensor", lambda E, j=j: E.matmul(
                PB[4 + j][:, :], ONESRb[:], ROW2[0:1, j * 512:(j + 1) * 512],
                start=True, stop=True),
                inc=("sMM", 1) if j == NNT - 1 else None)
        mm_bc = p.cnt["sMM"]

        def bc_evac(E):
            for j in range(NNT):
                E.activation(MB[:, j * 512:(j + 1) * 512], PB[j][:, :],
                             AF.Identity)
            last = None
            for j in range(NNT):
                last = E.activation(RB[:, j * 512:(j + 1) * 512],
                                    PB[4 + j][:, :], AF.Identity)
            return last
        p.on("scalar", bc_evac, wait=[("sMM", mm_bc)], inc=("sAC", 1))
        p.on("vector", None, wait=[("sAC", p.cnt["sAC"])], inc=("sDV", 1))
        return p.cnt["sDV"]

    if DEBUG_TAPS:
        p.on("gpsimd", lambda E: E.dma_start(out=tap_hln, in_=Hh[:]),
             wait=[("sAC", p.cnt["sAC"])], inc=("sTAP", 16), dma=True)
        p.on("gpsimd", lambda E, v=p.cnt["sTAP"]: E.wait_ge(sems["sTAP"], v))
        p.on("gpsimd", lambda E: E.dma_start(out=tap_mod, in_=MODC[:]),
             inc=("sTAP", 16), dma=True)
        p.on("gpsimd", lambda E, v=p.cnt["sTAP"]: E.wait_ge(sems["sTAP"], v))
    p.barrier()

    BH1p = [M1, B127]
    BH2p = [MB, BS1]
    RLp = [SC[3][:], BS2[:]]
    MNp = [SC[4][:], W1Q[:, 0:L]]
    evac_vals = {}

    chain_ends = {}

    def emit_mixer_mm_evac(e):
        par = e % 2
        for gi, wimg in enumerate((wfT, wiT, whT)):
            wv = []
            if gi == 0 and e >= 1:
                wv = [("sMM", p.cnt["sMM"])]
            p.on("sync", lambda E, gi=gi, w_=wimg, e=e: E.dma_start(
                out=WSTG[:, gi * 1024:(gi + 1) * 1024],
                in_=w_[:, e * 1024:(e + 1) * 1024]),
                wait=wv, inc=("sMW", 16))
        mw_now = p.cnt["sMW"]

        def slab_cast(E, e=e):
            E.tensor_copy(WSB[:, 0:1024], WSTG[:, 0:1024])
            E.tensor_copy(WSB[:, 1024:2048], WSTG[:, 1024:2048])
            return E.tensor_copy(WSB[:, 2048:3072], WSTG[:, 2048:3072])
        p.on("gpsimd", slab_cast, wait=[("sMW", mw_now)] +
             ([("sAC", p.cnt["sAC"])] if e == 0 else []), inc=("sPL", 1))
        cast_done = ("sPL", p.cnt["sPL"])

        for j in range(NNT):
            pp = 3 * (j % 2)
            wv = [cast_done]
            if j >= 2:
                wv += [("sDV", evac_vals[(e, j - 2, "dv")]),
                       ("sAC", evac_vals[(e, j - 2, "ac")])]
            elif e >= 1:
                wv += [("sDV", evac_vals[(e - 1, j + 2, "dv")]),
                       ("sAC", evac_vals[(e - 1, j + 2, "ac")])]
            for gi in range(3):
                for k in range(KC):
                    p.on("tensor", lambda E, e=e, j=j, gi=gi, k=k, pp=pp:
                         E.matmul(PB[pp + gi][:, :],
                                  WSB[:, gi * 1024 + k * 128:
                                      gi * 1024 + (k + 1) * 128],
                                  ntile(Hh, k, j),
                                  start=(k == 0), stop=(k == KC - 1)),
                         wait=wv if (gi == 0 and k == 0) else [],
                         inc=("sMM", 1) if k == KC - 1 else None)
            mm_gate = p.cnt["sMM"]

            def evac_act(E, e=e, j=j, pp=pp, par=par):
                E.activation(BH1p[par][:, j * 512:(j + 1) * 512],
                             PB[pp][:, :],
                             AF.Exp, scale=-1.0, bias=NBF[:, e:e + 1])
                return E.activation(BH2p[par][:, j * 512:(j + 1) * 512],
                                    PB[pp + 1][:, :],
                                    AF.Exp, scale=-1.0, bias=NBI[:, e:e + 1])
            ga_g = [("sDV", g_dvs[e - 2])] if (j == 0 and
                                               (e - 2) in g_dvs) else []
            p.on("scalar", evac_act,
                 wait=[("sMM", mm_gate)] + ga_g, inc=("sAC", 1))
            evac_vals[(e, j, "ac")] = p.cnt["sAC"]

            def evac_dve(E, e=e, j=j, pp=pp, par=par):
                E.tensor_scalar(RLp[par][:, j * 512:(j + 1) * 512],
                                PB[pp + 2][:, :], BHVt[:, e:e + 1], 0.0,
                                ALU.add, ALU.max)
                return E.tensor_scalar(MNp[par][:, j * 512:(j + 1) * 512],
                                       PB[pp + 2][:, :], BHVt[:, e:e + 1],
                                       0.0, ALU.add, ALU.min)
            p.on("vector", evac_dve,
                 wait=[("sMM", mm_gate)], inc=("sDV", 1))
            evac_vals[(e, j, "dv")] = p.cnt["sDV"]

    Ap = [SC[1][:], SC[0][:]]
    Bp = [SC[2][:], RB[:]]

    g_dvs = {}

    def emit_chain_pair(e0):
        es = (e0, e0 + 1)

        def arrs(e):
            par = e % 2
            return (Ap[par], Bp[par], BH1p[par][:], BH2p[par][:],
                    RLp[par], MNp[par])

        t_ac = {}
        t_dv = {}
        for e in es:
            A, B, BH1, BH2, RL, MN = arrs(e)
            w0 = [("sAC", evac_vals[(e, NNT - 1, "ac")]),
                  ("sDV", evac_vals[(e, NNT - 1, "dv")])]
            if (e - 2) in chain_ends:
                w0 += [("sDV", chain_ends[e - 2])]
            p.on("scalar", lambda E, A=A, BH2=BH2: E.activation(
                A, BH2, AF.Ln, bias=1.0), wait=w0, inc=("sAC", 1))
        for e in es:
            A, B, BH1, BH2, RL, MN = arrs(e)
            p.on("scalar", lambda E, B=B, BH1=BH1: E.activation(
                B, BH1, AF.Ln, bias=1.0), inc=("sAC", 1))
            t_ac[e] = p.cnt["sAC"]
        for e in es:
            A, B, BH1, BH2, RL, MN = arrs(e)
            p.on("vector", lambda E, A=A, B=B: E.tensor_tensor(
                A, B, A, ALU.subtract),
                wait=[("sAC", t_ac[e])], inc=("sDV", 1))
            t_dv[e] = p.cnt["sDV"]
        for e in es:
            A, B, BH1, BH2, RL, MN = arrs(e)
            p.on("scalar", lambda E, BH1=BH1, MN=MN: E.activation(
                BH1, MN, AF.Sigmoid), inc=("sAC", 1))
            t_ac[e] = p.cnt["sAC"]
        for e in es:
            A, B, BH1, BH2, RL, MN = arrs(e)
            p.on("scalar", lambda E, A=A, B=B: E.activation(
                B, A, AF.Sigmoid, scale=-1.0),
                wait=[("sDV", t_dv[e])], inc=("sAC", 1))
            t_dv[e] = p.cnt["sAC"]
        for e in es:
            A, B, BH1, BH2, RL, MN = arrs(e)
            p.on("vector", lambda E, BH1=BH1, RL=RL: E.tensor_tensor(
                RL, RL, BH1, ALU.add),
                wait=[("sAC", t_ac[e])], inc=("sDV", 1))
            g_dvs[e] = p.cnt["sDV"]
        for e in es:
            A, B, BH1, BH2, RL, MN = arrs(e)
            p.on("vector", lambda E, A=A, B=B: E.tensor_scalar(
                A, B, -1.0, 1.0, ALU.mult, ALU.add),
                wait=[("sAC", t_dv[e])], inc=("sDV", 1))
        for e in es:
            A, B, BH1, BH2, RL, MN = arrs(e)
            p.on("vector", lambda E, A=A, MN=MN, RL=RL: E.tensor_tensor(
                MN, RL, A, ALU.mult), inc=("sDV", 1))
        for e in es:
            A, B, BH1, BH2, RL, MN = arrs(e)

            def scan_epi(E, e=e, B=B, MN=MN, RL=RL):
                E.tensor_tensor_scan(RL, B, MN, 0.0, ALU.mult, ALU.add)
                return E.scalar_tensor_tensor(
                    chunk(X2, e), RL, GA[:, e:e + 1], chunk(X, e),
                    ALU.mult, ALU.add)
            p.on("vector", scan_epi, inc=("sDV", 1))
            chain_ends[e] = p.cnt["sDV"]

    for ep in range(KC // 2):
        if ep >= 1:
            emit_chain_pair(2 * ep - 2)
        emit_mixer_mm_evac(2 * ep)
        emit_mixer_mm_evac(2 * ep + 1)
    emit_chain_pair(KC - 2)
    p.barrier()

    mm_st2 = ln_stats(X2)
    p.on("vector", ln_rows, wait=[("sMM", mm_st2)], inc=("sDV", 1))
    p.on("scalar", ln_rstd, wait=[("sDV", p.cnt["sDV"])], inc=("sAC", 1))
    bc2_done = ln_bcast()

    for k in range(KC):
        wv = [("sDV", bc2_done)] if k == 0 else []

        def h2_v(E, k=k):
            E.tensor_tensor(QUU, chunk(X2, k), MB[:], ALU.subtract)
            return E.tensor_tensor(QAU, QUU, RB[:], ALU.mult)
        p.on("vector", h2_v, wait=wv, inc=("sDV", 1))
        p.on("scalar", lambda E, k=k: E.activation(
            chunk(Hh, k), QAU, AF.Identity,
            bias=slot(3)[:, k:k + 1], scale=CB1[:, k:k + 1]),
            wait=[("sDV", p.cnt["sDV"])], inc=("sAC", 1))
        acn = p.cnt["sAC"]
        M1n = XS(4096, 6144)
        if k == 0:
            p.on("vector", lambda E, M1n=M1n: (
                E.tensor_copy(M1[:], chunk(Hh, 0)),
                E.tensor_copy(M1n, chunk(Hh, 0)))[-1],
                wait=[("sAC", acn)], inc=("sDV", 1))
        else:
            p.on("vector", lambda E, k=k, M1n=M1n: (
                E.tensor_tensor(M1[:], M1[:], chunk(Hh, k), ALU.max),
                E.tensor_tensor(M1n, M1n, chunk(Hh, k), ALU.min))[-1],
                wait=[("sAC", acn)], inc=("sDV", 1))
    M1n = XS(4096, 6144)
    p.on("vector", lambda E, M1n=M1n: (
        E.tensor_scalar(M1n, M1n, -1.0, None, ALU.mult),
        E.tensor_tensor(M1[:], M1[:], M1n, ALU.max))[-1],
        inc=("sDV", 1))
    m1_done = p.cnt["sDV"]

    def make_scales(stm_tile, first_waits):
        p.on("vector", lambda E: E.tensor_copy(SC[0][:], M1[:]),
             wait=list(first_waits), inc=("sDV", 1))
        m1f_done = p.cnt["sDV"]
        for g4 in range(4):
            tgt = PB[6] if g4 % 2 == 0 else PB[7]
            wv = [("sDV", m1f_done)] if g4 == 0 else (
                [("sDV", p.cnt["sDV"])] if g4 >= 2 else [])
            for b4 in range(4):
                blk = g4 * 4 + b4
                p.on("tensor", lambda E, blk=blk, b4=b4, tgt=tgt: E.transpose(
                    tgt[:, b4 * 128:(b4 + 1) * 128],
                    SC[0][:, blk * 128:(blk + 1) * 128], IDNF[:]),
                    wait=wv if b4 == 0 else [],
                    inc=("sMM", 1) if b4 == 3 else None)
            mmn = p.cnt["sMM"]
            p.on("vector", lambda E, g4=g4, tgt=tgt, stm=stm_tile:
                 E.tensor_reduce(
                     stm[:, g4 * 4:(g4 + 1) * 4],
                     tgt[:, 0:512].rearrange("p (b c) -> p b c", b=4),
                     AX.X, ALU.max, apply_absolute_value=True),
                 wait=[("sMM", mmn)], inc=("sDV", 1))
        p.on("vector", lambda E, stm=stm_tile: E.tensor_scalar(
            stm[:], stm[:], EPS_Q, None, ALU.max), inc=("sDV", 1))
        return p.cnt["sDV"]

    stm_done = make_scales(STM, [("sDV", m1_done)])

    def xscale_vecs(E):
        E.reciprocal(STMH[:], STM[:])
        E.tensor_scalar(VB127[:], STMH[:], QB, None, ALU.mult)
        E.tensor_scalar(STMH[:], STM[:], 1.0 / QB, None, ALU.mult)
        return E.tensor_scalar(VBS1[:], STMH[:], WS1v[:, 0:1], None, ALU.mult)
    p.on("vector", xscale_vecs, wait=[("sDV", stm_done)], inc=("sDV", 1))
    xs_done = p.cnt["sDV"]

    def build_bcast(vec_tile, out_tile, after_dv):
        for g4 in range(4):
            pp = g4 % 2
            wv = [("sDV", after_dv)] if g4 == 0 else []
            if g4 >= 2:
                wv += [("sAC", p.cnt["sAC"])]
            for b4 in range(4):
                blk = g4 * 4 + b4
                p.on("vector", lambda E, blk=blk, b4=b4, v=vec_tile, pp=pp:
                     E.tensor_scalar(
                         WSB[:, (pp * 4 + b4) * 128:(pp * 4 + b4) * 128 + 128],
                         IDN[:], v[:, blk:blk + 1], None, ALU.mult),
                     wait=wv if b4 == 0 else [], inc=("sDV", 1))
                dvn = p.cnt["sDV"]
                p.on("tensor", lambda E, blk=blk, b4=b4, pp=pp: E.matmul(
                    PB[pp][:, b4 * 128:(b4 + 1) * 128], ONES128[:],
                    WSB[:, (pp * 4 + b4) * 128:(pp * 4 + b4) * 128 + 128],
                    start=True, stop=True),
                    wait=[("sDV", dvn)],
                    inc=("sMM", 1) if b4 == 3 else None)
            mmn = p.cnt["sMM"]
            p.on("scalar", lambda E, g4=g4, o=out_tile, pp=pp:
                 E.activation(o[:, g4 * 512:(g4 + 1) * 512],
                              PB[pp][:, 0:512], AF.Identity),
                 wait=[("sMM", mmn)], inc=("sAC", 1))
        p.on("vector", None, wait=[("sAC", p.cnt["sAC"])], inc=("sDV", 1))
        return p.cnt["sDV"]

    b127_done = build_bcast(VB127, B127, xs_done)
    bs1_done = build_bcast(VBS1, BS1, b127_done)

    QSET = [(QUU, QY, QRA, QT, QSG), (QAU, QR0, QRB, QNA, MB[:])]
    xact = {}

    def x_u(k):
        U, Y, RA, T, SG = QSET[k % 2]
        p.on("vector", lambda E, k=k, U=U: E.tensor_tensor(
            U, chunk(Hh, k), B127[:], ALU.mult),
            wait=([("sDV", bs1_done)] if k == 0 else []), inc=("sDV", 1))
        p.on("scalar", lambda E, U=U, Y=Y, SG=SG: (
            E.activation(Y, U, AF.Abs),
            E.activation(SG, U, AF.Sign))[-1],
            wait=[("sDV", p.cnt["sDV"])], inc=("sAC", 1))
        xact[k] = p.cnt["sAC"]

    x_u(0)
    for k in range(KC):
        if k + 1 < KC:
            x_u(k + 1)
        U, Y, RA, T, SG = QSET[k % 2]

        def xdve(E, k=k, Y=Y, RA=RA, T=T, SG=SG):
            E.tensor_scalar(Y, Y, 128.0, None, ALU.add)
            E.tensor_scalar(RA, Y, -127.5, None, ALU.add)
            for bit in (64.0, 32.0, 16.0, 8.0, 4.0, 2.0, 1.0):
                E.tensor_scalar(T, RA, bit, bit, ALU.is_ge, ALU.mult)
                E.tensor_tensor(RA, RA, T, ALU.subtract)
            E.scalar_tensor_tensor(T, RA, -1.0, Y, ALU.mult, ALU.add)
            return E.scalar_tensor_tensor(chunk(Hh, k), T, -127.5, SG,
                                          ALU.add, ALU.mult)
        p.on("vector", xdve, wait=[("sAC", xact[k])], inc=("sDV", 1))
    xq_done = p.cnt["sDV"]
    if DEBUG_TAPS:
        p.on("gpsimd", lambda E: E.dma_start(out=tap_xq, in_=Hh[:]),
             wait=[("sDV", p.cnt["sDV"])], inc=("sTAP", 16), dma=True)
        def tapstm(E):
            return E.dma_start(out=tap_stm[:, 0:16], in_=STM[:])
        p.on("gpsimd", tapstm, inc=("sTAP", 16), dma=True)
        p.on("gpsimd", lambda E, v=p.cnt["sTAP"]: E.wait_ge(sems["sTAP"], v))
        p.on("gpsimd", lambda E: E.dma_start(out=tap_bs[:, 0:L], in_=B127[:]),
             inc=("sTAP", 16), dma=True)
        p.on("gpsimd", lambda E, v=p.cnt["sTAP"]: E.wait_ge(sems["sTAP"], v))
        p.on("gpsimd", lambda E: E.dma_start(out=tap_bs[:, L:2 * L], in_=BS1[:]),
             inc=("sTAP", 16), dma=True)
        p.on("gpsimd", lambda E, v=p.cnt["sTAP"]: E.wait_ge(sems["sTAP"], v))

    w1stage = [WSTG[:, i * 1024:(i + 1) * 1024] for i in range(4)]
    w1q = [W1Q[:, 0:1024], W1Q[:, 1024:2048], RB[:, 0:1024], RB[:, 1024:2048]]
    tern_st = [(SC[1][:, 0:1024], SC[1][:, 1024:2048]),
               (SC[3][:, 0:1024], SC[3][:, 1024:2048])]
    tern_dvs = {}
    tern_pl = {}
    hc_mm = {}
    gelu_acs = []
    ffn1_evacs = []
    spill_sps = []

    def emit_tern(hc):
        sp4 = hc % 4
        wv = []
        if hc >= 4:
            wv = [tern_dvs[hc - 4]]
        sw1 = ["sW1a", "sW1b", "sW1c", "sW1d"][sp4]
        p.on("sync", lambda E, hc=hc, sp4=sp4: E.dma_start(
            out=w1stage[sp4], in_=w1T[hc]), wait=wv, inc=(sw1, 16))
        w1n = p.cnt[sw1]
        tp_, tn_ = tern_st[hc % 2]

        def tern_ts(E, sp4=sp4, tp_=tp_, tn_=tn_):
            E.tensor_scalar(tp_, w1stage[sp4], WS1i[:, 0:1], 0.5,
                            ALU.mult, ALU.is_ge)
            return E.tensor_scalar(tn_, w1stage[sp4], WS1n[:, 0:1], 0.5,
                                   ALU.mult, ALU.is_ge)
        tw = [(sw1, w1n)]
        if hc >= 2:
            tw += [("sPL", tern_pl[hc - 2])]
        p.on("vector", tern_ts, wait=tw, inc=("sDV", 1))
        subw = [("sDV", p.cnt["sDV"])]
        if hc >= 4:
            subw += [("sMM", hc_mm[hc - 4])]
        elif hc >= 2:
            subw += [("sDV", xq_done)]
        p.on("gpsimd", lambda E, hc=hc, tp_=tp_, tn_=tn_: E.tensor_tensor(
            w1q[hc % 4], tp_, tn_, ALU.subtract),
            wait=subw, inc=("sPL", 1))
        tern_pl[hc] = p.cnt["sPL"]
        tern_dvs[hc] = ("sPL", p.cnt["sPL"])

    emit_tern(0)
    emit_tern(1)
    for hc in range(HC):
        if hc + 2 < HC:
            emit_tern(hc + 2)
        for j in range(NNT):
            jp = (hc * NNT + j) % 4
            gidx = hc * NNT + j
            for k in range(KC):
                wv = []
                if k == 0:
                    wv = [tern_dvs[hc]]
                    if gidx == 0:
                        wv += [("sDV", xq_done)]
                    if gidx >= 4:
                        wv += [("sDV", ffn1_evacs[gidx - 4])]
                p.on("tensor", lambda E, hc=hc, j=j, k=k, jp=jp:
                     E.matmul(PB[jp][:, :],
                              w1q[hc % 4][:, k * 128:(k + 1) * 128],
                              ntile(Hh, k, j),
                              start=(k == 0), stop=(k == KC - 1)),
                     wait=wv, inc=("sMM", 1) if k == KC - 1 else None)
            mmn = p.cnt["sMM"]
            hc_mm[hc] = mmn
            p.on("vector", lambda E, j=j, jp=jp: E.tensor_tensor(
                SC[2][:, j * 512:(j + 1) * 512], PB[jp][:, :],
                BS1[:, j * 512:(j + 1) * 512], ALU.mult),
                wait=[("sMM", mmn)], inc=("sDV", 1))
            dvn = p.cnt["sDV"]
            ffn1_evacs.append(dvn)
            p.on("scalar", lambda E, hc=hc, j=j: E.activation(
                H1Tp[hc % 2][:, j * 512:(j + 1) * 512],
                SC[2][:, j * 512:(j + 1) * 512], AF.Gelu,
                bias=B1Vt[:, hc:hc + 1]),
                wait=[("sDV", dvn)], inc=("sAC", 1))
        gelu_acs.append(p.cnt["sAC"])
        H1T = H1Tp[hc % 2]
        if hc == 0:
            p.on("vector", lambda E, H1T=H1T: (
                E.tensor_copy(M1[:], H1T),
                E.tensor_copy(MB[:], H1T))[-1],
                 wait=[("sAC", gelu_acs[-1])], inc=("sDV", 1))
        else:
            p.on("vector", lambda E, H1T=H1T: (
                E.tensor_tensor(M1[:], M1[:], H1T, ALU.max),
                E.tensor_tensor(MB[:], MB[:], H1T, ALU.min))[-1],
                 wait=[("sAC", gelu_acs[-1])], inc=("sDV", 1))
        ssp = "sSPa" if hc % 2 == 0 else "sSPb"
        p.on("gpsimd", lambda E, hc=hc, H1T=H1T: E.dma_start(
            out=h1sp[hc], in_=H1T),
             wait=[("sAC", gelu_acs[-1])], inc=(ssp, 16), dma=True)
        spill_sps.append((ssp, p.cnt[ssp]))
        if DEBUG_TAPS and hc == 0:
            p.on("gpsimd", lambda E: E.dma_start(out=tap_h1, in_=H1Tp[0]),
                 inc=("sTAP", 16), dma=True)
            p.on("gpsimd", lambda E, v=p.cnt["sTAP"]: E.wait_ge(sems["sTAP"], v))
        if hc >= 1:
            sn, v = spill_sps[hc - 1]
            p.on("scalar", lambda E, sn=sn, v=v: E.wait_ge(sems[sn], v))
        if DEBUG_TAPS and hc == 0:
            p.on("scalar", lambda E, v=p.cnt["sTAP"]: E.wait_ge(sems["sTAP"], v))
    p.on("vector", lambda E: (
        E.tensor_scalar(MB[:], MB[:], -1.0, None, ALU.mult),
        E.tensor_tensor(M1[:], M1[:], MB[:], ALU.max))[-1],
        inc=("sDV", 1))
    pl_m1h = ("sDV", p.cnt["sDV"])

    w2stg4 = [SC[1][:], SC[2][:], SC[3][:], WSTG[:, 0:2048]]
    w2sem4 = ["sW2", "sW2b", "sW2c", "sW2d"]
    w2u_dv = {}
    w2spl = {}
    for u in range(2 * KC):
        s, h = u // 2, u % 2
        st = u % 4
        if u < 4:
            wv = [("sSPa", p.cnt["sSPa"]), ("sSPb", p.cnt["sSPb"])]
        else:
            wv = [("sDV", w2u_dv[u - 4])]
        sw = w2sem4[st]
        p.on("sync", lambda E, s=s, h=h, st=st: E.dma_start(
            out=w2stg4[st], in_=w2T[s][:, h * 2048:(h + 1) * 2048]),
            wait=wv, inc=(sw, 16))
        w2n = p.cnt[sw]

        ioff = 4096 + (s % 2) * 4096

        def tern2(E, st=st, h=h, ioff=ioff):
            E.tensor_scalar(W1Q[:], w2stg4[st], WS2i[:, 0:1], 0.5,
                            ALU.mult, ALU.is_ge)
            E.tensor_scalar(RB[:], w2stg4[st], WS2n[:, 0:1], 0.5,
                            ALU.mult, ALU.is_ge)
            return E.tensor_tensor(
                X[:, ioff + h * 2048:ioff + (h + 1) * 2048],
                W1Q[:], RB[:], ALU.subtract)
        tv = [(sw, w2n)]
        if h == 0 and s >= 2:
            tv += [w2spl[s - 2]]
        p.on("vector", tern2, wait=tv, inc=("sDV", 1))
        w2u_dv[u] = p.cnt["sDV"]
        if h == 1:
            ssp_s = "sSP" if s % 2 == 0 else "sSPa"
            p.on("gpsimd", lambda E, s=s, ioff=ioff: E.dma_start(
                out=w2qsp[:, :, s * 128:(s + 1) * 128].rearrange(
                    "k p c -> p k c"),
                in_=X[:, ioff:ioff + 4096].rearrange("p (k c) -> p k c",
                                                     k=HC)),
                wait=[("sDV", w2u_dv[u])], inc=(ssp_s, 16), dma=True)
            w2spl[s] = (ssp_s, p.cnt[ssp_s])
    sp_all = p.cnt["sSP"]

    stmh_done = make_scales(STMH, [pl_m1h])

    def hscale_vecs(E):
        E.reciprocal(STM[:], STMH[:])
        E.tensor_scalar(VB127[:], STM[:], QB, None, ALU.mult)
        E.tensor_scalar(STM[:], STMH[:], 1.0 / QB, None, ALU.mult)
        return E.tensor_scalar(VBS2[:], STM[:], WS2v[:, 0:1], None, ALU.mult)
    p.on("vector", hscale_vecs, wait=[("sDV", stmh_done)], inc=("sDV", 1))
    hs_done = p.cnt["sDV"]
    b127h_done = build_bcast(VB127, B127, hs_done)
    bs2_done = build_bcast(VBS2, BS2, b127h_done)
    if DEBUG_TAPS:
        p.on("gpsimd", lambda E: E.dma_start(out=tap_bs[:, 2 * L:3 * L],
                                             in_=BS2[:]),
             wait=[("sDV", bs2_done)], inc=("sTAP", 16), dma=True)
        p.on("gpsimd", lambda E, v=p.cnt["sTAP"]: E.wait_ge(sems["sTAP"], v))
        p.on("gpsimd", lambda E: E.dma_start(out=tap_stm[:, 16:32],
                                             in_=STMH[:]),
             inc=("sTAP", 16), dma=True)
        p.on("gpsimd", lambda E, v=p.cnt["sTAP"]: E.wait_ge(sems["sTAP"], v))
    p.barrier()

    HHS = lambda a, b: Hh[:, a:b]
    FIN = [HHS(0, 4096), XS(0, 4096)]
    FUU = [HHS(4096, 8192), XS(4096, 8192)]
    FRHO = [HHS(8192, 12288), XS(8192, 12288)]
    FSG = [HHS(12288, 16384), XS(12288, 16384)]
    H1Q2 = [XS(0, 2048), XS(2048, 4096), XS(4096, 6144), XS(6144, 8192)]
    W2R = [M1[:, 0:1024], M1[:, 1024:2048], RB[:, 0:1024], RB[:, 1024:2048]]

    def lad_dve2(E, pf):
        E.tensor_scalar(FIN[pf], FIN[pf], 128.0, None, ALU.add)
        E.tensor_scalar(FRHO[pf], FIN[pf], -127.5, None, ALU.add)
        for bit in (64.0, 32.0, 16.0, 8.0, 4.0, 2.0, 1.0):
            E.tensor_scalar(FUU[pf], FRHO[pf], bit, bit,
                            ALU.is_ge, ALU.mult)
            E.tensor_tensor(FRHO[pf], FRHO[pf], FUU[pf], ALU.subtract)
        E.scalar_tensor_tensor(FUU[pf], FRHO[pf], -1.0, FIN[pf],
                               ALU.mult, ALU.add)
        return E.scalar_tensor_tensor(FIN[pf], FUU[pf], -127.5,
                                      FSG[pf], ALU.add, ALU.mult)

    prev_evac_dv = 0
    bank_free_dv = 0
    last_mm = 0
    mm_vals = {}
    lad_vals = {}
    spl_vals = {}
    LBITS = (32.0, 16.0, 8.0, 4.0, 2.0, 1.0)

    for gi, (d0, d1) in enumerate([(0, 1), (2, 3), (4, 5), (6, 7)]):
        if gi == 0:
            uacts = {}
            udma = {}
            NPAIR = HC // 2

            def emit_uact(pr):
                pf = pr % 2
                srd = "sRDa" if pf == 0 else "sRDb"
                if pr <= 1:
                    wv = [("sSP", sp_all),
                          ("sSPa", p.cnt["sSPa"]), ("sSPb", p.cnt["sSPb"])]
                    if pr == 0:
                        wv += [("sDV", bs2_done)]
                else:
                    wv = [("sMM", mm_vals[(0, pr - 2)]),
                          ("sDV", lad_vals[(0, pr - 2)]),
                          (("sH1a" if pf == 0 else "sH1b"),
                           spl_vals[(0, pr - 2)])]
                p.on("sync", lambda E, pr=pr, pf=pf: E.dma_start(
                    out=FIN[pf].rearrange("p (k c) -> p k c", k=2),
                    in_=h1sp[2 * pr:2 * pr + 2].rearrange("k p c -> p k c")),
                    wait=wv, inc=(srd, 16))
                p.on("sync", lambda E, pr=pr, pf=pf: E.dma_start(
                    out=W2R[2 * pf], in_=w2qsp[2 * pr]), inc=(srd, 16))
                p.on("sync", lambda E, pr=pr, pf=pf: E.dma_start(
                    out=W2R[2 * pf + 1], in_=w2qsp[2 * pr + 1]),
                    inc=(srd, 16))
                udma[pr] = (srd, p.cnt[srd])

            def emit_ulad(pr):
                pf = pr % 2
                p.on("vector", lambda E, pf=pf: (
                    E.tensor_tensor(FUU[pf][:, 0:2048], FIN[pf][:, 0:2048],
                                    B127[:], ALU.mult),
                    E.tensor_tensor(FUU[pf][:, 2048:4096],
                                    FIN[pf][:, 2048:4096],
                                    B127[:], ALU.mult))[-1],
                     wait=[udma[pr]], inc=("sDV", 1))
                p.on("scalar", lambda E, pf=pf: (
                    E.activation(FIN[pf], FUU[pf], AF.Abs),
                    E.activation(FSG[pf], FUU[pf], AF.Sign))[-1],
                     wait=[("sDV", p.cnt["sDV"])], inc=("sAC", 1))
                uacts[pr] = p.cnt["sAC"]

            emit_uact(0)
            emit_ulad(0)
            for pr in range(NPAIR):
                pf = pr % 2
                if pr + 1 < NPAIR:
                    emit_uact(pr + 1)
                p.on("vector", lambda E, pf=pf: (
                    E.tensor_scalar(FIN[pf], FIN[pf], 128.0, None, ALU.add),
                    E.tensor_scalar(FRHO[pf], FIN[pf], -127.5, None,
                                    ALU.add),
                    E.tensor_scalar(FUU[pf], FRHO[pf], 64.0, 64.0,
                                    ALU.is_ge, ALU.mult),
                    E.tensor_tensor(FRHO[pf], FRHO[pf], FUU[pf],
                                    ALU.subtract),
                    E.tensor_scalar(FUU[pf], FRHO[pf], 32.0, 32.0,
                                    ALU.is_ge, ALU.mult),
                    E.tensor_tensor(FRHO[pf], FRHO[pf], FUU[pf],
                                    ALU.subtract),
                    E.tensor_scalar(FUU[pf], FRHO[pf], 16.0, 16.0,
                                    ALU.is_ge, ALU.mult),
                    E.tensor_tensor(FRHO[pf], FRHO[pf], FUU[pf],
                                    ALU.subtract))[-1],
                     wait=[("sAC", uacts[pr])], inc=("sDV", 1))
                if pr + 1 < NPAIR:
                    emit_ulad(pr + 1)

                def dve_tail(E, pf=pf):
                    for bit in (8.0, 4.0, 2.0, 1.0):
                        E.tensor_scalar(FUU[pf], FRHO[pf], bit, bit,
                                        ALU.is_ge, ALU.mult)
                        E.tensor_tensor(FRHO[pf], FRHO[pf], FUU[pf],
                                        ALU.subtract)
                    E.scalar_tensor_tensor(FUU[pf], FRHO[pf], -1.0, FIN[pf],
                                           ALU.mult, ALU.add)
                    return E.scalar_tensor_tensor(FIN[pf], FUU[pf], -127.5,
                                                  FSG[pf], ALU.add, ALU.mult)
                p.on("vector", dve_tail, inc=("sDV", 1))
                lad_vals[(0, pr)] = p.cnt["sDV"]
                sh1 = "sH1a" if pf == 0 else "sH1b"
                p.on("gpsimd", lambda E, pr=pr, pf=pf: E.dma_start(
                    out=h1qsp[2 * pr:2 * pr + 2].rearrange("k p c -> p k c"),
                    in_=FIN[pf].rearrange("p (k c) -> p k c", k=2)),
                    wait=[("sDV", lad_vals[(0, pr)])], inc=(sh1, 16),
                    dma=True)
                spl_vals[(0, pr)] = p.cnt[sh1]
                mmw = [("sDV", lad_vals[(0, pr)])]
                for kch in range(2):
                    for nt in range(NNT):
                        for di, d in enumerate((d0, d1)):
                            p.on("tensor",
                                 lambda E, pf=pf, kch=kch, nt=nt, di=di, d=d,
                                 pr=pr:
                                 E.matmul(
                                     PB[4 * di + nt][:, :],
                                     W2R[2 * pf + kch][:,
                                                       d * 128:(d + 1) * 128],
                                     FIN[pf][:, kch * 2048 + nt * 512:
                                             kch * 2048 + (nt + 1) * 512],
                                     start=(pr == 0 and kch == 0),
                                     stop=(pr == NPAIR - 1 and kch == 1)),
                                 wait=mmw
                                 if (kch == 0 and nt == 0 and di == 0)
                                 else [],
                                 inc=("sMM", 1)
                                 if (kch == 1 and nt == NNT - 1 and di == 1)
                                 else None)
                mm_vals[(0, pr)] = p.cnt["sMM"]
        else:
            for kc in range(HC):
                pp = kc % 4
                srd = ["sRDa", "sRDb", "sW1a", "sW1b"][pp]
                wv = []
                if kc < 4:
                    wv = [("sMM", last_mm)]
                    if gi == 1:
                        wv += [("sH1a", p.cnt["sH1a"]),
                               ("sH1b", p.cnt["sH1b"])]
                elif (gi, kc - 4) in mm_vals:
                    wv = [("sMM", mm_vals[(gi, kc - 4)])]
                heng = "sync" if kc % 2 == 0 else "scalar"
                weng = "scalar" if kc % 2 == 0 else "sync"
                p.on(heng, lambda E, kc=kc, pp=pp: E.dma_start(
                    out=H1Q2[pp], in_=h1qsp[kc]), wait=wv, inc=(srd, 16),
                    dma=True)
                p.on(weng, lambda E, kc=kc, pp=pp: E.dma_start(
                    out=W2R[pp], in_=w2qsp[kc]), wait=wv, inc=(srd, 16),
                    dma=True)
                rdn = p.cnt[srd]
                mmw = [(srd, rdn)]
                if kc == 0:
                    mmw += [("sDV", bank_free_dv)]
                for nt in range(NNT):
                    for di, d in enumerate((d0, d1)):
                        p.on("tensor",
                             lambda E, kc=kc, pp=pp, nt=nt, di=di, d=d:
                             E.matmul(PB[4 * di + nt][:, :],
                                      W2R[pp][:, d * 128:(d + 1) * 128],
                                      H1Q2[pp][:, nt * 512:(nt + 1) * 512],
                                      start=(kc == 0), stop=(kc == HC - 1)),
                             wait=mmw if (nt == 0 and di == 0) else [],
                             inc=("sMM", 1) if (nt == NNT - 1 and di == 1)
                             else None)
                mm_vals[(gi, kc)] = p.cnt["sMM"]
        mm_g = p.cnt["sMM"]
        last_mm = mm_g
        bank_free_trk = None
        for ei in range(8):
            di, nt = ei // 4, ei % 4
            d = (d0, d1)[di]
            pp = ei % 2
            sfin = "sFINa" if pp == 0 else "sFINb"
            wv = [("sMM", mm_g)] if ei == 0 else []
            if ei >= 2 or gi >= 1:
                wv += [(sfin, p.cnt[sfin])]
            p.on("vector", lambda E, di=di, nt=nt, pp=pp: E.tensor_tensor(
                OUTP[pp], PB[4 * di + nt][:, :],
                BS2[:, nt * 512:(nt + 1) * 512], ALU.mult),
                wait=wv, inc=("sDV", 1))
            dvn = p.cnt["sDV"]
            if ei == 7:
                bank_free_trk = dvn
            p.on("scalar", lambda E, d=d, pp=pp: E.activation(
                OUTP[pp], OUTP[pp], AF.Identity,
                bias=B2G[:, d:d + 1], scale=GB[:, d:d + 1]),
                wait=[("sDV", dvn)], inc=("sAC", 1))
            acn = p.cnt["sAC"]
            p.on("vector", lambda E, d=d, nt=nt, pp=pp: E.tensor_tensor(
                OUTP[pp], OUTP[pp],
                X2[:, d * L + nt * 512:d * L + (nt + 1) * 512], ALU.add),
                wait=[("sAC", acn)], inc=("sDV", 1))
            dvn2 = p.cnt["sDV"]
            p.on("gpsimd", lambda E, d=d, nt=nt, pp=pp: E.dma_start(
                out=out_img[d][:, nt * 512:(nt + 1) * 512], in_=OUTP[pp]),
                wait=[("sDV", dvn2)], inc=(sfin, 16), dma=True)
        prev_evac_dv = p.cnt["sDV"]
        bank_free_dv = bank_free_trk

    fa, fb = p.cnt["sFINa"], p.cnt["sFINb"]
    p.on("sync", lambda E: E.wait_ge(sems["sFINa"], fa))
    p.on("sync", lambda E: E.wait_ge(sems["sFINb"], fb))

    with nc.Block() as block:
        @block.sync
        def _(E):
            for f in p.ops["sync"]:
                f(E)

        @block.vector
        def _(E):
            E2 = Ser(E, sems["sVq"])
            for f in p.ops["vector"]:
                f(E2)

        @block.scalar
        def _(E):
            E2 = Ser(E, sems["sSq"])
            for f in p.ops["scalar"]:
                f(E2)

        @block.tensor
        def _(E):
            for f in p.ops["tensor"]:
                f(E)

        @block.gpsimd
        def _(E):
            E2 = Ser(E, sems["sPq"])
            for f in p.ops["gpsimd"]:
                f(E2)

    return nc


_NC_CACHE = None


def kernel(L0_attn_x, L0_attn_W_f, L0_attn_b_f, L0_attn_W_i, L0_attn_b_i,
           L0_attn_W_h, L0_attn_b_h, L0_ffn_W1, L0_ffn_b1, L0_ffn_W2,
           L0_ffn_b2, L0_c, L0_wmod):
    global _NC_CACHE
    import ml_dtypes
    x = np.asarray(L0_attn_x, dtype=np.float32)
    W_f = np.asarray(L0_attn_W_f, dtype=np.float32)
    W_i = np.asarray(L0_attn_W_i, dtype=np.float32)
    W_h = np.asarray(L0_attn_W_h, dtype=np.float32)
    b_f = np.asarray(L0_attn_b_f, dtype=np.float32)
    b_i = np.asarray(L0_attn_b_i, dtype=np.float32)
    b_h = np.asarray(L0_attn_b_h, dtype=np.float32)
    W1 = np.asarray(L0_ffn_W1, dtype=np.float32)
    b1 = np.asarray(L0_ffn_b1, dtype=np.float32)
    W2 = np.asarray(L0_ffn_W2, dtype=np.float32)
    b2 = np.asarray(L0_ffn_b2, dtype=np.float32)
    cmat = np.asarray(L0_c, dtype=np.float32)
    wmod = np.asarray(L0_wmod, dtype=np.float32)

    def lay_lhsT(W, mout, kin):
        return np.ascontiguousarray(
            W.reshape(mout, 128, kin, 128).transpose(3, 0, 2, 1)
            .reshape(128, mout * kin * 128))

    shared = {
        "wfT": lay_lhsT(W_f, KC, KC),
        "wiT": lay_lhsT(W_i, KC, KC),
        "whT": lay_lhsT(W_h, KC, KC),
        "w1T": np.ascontiguousarray(
            W1.reshape(HC, 128, KC, 128).transpose(0, 3, 2, 1)
            .reshape(HC, 128, KC * 128)),
        "w2T": np.ascontiguousarray(
            W2.reshape(KC, 128, HC, 128).transpose(0, 3, 2, 1)
            .reshape(KC, 128, HC * 128)),
        "wmodT": np.ascontiguousarray(
            wmod.reshape(KC, 128, 12, 512).transpose(2, 1, 0, 3)
            .reshape(12, 128, KC * 512)),
        "idnf": np.eye(128, dtype=np.float32),
        "cpack_b": np.concatenate([
            np.ones((128, 128), ml_dtypes.bfloat16),
            np.eye(128, dtype=ml_dtypes.bfloat16)], axis=1),

    }
    constv = np.broadcast_to(np.array(
        [1e-5, 1.0, 0.5, -1.0, -0.5, 2.0, 128.0, -127.5, -64.0,
         0.0, 0.0, 0.0], np.float32), (128, 12))
    base_pack = np.concatenate([
        np.zeros((128, 8), np.float32),
        np.ascontiguousarray(b_f.reshape(KC, 128).T),
        np.ascontiguousarray(b_i.reshape(KC, 128).T),
        np.ascontiguousarray(b_h.reshape(KC, 128).T),
        np.ascontiguousarray(b1.reshape(HC, 128).T),
        np.ascontiguousarray(b2.reshape(KC, 128).T),
        constv,
        np.ones((128, 128), np.float32)], axis=1)
    in_maps = []
    for b in range(B):
        m = dict(shared)
        m["xT"] = np.ascontiguousarray(x[b].T).reshape(KC, 128, L)
        pk = base_pack.copy()
        pk[:, 0:8] = cmat[b].reshape(KC, 128).T
        m["cpack_f"] = pk
        in_maps.append(m)

    if _NC_CACHE is None:
        _NC_CACHE = build_nc()
    res = run_bass_kernel_spmd(_NC_CACHE, in_maps, core_ids=list(range(B)))
    outs = []
    for b in range(B):
        oi = np.asarray(res.results[b]["out_img"], dtype=np.float32)
        outs.append(np.ascontiguousarray(oi.reshape(D, L).T))
    return np.stack(outs, axis=0)


# version 33 (final)
# speedup vs baseline: 1.0466x; 1.0098x over previous
import numpy as np
import concourse.bass as bass
import concourse.mybir as mybir
from concourse.bass_utils import run_bass_kernel_spmd

F32 = mybir.dt.float32
BF16 = mybir.dt.bfloat16
U8 = mybir.dt.uint8
AF = mybir.ActivationFunctionType
ALU = mybir.AluOpType
AX = mybir.AxisListType

DEBUG_TAPS = False
B, L, D, HID = 8, 2048, 1024, 4096
KC = D // 128
HC = HID // 128
NT = 512
NNT = L // NT
EPS_LN = 1e-5
EPS_Q = 1e-5
QB = 127.0

ENGINES = ("sync", "vector", "scalar", "tensor", "gpsimd")


class Prog:

    def __init__(self):
        self.ops = {e: [] for e in ENGINES}
        self.cnt = {}
        self.sems = None

    def sem(self, name):
        self.cnt[name] = 0

    def inc(self, name, n=1):
        self.cnt[name] += n
        return self.cnt[name]

    def on(self, eng, fn, wait=(), inc=None, dma=False):
        sems = self.sems
        serialized = (eng in ("vector", "scalar", "gpsimd")) and not dma
        for (s, v) in wait:
            if v > 0:
                self.ops[eng].append(lambda E, s=s, v=v: E.wait_ge(sems[s], v))
        if inc is not None:
            sname, amt = inc
            val = self.inc(sname, amt)
            if fn is None:
                self.ops[eng].append(
                    lambda E, s=sname, a=amt: E.sem_inc(sems[s], a))
            elif serialized:
                def run(E, f=fn, s=sname, a=amt):
                    f(E)
                    E.ser_flush_wait()
                    E.sem_inc(sems[s], a)
                self.ops[eng].append(run)
            else:
                self.ops[eng].append(
                    lambda E, f=fn, s=sname, a=amt: f(E).then_inc(sems[s], a))
            return val
        assert fn is not None
        self.ops[eng].append(fn)
        return None

    def barrier(self):
        for e in ENGINES:
            self.inc("sBR", 1)
        target = self.cnt["sBR"]
        sems = self.sems
        for e in ENGINES:
            self.ops[e].append(lambda E: E.sem_inc(sems["sBR"], 1))
            self.ops[e].append(lambda E: E.wait_ge(sems["sBR"], target))


SER_METHS = {"tensor_copy", "tensor_tensor", "tensor_scalar",
              "tensor_single_scalar", "scalar_tensor_tensor",
              "tensor_tensor_scan", "tensor_reduce", "select", "reciprocal",
              "memset", "memzero", "activation", "sign", "square", "sqrt",
              "mul", "add", "copy", "tensor_add", "tensor_sub", "tensor_mul",
              "tensor_max", "tensor_relu", "tensor_scalar_add",
              "tensor_scalar_mul", "tensor_scalar_max", "tensor_scalar_min"}


class Ser:

    def __init__(self, eng, sem):
        self._e, self._s, self._n = eng, sem, 0

    def ser_flush_wait(self):
        if self._n:
            self._e.wait_ge(self._s, self._n)

    def __getattr__(self, m):
        f = getattr(self._e, m)
        if m in SER_METHS:
            def w(*a, **k):
                if self._n:
                    self._e.wait_ge(self._s, self._n)
                ins = f(*a, **k)
                self._n += 1
                return ins.then_inc(self._s, 1)
            return w
        return f


def build_nc():
    nc = bass.Bass("TRN2", target_bir_lowering=False, debug=False)
    p = Prog()

    def din(name, shape, dt=F32):
        return nc.dram_tensor(name, shape, dt, kind="ExternalInput").ap()

    xT = din("xT", [KC, 128, L])
    cpack_f = din("cpack_f", [128, 84 + 128])
    cpack_b = din("cpack_b", [128, 256], BF16)
    wfT = din("wfT", [128, KC * D])
    wiT = din("wiT", [128, KC * D])
    whT = din("whT", [128, KC * D])
    w1T = din("w1T", [HC, 128, KC * 128])
    w2T = din("w2T", [KC, 128, HC * 128])
    wmodT = din("wmodT", [12, 128, 4096])
    idnf = din("idnf", [128, 128])

    out_img = nc.dram_tensor("out_img", [KC, 128, L], F32,
                             kind="ExternalOutput").ap()
    if DEBUG_TAPS:
        tap_hln = nc.dram_tensor("tap_hln", [128, KC * L], BF16,
                                 kind="ExternalOutput").ap()
        tap_x2 = nc.dram_tensor("tap_x2", [128, KC * L], BF16,
                                kind="ExternalOutput").ap()
        tap_xq = nc.dram_tensor("tap_xq", [128, KC * L], BF16,
                                kind="ExternalOutput").ap()
        tap_stm = nc.dram_tensor("tap_stm", [128, 64], F32,
                                 kind="ExternalOutput").ap()
        tap_mod = nc.dram_tensor("tap_mod", [128, 48], F32,
                                 kind="ExternalOutput").ap()
        tap_bs = nc.dram_tensor("tap_bs", [128, 3 * L], BF16,
                                kind="ExternalOutput").ap()
        tap_h1 = nc.dram_tensor("tap_h1", [128, L], BF16,
                                kind="ExternalOutput").ap()
        tap_f = nc.dram_tensor("tap_f", [128, L], BF16,
                               kind="ExternalOutput").ap()
        tap_v = nc.dram_tensor("tap_v", [128, L], BF16,
                               kind="ExternalOutput").ap()
        tap_hm = nc.dram_tensor("tap_hm", [128, L], F32,
                                kind="ExternalOutput").ap()
    h1sp = nc.dram_tensor("h1sp", [HC, 128, L], BF16, kind="Internal").ap()
    h1qsp = nc.dram_tensor("h1qsp", [HC, 128, L], BF16, kind="Internal").ap()
    w2qsp = nc.dram_tensor("w2qsp", [HC, 128, KC * 128], BF16,
                           kind="Internal").ap()

    def sb(name, shape, dt):
        return nc.alloc_sbuf_tensor(name, shape, dt).ap()

    X = sb("X", [128, KC * L], BF16)
    Hh = sb("Hh", [128, KC * L], BF16)
    X2 = sb("X2", [128, KC * L], BF16)
    SC = [sb(f"SC{i}", [128, L], F32) for i in range(5)]
    MKm = sb("MKm", [128, L], U8)
    MB = sb("MB", [128, L], BF16)
    RB = sb("RB", [128, L], BF16)
    M1 = sb("M1", [128, L], BF16)
    WSTG = sb("WSTG", [128, 4096], F32)
    WSB = sb("WSB", [128, 4096], BF16)
    W1Q = sb("W1Q", [128, 2048], BF16)
    B127 = sb("B127", [128, L], BF16)
    BS1 = sb("BS1", [128, L], BF16)
    BS2 = sb("BS2", [128, L], BF16)
    ROW1 = sb("ROW1", [1, L], BF16)
    ROW2 = sb("ROW2", [1, L], BF16)
    SCf = sb("SCf", [128, KC], F32)
    TMPa = sb("TMPa", [128, KC], F32)
    MODC = sb("MODC", [128, 48], F32)
    GA = sb("GAt", [128, KC], F32)
    GB = sb("GBt", [128, KC], F32)
    CA1 = sb("CA1", [128, KC], F32)
    CB1 = sb("CB1", [128, KC], F32)
    NBF = sb("NBF", [128, KC], F32)
    NBI = sb("NBI", [128, KC], F32)
    B2G = sb("B2G", [128, KC], F32)
    IDNF = sb("IDNF", [128, 128], F32)
    STM = sb("STM", [128, 16], F32)
    STMH = sb("STMH", [128, 16], F32)
    VB127 = sb("VB127", [128, 16], F32)
    VBS1 = sb("VBS1", [128, 16], F32)
    VBS2 = sb("VBS2", [128, 16], F32)
    WS1i = sb("WS1i", [128, 1], F32)
    WS1n = sb("WS1n", [128, 1], F32)
    WS2i = sb("WS2i", [128, 1], F32)
    WS2n = sb("WS2n", [128, 1], F32)
    WS1v = sb("WS1v", [128, 1], F32)
    WS2v = sb("WS2v", [128, 1], F32)
    WSrow = sb("WSrow", [1, 8], F32)
    CPF = sb("CPF", [128, 84 + 128], F32)
    CPB = sb("CPB", [128, 256], BF16)

    XS = lambda a, b: X[:, a:b]
    QUU, QAU = XS(0, 2048), XS(2048, 4096)
    QY, QR0 = XS(4096, 6144), XS(6144, 8192)
    QRA, QRB = XS(8192, 10240), XS(10240, 12288)
    QT, QNA = XS(12288, 14336), XS(14336, 16384)
    QSG = RB[:, 0:2048]
    H1Tp = [XS(0, 2048), XS(2048, 4096)]
    TRN2A = XS(4096, 6144)
    TRN2B = XS(6144, 8192)
    HP = [XS(6144, 6656), XS(6656, 7168)]
    H1Qp = [XS(7168, 7680), XS(7680, 8192)]
    W2QS = [XS(8192, 9216), XS(9216, 10240)]
    R0t = [XS(10240, 10752), XS(10752, 11264)]
    Rta = [XS(11264, 11776), XS(11776, 12288)]
    Rtb = [XS(12288, 12800), XS(12800, 13312)]
    Tt = [XS(13312, 13824), XS(13824, 14336)]
    UUt = [XS(14336, 14848), XS(14848, 15360)]
    AUt = [XS(15360, 15872), XS(15872, 16384)]
    NAt = [RB[:, 0:512], RB[:, 512:1024]]
    SGt = [RB[:, 1024:1536], RB[:, 1536:2048]]
    OUTP = [SC[2][:, 0:512], SC[2][:, 512:1024]]

    CVt = CPF[:, 0:8]
    BFVt = CPF[:, 8:16]
    BIVt = CPF[:, 16:24]
    BHVt = CPF[:, 24:32]
    B1Vt = CPF[:, 32:64]
    B2Vt = CPF[:, 64:72]
    CONSTV = CPF[:, 72:84]
    ONESRf = CPF[0:1, 84:212]
    _cvals = [EPS_LN, 1.0, 0.5, -1.0, -0.5, 2.0, 128.0, -127.5, -64.0,
              0.0, 0.0, 0.0]
    for _j, _v in enumerate(_cvals):
        nc.const_aps.register(F32, _v, CONSTV[:, 72 - 72 + _j:_j + 1]
                              if False else CPF[:, 72 + _j:72 + _j + 1])
    ONES128 = CPB[:, 0:128]
    IDN = CPB[:, 128:256]
    ONESRb = CPB[0:1, 0:128]
    ONESCb = CPB[:, 0:1]
    ONE1b = CPB[0:1, 0:1]

    PB = [nc.alloc_psum_tensor(f"PB{i}", [128, NT], F32).ap() for i in range(8)]

    semnames = ["sBR", "sMD", "sM1", "sM2", "sM3", "sLXa", "sLXb", "sCN", "sWMa", "sWMb", "sLWa", "sLWb",
                "sMW", "sW1a", "sW1b", "sW2", "sW2b", "sW2c", "sW2d", "sLWc", "sLWd", "sSP", "sSPa", "sSPb", "sRDa", "sRDb",
                "sFINa", "sFINb", "sMM", "sDV", "sAC", "sPL",
                "sVq", "sSq", "sPq", "sTAP", "sH1a", "sH1b", "sW1c", "sW1d"]
    sems = {s: nc.alloc_semaphore(s) for s in semnames}
    p.sems = sems
    for s in semnames:
        p.sem(s)

    chunk = lambda t, k: t[:, k * L:(k + 1) * L]
    ntile = lambda t, k, j: t[:, k * L + j * NT:k * L + (j + 1) * NT]
    slot = lambda s: MODC[:, s * 8:(s + 1) * 8]

    p.on("scalar", lambda E: E.dma_start(out=CPF[:], in_=cpack_f),
         inc=("sCN", 16), dma=True)
    p.on("scalar", lambda E: E.dma_start(out=CPB[:], in_=cpack_b),
         inc=("sCN", 16), dma=True)
    p.on("scalar", lambda E: E.dma_start(out=IDNF[:], in_=idnf),
         inc=("sCN", 16), dma=True)
    cn_all = p.cnt["sCN"]

    xstg = [SC[0], SC[1]]
    for k in range(KC):
        w = [("sDV", p.cnt["sDV"])] if k >= 2 else []
        sx = "sLXa" if k % 2 == 0 else "sLXb"
        p.on("sync", lambda E, k=k: E.dma_start(out=xstg[k % 2], in_=xT[k]),
             wait=w, inc=(sx, 16))
        p.on("vector", lambda E, k=k: E.tensor_copy(chunk(X, k), xstg[k % 2]),
             wait=[(sx, p.cnt[sx])], inc=("sDV", 1))
    xcast_done = p.cnt["sDV"]

    p.on("scalar", lambda E: E.activation(SCf[:], CVt[:], AF.Silu),
         wait=[("sCN", cn_all)], inc=("sAC", 1))
    ac_silu = p.cnt["sAC"]

    mod_dv_base = p.cnt["sDV"]
    mod_ev = {}
    modstg = [WSTG[:, 0:4096], None]

    def mod_piece(j, k):
        if j % 2 == 0:
            return WSTG[:, k * 512:(k + 1) * 512]
        t = SC[2] if k < 4 else SC[3]
        return t[:, (k % 4) * 512:(k % 4 + 1) * 512]

    def emit_mod_block(j):
        pp = j % 2
        if j >= 2:
            p.on("sync", lambda E, v=j - 1: E.wait_ge(sems["sMD"], v))
        swm = "sWMa" if pp == 0 else "sWMb"
        if pp == 0:
            p.on("sync", lambda E, j=j: E.dma_start(
                out=WSTG[:, 0:4096], in_=wmodT[j]), inc=(swm, 16))
        else:
            p.on("sync", lambda E, j=j: E.dma_start(
                out=SC[2][:], in_=wmodT[j][:, 0:2048]), inc=(swm, 16))
            p.on("sync", lambda E, j=j: E.dma_start(
                out=SC[3][:], in_=wmodT[j][:, 2048:4096]), inc=(swm, 16))
        wm_now = p.cnt[swm]
        for k in range(KC):
            w = []
            if k == 0:
                w = [(swm, wm_now), ("sAC", ac_silu)]
                if j >= 2:
                    w.append(("sDV", mod_ev[j - 1]))
            p.on("tensor", lambda E, j=j, k=k: E.matmul(
                PB[j % 2][0:1, :], SCf[:, k:k + 1], mod_piece(j, k),
                start=(k == 0), stop=(k == KC - 1)),
                wait=w, inc=("sMD", 1) if k == KC - 1 else None)
        mm_now = p.cnt["sMD"]
        if j < 4:
            dstrow = MB[0:1, j * 512:(j + 1) * 512]
        elif j < 8:
            dstrow = RB[0:1, (j - 4) * 512:(j - 4 + 1) * 512]
        else:
            dstrow = M1[0:1, (j - 8) * 512:(j - 8 + 1) * 512]
        p.on("vector", lambda E, d=dstrow, j=j: E.tensor_copy(
            d, PB[j % 2][0:1, :]), wait=[("sMD", mm_now)], inc=("sDV", 1))
        mod_ev[j] = p.cnt["sDV"]


    wabs_stage = [SC[4][:, 0:1024], SC[4][:, 1024:2048],
                  SC[3][:, 0:1024], SC[3][:, 1024:2048]]
    wabs_b = [WSB[:, i * 1024:(i + 1) * 1024] for i in range(4)]
    w2abs_stage = [SC[0], SC[1], WSTG[:, 0:2048], WSTG[:, 2048:4096]]
    w2abs_b = [X2[:, 6144 + i * 1024:6144 + (i + 1) * 1024] for i in range(4)]
    w2actn = {}
    w1actn = {}
    w1mm = {}
    w2mm_abs = {}

    def emit_w1_slab(s):
        if s < 10:
            st = s % 2
            w = [("sAC", w1actn[s - 2])] if s >= 2 else []
        elif s < 12:
            st = s % 4
            w = [("sMD", 12)]
        else:
            st = s % 4
            w = [("sAC", w1actn[s - 4])]
        slw = ["sLWa", "sLWb", "sLWc", "sLWd"][s % 4]
        p.on("gpsimd", lambda E, s=s, st=st: E.dma_start(
            out=wabs_stage[st], in_=w1T[s]),
             wait=w, inc=(slw, 16), dma=True)
        ob = s % 4
        wsMM = [("sM1", w1mm[s - 4])] if s >= 4 else []
        p.on("scalar", lambda E, s=s, st=st, ob=ob: E.activation(
            wabs_b[ob], wabs_stage[st], AF.Abs),
            wait=[(slw, p.cnt[slw])] + wsMM, inc=("sAC", 1))
        acn = p.cnt["sAC"]
        w1actn[s] = acn
        for h in range(2):
            p.on("tensor", lambda E, s=s, h=h, ob=ob: E.matmul(
                PB[4 + h][0:1, :], ONESCb[:],
                wabs_b[ob][:, h * 512:(h + 1) * 512],
                start=(s == 0), stop=(s == HC - 1)),
                wait=[("sAC", acn)] if h == 0 else [],
                inc=("sM1", 1))
        w1mm[s] = p.cnt["sM1"]

    def emit_w2_unit(t):
        if t < 2:
            st = t % 2
            w2w = [("sDV", xcast_done)]
        elif t < 6:
            st = t % 2
            w2w = [("sAC", w2actn[t - 2])]
        elif t < 8:
            st = t % 4
            w2w = [("sMD", 11)]
        else:
            st = t % 4
            w2w = [("sAC", w2actn[t - 4])]
        sw2 = ["sW2", "sW2b", "sW2c", "sW2d"][t % 4]
        p.on("scalar", lambda E, t=t, st=st: E.dma_start(
            out=w2abs_stage[st],
            in_=w2T[t // 2][:, (t % 2) * 2048:(t % 2) * 2048 + 2048]),
            wait=w2w, inc=(sw2, 16), dma=True)
        w2n = p.cnt[sw2]
        for h in range(2):
            a = 2 * t + h
            ob = a % 4
            wv2 = [(sw2, w2n)] if h == 0 else []
            if a >= 4:
                wv2 += [("sM2", w2mm_abs[a - 4])]
            p.on("scalar", lambda E, t=t, h=h, st=st, ob=ob: E.activation(
                w2abs_b[ob], w2abs_stage[st][:, h * 1024:(h + 1) * 1024],
                AF.Abs), wait=wv2, inc=("sAC", 1))
            a2 = p.cnt["sAC"]
            for q in range(2):
                p.on("tensor", lambda E, t=t, h=h, q=q, ob=ob: E.matmul(
                    PB[6][0:1, :], ONESCb[:],
                    w2abs_b[ob][:, q * 512:(q + 1) * 512],
                    start=(t == 0 and h == 0 and q == 0),
                    stop=(t == 15 and h == 1 and q == 1)),
                    wait=[("sAC", a2)] if q == 0 else [],
                    inc=("sM2", 1))
            w2mm_abs[a] = p.cnt["sM2"]
        w2actn[t] = p.cnt["sAC"]

    for j in range(12):
        emit_mod_block(j)
        if j < 10:
            emit_w1_slab(j)
        if j < 6:
            emit_w2_unit(j)
    mod_rows_done = p.cnt["sDV"]

    def modr_piece(t):
        if t < 16:
            return MB[0:1, t * 128:(t + 1) * 128]
        if t < 32:
            return RB[0:1, (t - 16) * 128:(t - 16) * 128 + 128]
        return M1[0:1, (t - 32) * 128:(t - 32) * 128 + 128]

    for t in range(48):
        w = [("sDV", mod_rows_done), ("sCN", cn_all)] if t == 0 else []
        p.on("tensor", lambda E, t=t: E.matmul(
            PB[2][:, t:t + 1], modr_piece(t), ONE1b[:],
            start=True, stop=True), wait=w,
            inc=("sMD", 1) if t == 47 else None)
    mm_modc = p.cnt["sMD"]
    p.on("vector", lambda E: E.tensor_copy(MODC[:], PB[2][:, 0:48]),
         wait=[("sMD", mm_modc)], inc=("sDV", 1))
    modc_done = p.cnt["sDV"]

    def derive(E):
        E.tensor_scalar(CA1[:], slot(1), 1.0, None, ALU.add)
        E.tensor_scalar(CB1[:], slot(4), 1.0, None, ALU.add)
        E.tensor_scalar(GA[:], slot(2), 0.0, 2.0, ALU.is_ge, ALU.mult)
        E.tensor_scalar(GA[:], GA[:], -1.0, None, ALU.add)
        E.tensor_scalar(TMPa[:], slot(2), -1.0, None, ALU.mult)
        E.tensor_tensor(TMPa[:], TMPa[:], slot(2), ALU.max)
        E.tensor_scalar(TMPa[:], TMPa[:], 0.25, None, ALU.max)
        E.tensor_tensor(GA[:], GA[:], TMPa[:], ALU.mult)
        E.tensor_scalar(GB[:], slot(5), 0.0, 2.0, ALU.is_ge, ALU.mult)
        E.tensor_scalar(GB[:], GB[:], -1.0, None, ALU.add)
        E.tensor_scalar(TMPa[:], slot(5), -1.0, None, ALU.mult)
        E.tensor_tensor(TMPa[:], TMPa[:], slot(5), ALU.max)
        E.tensor_scalar(TMPa[:], TMPa[:], 0.25, None, ALU.max)
        E.tensor_tensor(GB[:], GB[:], TMPa[:], ALU.mult)
        E.tensor_scalar(NBF[:], BFVt[:], -1.0, None, ALU.mult)
        E.tensor_scalar(NBI[:], BIVt[:], -1.0, None, ALU.mult)
        return E.tensor_tensor(B2G[:], GB[:], B2Vt[:], ALU.mult)
    p.on("vector", derive, wait=[("sDV", modc_done)], inc=("sDV", 1))

    for r in range(11):
        emit_w1_slab(10 + 2 * r)
        emit_w1_slab(11 + 2 * r)
        if r < 10:
            emit_w2_unit(6 + r)
    mm_w1abs = ("sM1", p.cnt["sM1"])
    mm_w2abs = ("sM2", p.cnt["sM2"])


    def ws1_fin(E):
        E.tensor_reduce(WSrow[0:1, 0:1], PB[4][0:1, :], AX.X, ALU.add)
        E.tensor_reduce(WSrow[0:1, 4:5], PB[5][0:1, :], AX.X, ALU.add)
        E.tensor_tensor(WSrow[0:1, 0:1], WSrow[0:1, 0:1], WSrow[0:1, 4:5],
                        ALU.add)
        E.tensor_scalar(WSrow[0:1, 2:3], WSrow[0:1, 0:1],
                        1.0 / (HID * D), None, ALU.mult)
        return E.tensor_scalar(WSrow[0:1, 2:3], WSrow[0:1, 2:3], EPS_Q,
                               None, ALU.max)
    p.on("vector", ws1_fin, wait=[mm_w1abs], inc=("sDV", 1))
    ws1_done = p.cnt["sDV"]

    def ws2_fin(E):
        E.tensor_reduce(WSrow[0:1, 1:2], PB[6][0:1, :], AX.X, ALU.add)
        E.tensor_scalar(WSrow[0:1, 3:4], WSrow[0:1, 1:2],
                        1.0 / (HID * D), None, ALU.mult)
        return E.tensor_scalar(WSrow[0:1, 3:4], WSrow[0:1, 3:4], EPS_Q,
                               None, ALU.max)
    p.on("vector", ws2_fin, wait=[mm_w2abs], inc=("sDV", 1))
    ws2_done = p.cnt["sDV"]

    p.on("tensor", lambda E: E.matmul(PB[7][:, 0:1], ONESRf[:],
                                      WSrow[0:1, 2:3], start=True, stop=True),
         wait=[("sDV", ws2_done)])
    p.on("tensor", lambda E: E.matmul(PB[7][:, 1:2], ONESRf[:],
                                      WSrow[0:1, 3:4], start=True, stop=True),
         inc=("sMM", 1))
    mm_wsb = p.cnt["sMM"]

    def ws_vecs(E):
        E.tensor_copy(WS1v[:], PB[7][:, 0:1])
        E.tensor_copy(WS2v[:], PB[7][:, 1:2])
        E.reciprocal(WS1i[:], WS1v[:])
        E.reciprocal(WS2i[:], WS2v[:])
        E.tensor_scalar(WS1n[:], WS1i[:], -1.0, None, ALU.mult)
        return E.tensor_scalar(WS2n[:], WS2i[:], -1.0, None, ALU.mult)
    p.on("vector", ws_vecs, wait=[("sMM", mm_wsb)], inc=("sDV", 1))

    for k in range(KC):
        wv = [("sDV", modc_done)] if k == 0 else []
        for j in range(NNT):
            p.on("tensor", lambda E, k=k, j=j: E.matmul(
                PB[j][0:1, :], ONESCb[:], ntile(X, k, j),
                start=(k == 0), stop=(k == KC - 1)),
                wait=wv if j == 0 else [],
                inc=("sM3", 1) if (k == KC - 1 and j == NNT - 1) else None)
    mm_sum1 = p.cnt["sM3"]

    def ln1_mean(E):
        for j in range(NNT):
            E.tensor_scalar(ROW1[0:1, j * 512:(j + 1) * 512],
                            PB[j][0:1, :], 1.0 / D, None, ALU.mult)
        last = None
        for j in range(NNT):
            last = E.tensor_tensor(ROW2[0:1, j * 512:(j + 1) * 512],
                                   ROW1[0:1, j * 512:(j + 1) * 512],
                                   ROW1[0:1, j * 512:(j + 1) * 512],
                                   ALU.mult)
        return last
    p.on("vector", ln1_mean, wait=[("sM3", mm_sum1)], inc=("sDV", 1))

    sq1 = [X2[:, 2048:4096], X2[:, 4096:6144]]
    st1_mm = {}
    for k in range(KC):
        wv = [("sM3", st1_mm[k - 2])] if k >= 2 else []
        p.on("vector", lambda E, k=k: E.tensor_tensor(
            sq1[k % 2], chunk(X, k), chunk(X, k), ALU.mult),
            wait=wv, inc=("sDV", 1))
        sqdv = p.cnt["sDV"]
        for j in range(NNT):
            p.on("tensor", lambda E, k=k, j=j: E.matmul(
                PB[j][0:1, :], ONESCb[:],
                sq1[k % 2][:, j * 512:(j + 1) * 512],
                start=(k == 0), stop=(k == KC - 1)),
                wait=[("sDV", sqdv)] if j == 0 else [],
                inc=("sM3", 1) if j == NNT - 1 else None)
        st1_mm[k] = p.cnt["sM3"]
    mm_sq1 = p.cnt["sM3"]

    def ln1_var(E):
        for j in range(NNT):
            E.tensor_scalar(PB[j][0:1, :], PB[j][0:1, :], 1.0 / D,
                            None, ALU.mult)
        last = None
        for j in range(NNT):
            last = E.tensor_tensor(PB[j][0:1, :], PB[j][0:1, :],
                                   ROW2[0:1, j * 512:(j + 1) * 512],
                                   ALU.subtract)
        return last
    p.on("vector", ln1_var, wait=[("sM3", mm_sq1)], inc=("sDV", 1))

    def ln1_rstd(E):
        for j in range(NNT):
            E.activation(SC[2][0:1, j * 512:(j + 1) * 512], PB[j][0:1, :],
                         AF.Ln, bias=EPS_LN)
        last = None
        for j in range(NNT):
            last = E.activation(ROW2[0:1, j * 512:(j + 1) * 512],
                                SC[2][0:1, j * 512:(j + 1) * 512],
                                AF.Exp, scale=-0.5)
        return last
    p.on("scalar", ln1_rstd, wait=[("sDV", p.cnt["sDV"])], inc=("sAC", 1))
    rstd1_ac = p.cnt["sAC"]

    for j in range(NNT):
        p.on("tensor", lambda E, j=j: E.matmul(
            PB[j][:, :], ONESRb[:], ROW1[0:1, j * 512:(j + 1) * 512],
            start=True, stop=True),
            wait=[("sAC", rstd1_ac)] if j == 0 else [],
            inc=("sM3", 1) if j == NNT - 1 else None)
    bcw1 = p.cnt["sM3"]

    def bc1_evac(E):
        last = None
        for j in range(NNT):
            last = E.activation(MB[:, j * 512:(j + 1) * 512], PB[j][:, :],
                                AF.Identity)
        return last
    p.on("scalar", bc1_evac, wait=[("sM3", bcw1)], inc=("sAC", 1))
    bc1e_ac = p.cnt["sAC"]
    for j in range(NNT):
        p.on("tensor", lambda E, j=j: E.matmul(
            PB[j][:, :], ONESRb[:], ROW2[0:1, j * 512:(j + 1) * 512],
            start=True, stop=True),
            wait=[("sAC", bc1e_ac)] if j == 0 else [],
            inc=("sM3", 1) if j == NNT - 1 else None)
    bcw2 = p.cnt["sM3"]

    def bc2_evac(E):
        last = None
        for j in range(NNT):
            last = E.activation(RB[:, j * 512:(j + 1) * 512], PB[j][:, :],
                                AF.Identity)
        return last
    p.on("scalar", bc2_evac, wait=[("sM3", bcw2)], inc=("sAC", 1))
    p.on("vector", None, wait=[("sAC", p.cnt["sAC"])], inc=("sDV", 1))
    bc1_done = p.cnt["sDV"]

    for k in range(KC):
        wv = [("sDV", bc1_done)] if k == 0 else [("sAC", p.cnt["sAC"])]

        def ln_apply_v(E, k=k, src=X):
            E.tensor_tensor(M1[:], chunk(src, k), MB[:], ALU.subtract)
            return E.tensor_tensor(X2[:, 0:L], M1[:], RB[:], ALU.mult)
        p.on("vector", ln_apply_v, wait=wv, inc=("sDV", 1))
        p.on("scalar", lambda E, k=k: E.activation(
            chunk(Hh, k), X2[:, 0:L], AF.Identity,
            bias=slot(0)[:, k:k + 1], scale=CA1[:, k:k + 1]),
            wait=[("sDV", p.cnt["sDV"])], inc=("sAC", 1))

    sq_slab = [WSB[:, 0:2048], WSB[:, 2048:4096]]

    def ln_stats(src_tile):
        mm_chunk_vals = []
        for k in range(KC):
            wv = []
            if k >= 2:
                wv = [("sMM", mm_chunk_vals[k - 2])]
            p.on("scalar", lambda E, k=k, s=src_tile: E.activation(
                sq_slab[k % 2], chunk(s, k), AF.Square),
                wait=wv, inc=("sAC", 1))
            acn = p.cnt["sAC"]
            for j in range(NNT):
                p.on("tensor", lambda E, k=k, j=j, s=src_tile: E.matmul(
                    PB[j][0:1, :], ONESCb[:], ntile(s, k, j),
                    start=(k == 0), stop=(k == KC - 1)))
            for j in range(NNT):
                p.on("tensor", lambda E, k=k, j=j: E.matmul(
                    PB[4 + j][0:1, :], ONESCb[:],
                    sq_slab[k % 2][:, j * 512:(j + 1) * 512],
                    start=(k == 0), stop=(k == KC - 1)),
                    wait=[("sAC", acn)] if j == 0 else [],
                    inc=("sMM", 1) if j == NNT - 1 else None)
            mm_chunk_vals.append(p.cnt["sMM"])
        return p.cnt["sMM"]

    def ln_rows(E):
        for j in range(NNT):
            E.tensor_scalar(ROW1[0:1, j * 512:(j + 1) * 512],
                            PB[j][0:1, :], 1.0 / D, None, ALU.mult)
        for j in range(NNT):
            E.tensor_scalar(PB[4 + j][0:1, :], PB[4 + j][0:1, :],
                            1.0 / D, None, ALU.mult)
            E.tensor_tensor(ROW2[0:1, j * 512:(j + 1) * 512],
                            ROW1[0:1, j * 512:(j + 1) * 512],
                            ROW1[0:1, j * 512:(j + 1) * 512], ALU.mult)
        last = None
        for j in range(NNT):
            last = E.tensor_tensor(PB[4 + j][0:1, :], PB[4 + j][0:1, :],
                                   ROW2[0:1, j * 512:(j + 1) * 512],
                                   ALU.subtract)
        return last

    def ln_rstd(E):
        for j in range(NNT):
            E.activation(PB[j][0:1, :], PB[4 + j][0:1, :], AF.Ln, bias=EPS_LN)
        last = None
        for j in range(NNT):
            last = E.activation(ROW2[0:1, j * 512:(j + 1) * 512],
                                PB[j][0:1, :], AF.Exp, scale=-0.5)
        return last

    def ln_bcast():
        ac_r = p.cnt["sAC"]
        for j in range(NNT):
            p.on("tensor", lambda E, j=j: E.matmul(
                PB[j][:, :], ONESRb[:], ROW1[0:1, j * 512:(j + 1) * 512],
                start=True, stop=True),
                wait=[("sAC", ac_r)] if j == 0 else [])
            p.on("tensor", lambda E, j=j: E.matmul(
                PB[4 + j][:, :], ONESRb[:], ROW2[0:1, j * 512:(j + 1) * 512],
                start=True, stop=True),
                inc=("sMM", 1) if j == NNT - 1 else None)
        mm_bc = p.cnt["sMM"]

        def bc_evac(E):
            for j in range(NNT):
                E.activation(MB[:, j * 512:(j + 1) * 512], PB[j][:, :],
                             AF.Identity)
            last = None
            for j in range(NNT):
                last = E.activation(RB[:, j * 512:(j + 1) * 512],
                                    PB[4 + j][:, :], AF.Identity)
            return last
        p.on("scalar", bc_evac, wait=[("sMM", mm_bc)], inc=("sAC", 1))
        p.on("vector", None, wait=[("sAC", p.cnt["sAC"])], inc=("sDV", 1))
        return p.cnt["sDV"]

    if DEBUG_TAPS:
        p.on("gpsimd", lambda E: E.dma_start(out=tap_hln, in_=Hh[:]),
             wait=[("sAC", p.cnt["sAC"])], inc=("sTAP", 16), dma=True)
        p.on("gpsimd", lambda E, v=p.cnt["sTAP"]: E.wait_ge(sems["sTAP"], v))
        p.on("gpsimd", lambda E: E.dma_start(out=tap_mod, in_=MODC[:]),
             inc=("sTAP", 16), dma=True)
        p.on("gpsimd", lambda E, v=p.cnt["sTAP"]: E.wait_ge(sems["sTAP"], v))
    p.barrier()

    BH1p = [M1, B127]
    BH2p = [MB, BS1]
    RLp = [SC[3][:], BS2[:]]
    MNp = [SC[4][:], W1Q[:, 0:L]]
    evac_vals = {}

    chain_ends = {}

    def emit_mixer_mm_evac(e):
        par = e % 2
        for gi, wimg in enumerate((wfT, wiT, whT)):
            wv = []
            if gi == 0 and e >= 1:
                wv = [("sMM", p.cnt["sMM"])]
            p.on("sync", lambda E, gi=gi, w_=wimg, e=e: E.dma_start(
                out=WSTG[:, gi * 1024:(gi + 1) * 1024],
                in_=w_[:, e * 1024:(e + 1) * 1024]),
                wait=wv, inc=("sMW", 16))
        mw_now = p.cnt["sMW"]

        def slab_cast(E, e=e):
            E.tensor_copy(WSB[:, 0:1024], WSTG[:, 0:1024])
            E.tensor_copy(WSB[:, 1024:2048], WSTG[:, 1024:2048])
            return E.tensor_copy(WSB[:, 2048:3072], WSTG[:, 2048:3072])
        p.on("gpsimd", slab_cast, wait=[("sMW", mw_now)] +
             ([("sAC", p.cnt["sAC"])] if e == 0 else []), inc=("sPL", 1))
        cast_done = ("sPL", p.cnt["sPL"])

        for j in range(NNT):
            pp = 3 * (j % 2)
            wv = [cast_done]
            if j >= 2:
                wv += [("sDV", evac_vals[(e, j - 2, "dv")]),
                       ("sAC", evac_vals[(e, j - 2, "ac")])]
            elif e >= 1:
                wv += [("sDV", evac_vals[(e - 1, j + 2, "dv")]),
                       ("sAC", evac_vals[(e - 1, j + 2, "ac")])]
            for gi in range(3):
                for k in range(KC):
                    p.on("tensor", lambda E, e=e, j=j, gi=gi, k=k, pp=pp:
                         E.matmul(PB[pp + gi][:, :],
                                  WSB[:, gi * 1024 + k * 128:
                                      gi * 1024 + (k + 1) * 128],
                                  ntile(Hh, k, j),
                                  start=(k == 0), stop=(k == KC - 1)),
                         wait=wv if (gi == 0 and k == 0) else [],
                         inc=("sMM", 1) if k == KC - 1 else None)
            mm_gate = p.cnt["sMM"]

            def evac_act(E, e=e, j=j, pp=pp, par=par):
                E.activation(BH1p[par][:, j * 512:(j + 1) * 512],
                             PB[pp][:, :],
                             AF.Exp, scale=-1.0, bias=NBF[:, e:e + 1])
                return E.activation(BH2p[par][:, j * 512:(j + 1) * 512],
                                    PB[pp + 1][:, :],
                                    AF.Exp, scale=-1.0, bias=NBI[:, e:e + 1])
            ga_g = [("sDV", g_dvs[e - 2])] if (j == 0 and
                                               (e - 2) in g_dvs) else []
            p.on("scalar", evac_act,
                 wait=[("sMM", mm_gate)] + ga_g, inc=("sAC", 1))
            evac_vals[(e, j, "ac")] = p.cnt["sAC"]

            def evac_dve(E, e=e, j=j, pp=pp, par=par):
                E.tensor_scalar(RLp[par][:, j * 512:(j + 1) * 512],
                                PB[pp + 2][:, :], BHVt[:, e:e + 1], 0.0,
                                ALU.add, ALU.max)
                return E.tensor_scalar(MNp[par][:, j * 512:(j + 1) * 512],
                                       PB[pp + 2][:, :], BHVt[:, e:e + 1],
                                       0.0, ALU.add, ALU.min)
            p.on("vector", evac_dve,
                 wait=[("sMM", mm_gate)], inc=("sDV", 1))
            evac_vals[(e, j, "dv")] = p.cnt["sDV"]

    Ap = [SC[1][:], SC[0][:]]
    Bp = [SC[2][:], RB[:]]

    g_dvs = {}

    def emit_chain_pair(e0):
        es = (e0, e0 + 1)

        def arrs(e):
            par = e % 2
            return (Ap[par], Bp[par], BH1p[par][:], BH2p[par][:],
                    RLp[par], MNp[par])

        t_ac = {}
        t_dv = {}
        for e in es:
            A, B, BH1, BH2, RL, MN = arrs(e)
            w0 = [("sAC", evac_vals[(e, NNT - 1, "ac")]),
                  ("sDV", evac_vals[(e, NNT - 1, "dv")])]
            if (e - 2) in chain_ends:
                w0 += [("sDV", chain_ends[e - 2])]
            p.on("scalar", lambda E, A=A, BH2=BH2: E.activation(
                A, BH2, AF.Ln, bias=1.0), wait=w0, inc=("sAC", 1))
        for e in es:
            A, B, BH1, BH2, RL, MN = arrs(e)
            p.on("scalar", lambda E, B=B, BH1=BH1: E.activation(
                B, BH1, AF.Ln, bias=1.0), inc=("sAC", 1))
            t_ac[e] = p.cnt["sAC"]
        for e in es:
            A, B, BH1, BH2, RL, MN = arrs(e)
            p.on("vector", lambda E, A=A, B=B: E.tensor_tensor(
                A, B, A, ALU.subtract),
                wait=[("sAC", t_ac[e])], inc=("sDV", 1))
            t_dv[e] = p.cnt["sDV"]
        for e in es:
            A, B, BH1, BH2, RL, MN = arrs(e)
            p.on("scalar", lambda E, BH1=BH1, MN=MN: E.activation(
                BH1, MN, AF.Sigmoid), inc=("sAC", 1))
            t_ac[e] = p.cnt["sAC"]
        for e in es:
            A, B, BH1, BH2, RL, MN = arrs(e)
            p.on("scalar", lambda E, A=A, B=B: E.activation(
                B, A, AF.Sigmoid, scale=-1.0),
                wait=[("sDV", t_dv[e])], inc=("sAC", 1))
            t_dv[e] = p.cnt["sAC"]
        for e in es:
            A, B, BH1, BH2, RL, MN = arrs(e)
            p.on("vector", lambda E, BH1=BH1, RL=RL: E.tensor_tensor(
                RL, RL, BH1, ALU.add),
                wait=[("sAC", t_ac[e])], inc=("sDV", 1))
            g_dvs[e] = p.cnt["sDV"]
        for e in es:
            A, B, BH1, BH2, RL, MN = arrs(e)
            p.on("vector", lambda E, A=A, B=B: E.tensor_scalar(
                A, B, -1.0, 1.0, ALU.mult, ALU.add),
                wait=[("sAC", t_dv[e])], inc=("sDV", 1))
        for e in es:
            A, B, BH1, BH2, RL, MN = arrs(e)
            p.on("vector", lambda E, A=A, MN=MN, RL=RL: E.tensor_tensor(
                MN, RL, A, ALU.mult), inc=("sDV", 1))
        for e in es:
            A, B, BH1, BH2, RL, MN = arrs(e)

            def scan_epi(E, e=e, B=B, MN=MN, RL=RL):
                E.tensor_tensor_scan(RL, B, MN, 0.0, ALU.mult, ALU.add)
                return E.scalar_tensor_tensor(
                    chunk(X2, e), RL, GA[:, e:e + 1], chunk(X, e),
                    ALU.mult, ALU.add)
            p.on("vector", scan_epi, inc=("sDV", 1))
            chain_ends[e] = p.cnt["sDV"]

    for ep in range(KC // 2):
        if ep >= 1:
            emit_chain_pair(2 * ep - 2)
        emit_mixer_mm_evac(2 * ep)
        emit_mixer_mm_evac(2 * ep + 1)
    emit_chain_pair(KC - 2)
    p.barrier()

    mm_st2 = ln_stats(X2)
    p.on("vector", ln_rows, wait=[("sMM", mm_st2)], inc=("sDV", 1))
    p.on("scalar", ln_rstd, wait=[("sDV", p.cnt["sDV"])], inc=("sAC", 1))
    bc2_done = ln_bcast()

    for k in range(KC):
        wv = [("sDV", bc2_done)] if k == 0 else []

        def h2_v(E, k=k):
            E.tensor_tensor(QUU, chunk(X2, k), MB[:], ALU.subtract)
            return E.tensor_tensor(QAU, QUU, RB[:], ALU.mult)
        p.on("vector", h2_v, wait=wv, inc=("sDV", 1))
        p.on("scalar", lambda E, k=k: E.activation(
            chunk(Hh, k), QAU, AF.Identity,
            bias=slot(3)[:, k:k + 1], scale=CB1[:, k:k + 1]),
            wait=[("sDV", p.cnt["sDV"])], inc=("sAC", 1))
        acn = p.cnt["sAC"]
        M1n = XS(4096, 6144)
        if k == 0:
            p.on("vector", lambda E, M1n=M1n: (
                E.tensor_copy(M1[:], chunk(Hh, 0)),
                E.tensor_copy(M1n, chunk(Hh, 0)))[-1],
                wait=[("sAC", acn)], inc=("sDV", 1))
        else:
            p.on("vector", lambda E, k=k, M1n=M1n: (
                E.tensor_tensor(M1[:], M1[:], chunk(Hh, k), ALU.max),
                E.tensor_tensor(M1n, M1n, chunk(Hh, k), ALU.min))[-1],
                wait=[("sAC", acn)], inc=("sDV", 1))
    M1n = XS(4096, 6144)
    p.on("vector", lambda E, M1n=M1n: (
        E.tensor_scalar(M1n, M1n, -1.0, None, ALU.mult),
        E.tensor_tensor(M1[:], M1[:], M1n, ALU.max))[-1],
        inc=("sDV", 1))
    m1_done = p.cnt["sDV"]

    def make_scales(stm_tile, first_waits):
        p.on("vector", lambda E: E.tensor_copy(SC[0][:], M1[:]),
             wait=list(first_waits), inc=("sDV", 1))
        m1f_done = p.cnt["sDV"]
        for g4 in range(4):
            tgt = PB[6] if g4 % 2 == 0 else PB[7]
            wv = [("sDV", m1f_done)] if g4 == 0 else (
                [("sDV", p.cnt["sDV"])] if g4 >= 2 else [])
            for b4 in range(4):
                blk = g4 * 4 + b4
                p.on("tensor", lambda E, blk=blk, b4=b4, tgt=tgt: E.transpose(
                    tgt[:, b4 * 128:(b4 + 1) * 128],
                    SC[0][:, blk * 128:(blk + 1) * 128], IDNF[:]),
                    wait=wv if b4 == 0 else [],
                    inc=("sMM", 1) if b4 == 3 else None)
            mmn = p.cnt["sMM"]
            p.on("vector", lambda E, g4=g4, tgt=tgt, stm=stm_tile:
                 E.tensor_reduce(
                     stm[:, g4 * 4:(g4 + 1) * 4],
                     tgt[:, 0:512].rearrange("p (b c) -> p b c", b=4),
                     AX.X, ALU.max, apply_absolute_value=True),
                 wait=[("sMM", mmn)], inc=("sDV", 1))
        p.on("vector", lambda E, stm=stm_tile: E.tensor_scalar(
            stm[:], stm[:], EPS_Q, None, ALU.max), inc=("sDV", 1))
        return p.cnt["sDV"]

    stm_done = make_scales(STM, [("sDV", m1_done)])

    def xscale_vecs(E):
        E.reciprocal(STMH[:], STM[:])
        E.tensor_scalar(VB127[:], STMH[:], QB, None, ALU.mult)
        E.tensor_scalar(STMH[:], STM[:], 1.0 / QB, None, ALU.mult)
        return E.tensor_scalar(VBS1[:], STMH[:], WS1v[:, 0:1], None, ALU.mult)
    p.on("vector", xscale_vecs, wait=[("sDV", stm_done)], inc=("sDV", 1))
    xs_done = p.cnt["sDV"]

    def build_bcast(vec_tile, out_tile, after_dv):
        for g4 in range(4):
            pp = g4 % 2
            wv = [("sDV", after_dv)] if g4 == 0 else []
            if g4 >= 2:
                wv += [("sAC", p.cnt["sAC"])]
            for b4 in range(4):
                blk = g4 * 4 + b4
                p.on("vector", lambda E, blk=blk, b4=b4, v=vec_tile, pp=pp:
                     E.tensor_scalar(
                         WSB[:, (pp * 4 + b4) * 128:(pp * 4 + b4) * 128 + 128],
                         IDN[:], v[:, blk:blk + 1], None, ALU.mult),
                     wait=wv if b4 == 0 else [], inc=("sDV", 1))
                dvn = p.cnt["sDV"]
                p.on("tensor", lambda E, blk=blk, b4=b4, pp=pp: E.matmul(
                    PB[pp][:, b4 * 128:(b4 + 1) * 128], ONES128[:],
                    WSB[:, (pp * 4 + b4) * 128:(pp * 4 + b4) * 128 + 128],
                    start=True, stop=True),
                    wait=[("sDV", dvn)],
                    inc=("sMM", 1) if b4 == 3 else None)
            mmn = p.cnt["sMM"]
            p.on("scalar", lambda E, g4=g4, o=out_tile, pp=pp:
                 E.activation(o[:, g4 * 512:(g4 + 1) * 512],
                              PB[pp][:, 0:512], AF.Identity),
                 wait=[("sMM", mmn)], inc=("sAC", 1))
        p.on("vector", None, wait=[("sAC", p.cnt["sAC"])], inc=("sDV", 1))
        return p.cnt["sDV"]

    b127_done = build_bcast(VB127, B127, xs_done)
    bs1_done = build_bcast(VBS1, BS1, b127_done)

    QSET = [(QUU, QY, QRA, QT, QSG), (QAU, QR0, QRB, QNA, MB[:])]
    xact = {}

    def x_u(k):
        U, Y, RA, T, SG = QSET[k % 2]
        p.on("vector", lambda E, k=k, U=U: E.tensor_tensor(
            U, chunk(Hh, k), B127[:], ALU.mult),
            wait=([("sDV", bs1_done)] if k == 0 else []), inc=("sDV", 1))
        p.on("scalar", lambda E, U=U, Y=Y, SG=SG: (
            E.activation(Y, U, AF.Abs),
            E.activation(SG, U, AF.Sign))[-1],
            wait=[("sDV", p.cnt["sDV"])], inc=("sAC", 1))
        xact[k] = p.cnt["sAC"]

    x_u(0)
    for k in range(KC):
        if k + 1 < KC:
            x_u(k + 1)
        U, Y, RA, T, SG = QSET[k % 2]

        def xdve(E, k=k, Y=Y, RA=RA, T=T, SG=SG):
            E.tensor_scalar(Y, Y, 128.0, None, ALU.add)
            E.tensor_scalar(RA, Y, -127.5, None, ALU.add)
            for bit in (64.0, 32.0, 16.0, 8.0, 4.0, 2.0, 1.0):
                E.tensor_scalar(T, RA, bit, bit, ALU.is_ge, ALU.mult)
                E.tensor_tensor(RA, RA, T, ALU.subtract)
            E.scalar_tensor_tensor(T, RA, -1.0, Y, ALU.mult, ALU.add)
            return E.scalar_tensor_tensor(chunk(Hh, k), T, -127.5, SG,
                                          ALU.add, ALU.mult)
        p.on("vector", xdve, wait=[("sAC", xact[k])], inc=("sDV", 1))
    xq_done = p.cnt["sDV"]
    if DEBUG_TAPS:
        p.on("gpsimd", lambda E: E.dma_start(out=tap_xq, in_=Hh[:]),
             wait=[("sDV", p.cnt["sDV"])], inc=("sTAP", 16), dma=True)
        def tapstm(E):
            return E.dma_start(out=tap_stm[:, 0:16], in_=STM[:])
        p.on("gpsimd", tapstm, inc=("sTAP", 16), dma=True)
        p.on("gpsimd", lambda E, v=p.cnt["sTAP"]: E.wait_ge(sems["sTAP"], v))
        p.on("gpsimd", lambda E: E.dma_start(out=tap_bs[:, 0:L], in_=B127[:]),
             inc=("sTAP", 16), dma=True)
        p.on("gpsimd", lambda E, v=p.cnt["sTAP"]: E.wait_ge(sems["sTAP"], v))
        p.on("gpsimd", lambda E: E.dma_start(out=tap_bs[:, L:2 * L], in_=BS1[:]),
             inc=("sTAP", 16), dma=True)
        p.on("gpsimd", lambda E, v=p.cnt["sTAP"]: E.wait_ge(sems["sTAP"], v))

    w1stage = [WSTG[:, i * 1024:(i + 1) * 1024] for i in range(4)]
    w1q = [W1Q[:, 0:1024], W1Q[:, 1024:2048], RB[:, 0:1024], RB[:, 1024:2048]]
    tern_st = [(SC[1][:, 0:1024], SC[1][:, 1024:2048]),
               (SC[3][:, 0:1024], SC[3][:, 1024:2048])]
    tern_dvs = {}
    tern_pl = {}
    hc_mm = {}
    gelu_acs = []
    ffn1_evacs = []
    spill_sps = []

    def emit_tern(hc):
        sp4 = hc % 4
        wv = []
        if hc >= 4:
            wv = [tern_dvs[hc - 4]]
        sw1 = ["sW1a", "sW1b", "sW1c", "sW1d"][sp4]
        p.on("sync", lambda E, hc=hc, sp4=sp4: E.dma_start(
            out=w1stage[sp4], in_=w1T[hc]), wait=wv, inc=(sw1, 16))
        w1n = p.cnt[sw1]
        tp_, tn_ = tern_st[hc % 2]

        def tern_ts(E, sp4=sp4, tp_=tp_, tn_=tn_):
            E.tensor_scalar(tp_, w1stage[sp4], WS1i[:, 0:1], 0.5,
                            ALU.mult, ALU.is_ge)
            return E.tensor_scalar(tn_, w1stage[sp4], WS1n[:, 0:1], 0.5,
                                   ALU.mult, ALU.is_ge)
        tw = [(sw1, w1n)]
        if hc >= 2:
            tw += [("sPL", tern_pl[hc - 2])]
        p.on("vector", tern_ts, wait=tw, inc=("sDV", 1))
        subw = [("sDV", p.cnt["sDV"])]
        if hc >= 4:
            subw += [("sMM", hc_mm[hc - 4])]
        elif hc >= 2:
            subw += [("sDV", xq_done)]
        p.on("gpsimd", lambda E, hc=hc, tp_=tp_, tn_=tn_: E.tensor_tensor(
            w1q[hc % 4], tp_, tn_, ALU.subtract),
            wait=subw, inc=("sPL", 1))
        tern_pl[hc] = p.cnt["sPL"]
        tern_dvs[hc] = ("sPL", p.cnt["sPL"])

    emit_tern(0)
    emit_tern(1)
    for hc in range(HC):
        if hc + 2 < HC:
            emit_tern(hc + 2)
        for j in range(NNT):
            jp = (hc * NNT + j) % 4
            gidx = hc * NNT + j
            for k in range(KC):
                wv = []
                if k == 0:
                    wv = [tern_dvs[hc]]
                    if gidx == 0:
                        wv += [("sDV", xq_done)]
                    if gidx >= 4:
                        wv += [("sDV", ffn1_evacs[gidx - 4])]
                p.on("tensor", lambda E, hc=hc, j=j, k=k, jp=jp:
                     E.matmul(PB[jp][:, :],
                              w1q[hc % 4][:, k * 128:(k + 1) * 128],
                              ntile(Hh, k, j),
                              start=(k == 0), stop=(k == KC - 1)),
                     wait=wv, inc=("sMM", 1) if k == KC - 1 else None)
            mmn = p.cnt["sMM"]
            hc_mm[hc] = mmn
            p.on("vector", lambda E, j=j, jp=jp: E.tensor_tensor(
                SC[2][:, j * 512:(j + 1) * 512], PB[jp][:, :],
                BS1[:, j * 512:(j + 1) * 512], ALU.mult),
                wait=[("sMM", mmn)], inc=("sDV", 1))
            dvn = p.cnt["sDV"]
            ffn1_evacs.append(dvn)
            p.on("scalar", lambda E, hc=hc, j=j: E.activation(
                H1Tp[hc % 2][:, j * 512:(j + 1) * 512],
                SC[2][:, j * 512:(j + 1) * 512], AF.Gelu,
                bias=B1Vt[:, hc:hc + 1]),
                wait=[("sDV", dvn)], inc=("sAC", 1))
        gelu_acs.append(p.cnt["sAC"])
        H1T = H1Tp[hc % 2]
        if hc == 0:
            p.on("vector", lambda E, H1T=H1T: (
                E.tensor_copy(M1[:], H1T),
                E.tensor_copy(MB[:], H1T))[-1],
                 wait=[("sAC", gelu_acs[-1])], inc=("sDV", 1))
        else:
            p.on("vector", lambda E, H1T=H1T: (
                E.tensor_tensor(M1[:], M1[:], H1T, ALU.max),
                E.tensor_tensor(MB[:], MB[:], H1T, ALU.min))[-1],
                 wait=[("sAC", gelu_acs[-1])], inc=("sDV", 1))
        ssp = "sSPa" if hc % 2 == 0 else "sSPb"
        p.on("gpsimd", lambda E, hc=hc, H1T=H1T: E.dma_start(
            out=h1sp[hc], in_=H1T),
             wait=[("sAC", gelu_acs[-1])], inc=(ssp, 16), dma=True)
        spill_sps.append((ssp, p.cnt[ssp]))
        if DEBUG_TAPS and hc == 0:
            p.on("gpsimd", lambda E: E.dma_start(out=tap_h1, in_=H1Tp[0]),
                 inc=("sTAP", 16), dma=True)
            p.on("gpsimd", lambda E, v=p.cnt["sTAP"]: E.wait_ge(sems["sTAP"], v))
        if hc >= 1:
            sn, v = spill_sps[hc - 1]
            p.on("scalar", lambda E, sn=sn, v=v: E.wait_ge(sems[sn], v))
        if DEBUG_TAPS and hc == 0:
            p.on("scalar", lambda E, v=p.cnt["sTAP"]: E.wait_ge(sems["sTAP"], v))
    p.on("vector", lambda E: (
        E.tensor_scalar(MB[:], MB[:], -1.0, None, ALU.mult),
        E.tensor_tensor(M1[:], M1[:], MB[:], ALU.max))[-1],
        inc=("sDV", 1))
    pl_m1h = ("sDV", p.cnt["sDV"])

    w2stg4 = [SC[1][:], SC[2][:], SC[3][:], WSTG[:, 0:2048]]
    w2sem4 = ["sW2", "sW2b", "sW2c", "sW2d"]
    w2u_dv = {}
    w2spl = {}
    for u in range(2 * KC):
        s, h = u // 2, u % 2
        st = u % 4
        if u < 4:
            wv = [("sSPa", p.cnt["sSPa"]), ("sSPb", p.cnt["sSPb"])]
        else:
            wv = [("sDV", w2u_dv[u - 4])]
        sw = w2sem4[st]
        p.on("sync", lambda E, s=s, h=h, st=st: E.dma_start(
            out=w2stg4[st], in_=w2T[s][:, h * 2048:(h + 1) * 2048]),
            wait=wv, inc=(sw, 16))
        w2n = p.cnt[sw]

        ioff = 4096 + (s % 2) * 4096

        def tern2(E, st=st, h=h, ioff=ioff):
            E.tensor_scalar(W1Q[:], w2stg4[st], WS2i[:, 0:1], 0.5,
                            ALU.mult, ALU.is_ge)
            E.tensor_scalar(RB[:], w2stg4[st], WS2n[:, 0:1], 0.5,
                            ALU.mult, ALU.is_ge)
            return E.tensor_tensor(
                X[:, ioff + h * 2048:ioff + (h + 1) * 2048],
                W1Q[:], RB[:], ALU.subtract)
        tv = [(sw, w2n)]
        if h == 0 and s >= 2:
            tv += [w2spl[s - 2]]
        p.on("vector", tern2, wait=tv, inc=("sDV", 1))
        w2u_dv[u] = p.cnt["sDV"]
        if h == 1:
            ssp_s = "sSP" if s % 2 == 0 else "sSPa"
            p.on("gpsimd", lambda E, s=s, ioff=ioff: E.dma_start(
                out=w2qsp[:, :, s * 128:(s + 1) * 128].rearrange(
                    "k p c -> p k c"),
                in_=X[:, ioff:ioff + 4096].rearrange("p (k c) -> p k c",
                                                     k=HC)),
                wait=[("sDV", w2u_dv[u])], inc=(ssp_s, 16), dma=True)
            w2spl[s] = (ssp_s, p.cnt[ssp_s])
    sp_all = p.cnt["sSP"]

    stmh_done = make_scales(STMH, [pl_m1h])

    def hscale_vecs(E):
        E.reciprocal(STM[:], STMH[:])
        E.tensor_scalar(VB127[:], STM[:], QB, None, ALU.mult)
        E.tensor_scalar(STM[:], STMH[:], 1.0 / QB, None, ALU.mult)
        return E.tensor_scalar(VBS2[:], STM[:], WS2v[:, 0:1], None, ALU.mult)
    p.on("vector", hscale_vecs, wait=[("sDV", stmh_done)], inc=("sDV", 1))
    hs_done = p.cnt["sDV"]
    b127h_done = build_bcast(VB127, B127, hs_done)
    bs2_done = build_bcast(VBS2, BS2, b127h_done)
    if DEBUG_TAPS:
        p.on("gpsimd", lambda E: E.dma_start(out=tap_bs[:, 2 * L:3 * L],
                                             in_=BS2[:]),
             wait=[("sDV", bs2_done)], inc=("sTAP", 16), dma=True)
        p.on("gpsimd", lambda E, v=p.cnt["sTAP"]: E.wait_ge(sems["sTAP"], v))
        p.on("gpsimd", lambda E: E.dma_start(out=tap_stm[:, 16:32],
                                             in_=STMH[:]),
             inc=("sTAP", 16), dma=True)
        p.on("gpsimd", lambda E, v=p.cnt["sTAP"]: E.wait_ge(sems["sTAP"], v))
    p.barrier()

    HHS = lambda a, b: Hh[:, a:b]
    FIN = [HHS(0, 4096), XS(0, 4096)]
    FUU = [HHS(4096, 8192), XS(4096, 8192)]
    FRHO = [HHS(8192, 12288), XS(8192, 12288)]
    FSG = [HHS(12288, 16384), XS(12288, 16384)]
    H1Q2 = [XS(i * 2048, (i + 1) * 2048) for i in range(8)]
    W2R = [M1[:, 0:1024], M1[:, 1024:2048], RB[:, 0:1024], RB[:, 1024:2048],
           MB[:, 0:1024], MB[:, 1024:2048], W1Q[:, 0:1024],
           W1Q[:, 1024:2048]]

    def lad_dve2(E, pf):
        E.tensor_scalar(FIN[pf], FIN[pf], 128.0, None, ALU.add)
        E.tensor_scalar(FRHO[pf], FIN[pf], -127.5, None, ALU.add)
        for bit in (64.0, 32.0, 16.0, 8.0, 4.0, 2.0, 1.0):
            E.tensor_scalar(FUU[pf], FRHO[pf], bit, bit,
                            ALU.is_ge, ALU.mult)
            E.tensor_tensor(FRHO[pf], FRHO[pf], FUU[pf], ALU.subtract)
        E.scalar_tensor_tensor(FUU[pf], FRHO[pf], -1.0, FIN[pf],
                               ALU.mult, ALU.add)
        return E.scalar_tensor_tensor(FIN[pf], FUU[pf], -127.5,
                                      FSG[pf], ALU.add, ALU.mult)

    prev_evac_dv = 0
    bank_free_dv = 0
    last_mm = 0
    mm_vals = {}
    lad_vals = {}
    spl_vals = {}
    LBITS = (32.0, 16.0, 8.0, 4.0, 2.0, 1.0)

    for gi, (d0, d1) in enumerate([(0, 1), (2, 3), (4, 5), (6, 7)]):
        if gi == 0:
            uacts = {}
            udma = {}
            NPAIR = HC // 2

            def emit_uact(pr):
                pf = pr % 2
                srd = "sRDa" if pf == 0 else "sRDb"
                if pr <= 1:
                    wv = [("sSP", sp_all),
                          ("sSPa", p.cnt["sSPa"]), ("sSPb", p.cnt["sSPb"])]
                    if pr == 0:
                        wv += [("sDV", bs2_done)]
                else:
                    wv = [("sMM", mm_vals[(0, pr - 2)]),
                          ("sDV", lad_vals[(0, pr - 2)]),
                          (("sH1a" if pf == 0 else "sH1b"),
                           spl_vals[(0, pr - 2)])]
                p.on("sync", lambda E, pr=pr, pf=pf: E.dma_start(
                    out=FIN[pf].rearrange("p (k c) -> p k c", k=2),
                    in_=h1sp[2 * pr:2 * pr + 2].rearrange("k p c -> p k c")),
                    wait=wv, inc=(srd, 16))
                p.on("sync", lambda E, pr=pr, pf=pf: E.dma_start(
                    out=W2R[2 * pf], in_=w2qsp[2 * pr]), inc=(srd, 16))
                p.on("sync", lambda E, pr=pr, pf=pf: E.dma_start(
                    out=W2R[2 * pf + 1], in_=w2qsp[2 * pr + 1]),
                    inc=(srd, 16))
                udma[pr] = (srd, p.cnt[srd])

            def emit_ulad(pr):
                pf = pr % 2
                p.on("vector", lambda E, pf=pf: (
                    E.tensor_tensor(FUU[pf][:, 0:2048], FIN[pf][:, 0:2048],
                                    B127[:], ALU.mult),
                    E.tensor_tensor(FUU[pf][:, 2048:4096],
                                    FIN[pf][:, 2048:4096],
                                    B127[:], ALU.mult))[-1],
                     wait=[udma[pr]], inc=("sDV", 1))
                p.on("scalar", lambda E, pf=pf: (
                    E.activation(FIN[pf], FUU[pf], AF.Abs),
                    E.activation(FSG[pf], FUU[pf], AF.Sign))[-1],
                     wait=[("sDV", p.cnt["sDV"])], inc=("sAC", 1))
                uacts[pr] = p.cnt["sAC"]

            emit_uact(0)
            emit_ulad(0)
            for pr in range(NPAIR):
                pf = pr % 2
                if pr + 1 < NPAIR:
                    emit_uact(pr + 1)
                p.on("vector", lambda E, pf=pf: (
                    E.tensor_scalar(FIN[pf], FIN[pf], 128.0, None, ALU.add),
                    E.tensor_scalar(FRHO[pf], FIN[pf], -127.5, None,
                                    ALU.add),
                    E.tensor_scalar(FUU[pf], FRHO[pf], 64.0, 64.0,
                                    ALU.is_ge, ALU.mult),
                    E.tensor_tensor(FRHO[pf], FRHO[pf], FUU[pf],
                                    ALU.subtract),
                    E.tensor_scalar(FUU[pf], FRHO[pf], 32.0, 32.0,
                                    ALU.is_ge, ALU.mult),
                    E.tensor_tensor(FRHO[pf], FRHO[pf], FUU[pf],
                                    ALU.subtract),
                    E.tensor_scalar(FUU[pf], FRHO[pf], 16.0, 16.0,
                                    ALU.is_ge, ALU.mult),
                    E.tensor_tensor(FRHO[pf], FRHO[pf], FUU[pf],
                                    ALU.subtract))[-1],
                     wait=[("sAC", uacts[pr])], inc=("sDV", 1))
                if pr + 1 < NPAIR:
                    emit_ulad(pr + 1)

                def dve_tail(E, pf=pf):
                    for bit in (8.0, 4.0, 2.0, 1.0):
                        E.tensor_scalar(FUU[pf], FRHO[pf], bit, bit,
                                        ALU.is_ge, ALU.mult)
                        E.tensor_tensor(FRHO[pf], FRHO[pf], FUU[pf],
                                        ALU.subtract)
                    E.scalar_tensor_tensor(FUU[pf], FRHO[pf], -1.0, FIN[pf],
                                           ALU.mult, ALU.add)
                    return E.scalar_tensor_tensor(FIN[pf], FUU[pf], -127.5,
                                                  FSG[pf], ALU.add, ALU.mult)
                p.on("vector", dve_tail, inc=("sDV", 1))
                lad_vals[(0, pr)] = p.cnt["sDV"]
                sh1 = "sH1a" if pf == 0 else "sH1b"
                p.on("gpsimd", lambda E, pr=pr, pf=pf: E.dma_start(
                    out=h1qsp[2 * pr:2 * pr + 2].rearrange("k p c -> p k c"),
                    in_=FIN[pf].rearrange("p (k c) -> p k c", k=2)),
                    wait=[("sDV", lad_vals[(0, pr)])], inc=(sh1, 16),
                    dma=True)
                spl_vals[(0, pr)] = p.cnt[sh1]
                mmw = [("sDV", lad_vals[(0, pr)])]
                for kch in range(2):
                    for nt in range(NNT):
                        for di, d in enumerate((d0, d1)):
                            p.on("tensor",
                                 lambda E, pf=pf, kch=kch, nt=nt, di=di, d=d,
                                 pr=pr:
                                 E.matmul(
                                     PB[4 * di + nt][:, :],
                                     W2R[2 * pf + kch][:,
                                                       d * 128:(d + 1) * 128],
                                     FIN[pf][:, kch * 2048 + nt * 512:
                                             kch * 2048 + (nt + 1) * 512],
                                     start=(pr == 0 and kch == 0),
                                     stop=(pr == NPAIR - 1 and kch == 1)),
                                 wait=mmw
                                 if (kch == 0 and nt == 0 and di == 0)
                                 else [],
                                 inc=("sMM", 1)
                                 if (kch == 1 and nt == NNT - 1 and di == 1)
                                 else None)
                mm_vals[(0, pr)] = p.cnt["sMM"]
        else:
            for kc in range(HC):
                pp = kc % 8
                srd = ["sRDa", "sRDb", "sW1a", "sW1b",
                       "sW2", "sW2b", "sW2c", "sW2d"][pp]
                wv = []
                if kc < 8:
                    wv = [("sMM", last_mm)]
                    if gi == 1:
                        wv += [("sH1a", p.cnt["sH1a"]),
                               ("sH1b", p.cnt["sH1b"])]
                elif (gi, kc - 8) in mm_vals:
                    wv = [("sMM", mm_vals[(gi, kc - 8)])]
                p.on("sync", lambda E, kc=kc, pp=pp: E.dma_start(
                    out=H1Q2[pp], in_=h1qsp[kc]), wait=wv, inc=(srd, 16))
                p.on("scalar", lambda E, kc=kc, pp=pp: E.dma_start(
                    out=W2R[pp], in_=w2qsp[kc]), wait=wv, inc=(srd, 16),
                    dma=True)
                rdn = p.cnt[srd]
                mmw = [(srd, rdn)]
                if kc == 0:
                    mmw += [("sDV", bank_free_dv)]
                for nt in range(NNT):
                    for di, d in enumerate((d0, d1)):
                        p.on("tensor",
                             lambda E, kc=kc, pp=pp, nt=nt, di=di, d=d:
                             E.matmul(PB[4 * di + nt][:, :],
                                      W2R[pp][:, d * 128:(d + 1) * 128],
                                      H1Q2[pp][:, nt * 512:(nt + 1) * 512],
                                      start=(kc == 0), stop=(kc == HC - 1)),
                             wait=mmw if (nt == 0 and di == 0) else [],
                             inc=("sMM", 1) if (nt == NNT - 1 and di == 1)
                             else None)
                mm_vals[(gi, kc)] = p.cnt["sMM"]
        mm_g = p.cnt["sMM"]
        last_mm = mm_g
        bank_free_trk = None
        for ei in range(8):
            di, nt = ei // 4, ei % 4
            d = (d0, d1)[di]
            pp = ei % 2
            sfin = "sFINa" if pp == 0 else "sFINb"
            wv = [("sMM", mm_g)] if ei == 0 else []
            if ei >= 2 or gi >= 1:
                wv += [(sfin, p.cnt[sfin])]
            p.on("vector", lambda E, di=di, nt=nt, pp=pp: E.tensor_tensor(
                OUTP[pp], PB[4 * di + nt][:, :],
                BS2[:, nt * 512:(nt + 1) * 512], ALU.mult),
                wait=wv, inc=("sDV", 1))
            dvn = p.cnt["sDV"]
            if ei == 7:
                bank_free_trk = dvn
            p.on("scalar", lambda E, d=d, pp=pp: E.activation(
                OUTP[pp], OUTP[pp], AF.Identity,
                bias=B2G[:, d:d + 1], scale=GB[:, d:d + 1]),
                wait=[("sDV", dvn)], inc=("sAC", 1))
            acn = p.cnt["sAC"]
            p.on("vector", lambda E, d=d, nt=nt, pp=pp: E.tensor_tensor(
                OUTP[pp], OUTP[pp],
                X2[:, d * L + nt * 512:d * L + (nt + 1) * 512], ALU.add),
                wait=[("sAC", acn)], inc=("sDV", 1))
            dvn2 = p.cnt["sDV"]
            p.on("gpsimd", lambda E, d=d, nt=nt, pp=pp: E.dma_start(
                out=out_img[d][:, nt * 512:(nt + 1) * 512], in_=OUTP[pp]),
                wait=[("sDV", dvn2)], inc=(sfin, 16), dma=True)
        prev_evac_dv = p.cnt["sDV"]
        bank_free_dv = bank_free_trk

    fa, fb = p.cnt["sFINa"], p.cnt["sFINb"]
    p.on("sync", lambda E: E.wait_ge(sems["sFINa"], fa))
    p.on("sync", lambda E: E.wait_ge(sems["sFINb"], fb))

    with nc.Block() as block:
        @block.sync
        def _(E):
            for f in p.ops["sync"]:
                f(E)

        @block.vector
        def _(E):
            E2 = Ser(E, sems["sVq"])
            for f in p.ops["vector"]:
                f(E2)

        @block.scalar
        def _(E):
            E2 = Ser(E, sems["sSq"])
            for f in p.ops["scalar"]:
                f(E2)

        @block.tensor
        def _(E):
            for f in p.ops["tensor"]:
                f(E)

        @block.gpsimd
        def _(E):
            E2 = Ser(E, sems["sPq"])
            for f in p.ops["gpsimd"]:
                f(E2)

    return nc


_NC_CACHE = None


def kernel(L0_attn_x, L0_attn_W_f, L0_attn_b_f, L0_attn_W_i, L0_attn_b_i,
           L0_attn_W_h, L0_attn_b_h, L0_ffn_W1, L0_ffn_b1, L0_ffn_W2,
           L0_ffn_b2, L0_c, L0_wmod):
    global _NC_CACHE
    import ml_dtypes
    x = np.asarray(L0_attn_x, dtype=np.float32)
    W_f = np.asarray(L0_attn_W_f, dtype=np.float32)
    W_i = np.asarray(L0_attn_W_i, dtype=np.float32)
    W_h = np.asarray(L0_attn_W_h, dtype=np.float32)
    b_f = np.asarray(L0_attn_b_f, dtype=np.float32)
    b_i = np.asarray(L0_attn_b_i, dtype=np.float32)
    b_h = np.asarray(L0_attn_b_h, dtype=np.float32)
    W1 = np.asarray(L0_ffn_W1, dtype=np.float32)
    b1 = np.asarray(L0_ffn_b1, dtype=np.float32)
    W2 = np.asarray(L0_ffn_W2, dtype=np.float32)
    b2 = np.asarray(L0_ffn_b2, dtype=np.float32)
    cmat = np.asarray(L0_c, dtype=np.float32)
    wmod = np.asarray(L0_wmod, dtype=np.float32)

    def lay_lhsT(W, mout, kin):
        return np.ascontiguousarray(
            W.reshape(mout, 128, kin, 128).transpose(3, 0, 2, 1)
            .reshape(128, mout * kin * 128))

    shared = {
        "wfT": lay_lhsT(W_f, KC, KC),
        "wiT": lay_lhsT(W_i, KC, KC),
        "whT": lay_lhsT(W_h, KC, KC),
        "w1T": np.ascontiguousarray(
            W1.reshape(HC, 128, KC, 128).transpose(0, 3, 2, 1)
            .reshape(HC, 128, KC * 128)),
        "w2T": np.ascontiguousarray(
            W2.reshape(KC, 128, HC, 128).transpose(0, 3, 2, 1)
            .reshape(KC, 128, HC * 128)),
        "wmodT": np.ascontiguousarray(
            wmod.reshape(KC, 128, 12, 512).transpose(2, 1, 0, 3)
            .reshape(12, 128, KC * 512)),
        "idnf": np.eye(128, dtype=np.float32),
        "cpack_b": np.concatenate([
            np.ones((128, 128), ml_dtypes.bfloat16),
            np.eye(128, dtype=ml_dtypes.bfloat16)], axis=1),

    }
    constv = np.broadcast_to(np.array(
        [1e-5, 1.0, 0.5, -1.0, -0.5, 2.0, 128.0, -127.5, -64.0,
         0.0, 0.0, 0.0], np.float32), (128, 12))
    base_pack = np.concatenate([
        np.zeros((128, 8), np.float32),
        np.ascontiguousarray(b_f.reshape(KC, 128).T),
        np.ascontiguousarray(b_i.reshape(KC, 128).T),
        np.ascontiguousarray(b_h.reshape(KC, 128).T),
        np.ascontiguousarray(b1.reshape(HC, 128).T),
        np.ascontiguousarray(b2.reshape(KC, 128).T),
        constv,
        np.ones((128, 128), np.float32)], axis=1)
    in_maps = []
    for b in range(B):
        m = dict(shared)
        m["xT"] = np.ascontiguousarray(x[b].T).reshape(KC, 128, L)
        pk = base_pack.copy()
        pk[:, 0:8] = cmat[b].reshape(KC, 128).T
        m["cpack_f"] = pk
        in_maps.append(m)

    if _NC_CACHE is None:
        _NC_CACHE = build_nc()
    res = run_bass_kernel_spmd(_NC_CACHE, in_maps, core_ids=list(range(B)))
    outs = []
    for b in range(B):
        oi = np.asarray(res.results[b]["out_img"], dtype=np.float32)
        outs.append(np.ascontiguousarray(oi.reshape(D, L).T))
    return np.stack(outs, axis=0)


# version 34
# speedup vs baseline: 1.0592x; 1.0121x over previous
import numpy as np
import concourse.bass as bass
import concourse.mybir as mybir
from concourse.bass_utils import run_bass_kernel_spmd

F32 = mybir.dt.float32
BF16 = mybir.dt.bfloat16
U8 = mybir.dt.uint8
AF = mybir.ActivationFunctionType
ALU = mybir.AluOpType
AX = mybir.AxisListType

DEBUG_TAPS = False
B, L, D, HID = 8, 2048, 1024, 4096
KC = D // 128
HC = HID // 128
NT = 512
NNT = L // NT
EPS_LN = 1e-5
EPS_Q = 1e-5
QB = 127.0

ENGINES = ("sync", "vector", "scalar", "tensor", "gpsimd")


class Prog:

    def __init__(self):
        self.ops = {e: [] for e in ENGINES}
        self.cnt = {}
        self.sems = None

    def sem(self, name):
        self.cnt[name] = 0

    def inc(self, name, n=1):
        self.cnt[name] += n
        return self.cnt[name]

    def on(self, eng, fn, wait=(), inc=None, dma=False):
        sems = self.sems
        serialized = (eng in ("vector", "scalar", "gpsimd")) and not dma
        for (s, v) in wait:
            if v > 0:
                self.ops[eng].append(lambda E, s=s, v=v: E.wait_ge(sems[s], v))
        if inc is not None:
            sname, amt = inc
            val = self.inc(sname, amt)
            if fn is None:
                self.ops[eng].append(
                    lambda E, s=sname, a=amt: E.sem_inc(sems[s], a))
            elif serialized:
                def run(E, f=fn, s=sname, a=amt):
                    f(E)
                    E.ser_flush_wait()
                    E.sem_inc(sems[s], a)
                self.ops[eng].append(run)
            else:
                self.ops[eng].append(
                    lambda E, f=fn, s=sname, a=amt: f(E).then_inc(sems[s], a))
            return val
        assert fn is not None
        self.ops[eng].append(fn)
        return None

    def barrier(self):
        for e in ENGINES:
            self.inc("sBR", 1)
        target = self.cnt["sBR"]
        sems = self.sems
        for e in ENGINES:
            self.ops[e].append(lambda E: E.sem_inc(sems["sBR"], 1))
            self.ops[e].append(lambda E: E.wait_ge(sems["sBR"], target))


SER_METHS = {"tensor_copy", "tensor_tensor", "tensor_scalar",
              "tensor_single_scalar", "scalar_tensor_tensor",
              "tensor_tensor_scan", "tensor_reduce", "select", "reciprocal",
              "memset", "memzero", "activation", "sign", "square", "sqrt",
              "mul", "add", "copy", "tensor_add", "tensor_sub", "tensor_mul",
              "tensor_max", "tensor_relu", "tensor_scalar_add",
              "tensor_scalar_mul", "tensor_scalar_max", "tensor_scalar_min"}


class Ser:

    def __init__(self, eng, sem):
        self._e, self._s, self._n = eng, sem, 0

    def ser_flush_wait(self):
        if self._n:
            self._e.wait_ge(self._s, self._n)

    def __getattr__(self, m):
        f = getattr(self._e, m)
        if m in SER_METHS:
            def w(*a, **k):
                if self._n:
                    self._e.wait_ge(self._s, self._n)
                ins = f(*a, **k)
                self._n += 1
                return ins.then_inc(self._s, 1)
            return w
        return f


def build_nc():
    nc = bass.Bass("TRN2", target_bir_lowering=False, debug=False)
    p = Prog()

    def din(name, shape, dt=F32):
        return nc.dram_tensor(name, shape, dt, kind="ExternalInput").ap()

    xT = din("xT", [KC, 128, L])
    cpack_f = din("cpack_f", [128, 84 + 128])
    cpack_b = din("cpack_b", [128, 256], BF16)
    wfT = din("wfT", [128, KC * D])
    wiT = din("wiT", [128, KC * D])
    whT = din("whT", [128, KC * D])
    w1T = din("w1T", [HC, 128, KC * 128])
    w2T = din("w2T", [KC, 128, HC * 128])
    wmodT = din("wmodT", [12, 128, 4096])
    idnf = din("idnf", [128, 128])

    out_img = nc.dram_tensor("out_img", [KC, 128, L], F32,
                             kind="ExternalOutput").ap()
    if DEBUG_TAPS:
        tap_hln = nc.dram_tensor("tap_hln", [128, KC * L], BF16,
                                 kind="ExternalOutput").ap()
        tap_x2 = nc.dram_tensor("tap_x2", [128, KC * L], BF16,
                                kind="ExternalOutput").ap()
        tap_xq = nc.dram_tensor("tap_xq", [128, KC * L], BF16,
                                kind="ExternalOutput").ap()
        tap_stm = nc.dram_tensor("tap_stm", [128, 64], F32,
                                 kind="ExternalOutput").ap()
        tap_mod = nc.dram_tensor("tap_mod", [128, 48], F32,
                                 kind="ExternalOutput").ap()
        tap_bs = nc.dram_tensor("tap_bs", [128, 3 * L], BF16,
                                kind="ExternalOutput").ap()
        tap_h1 = nc.dram_tensor("tap_h1", [128, L], BF16,
                                kind="ExternalOutput").ap()
        tap_f = nc.dram_tensor("tap_f", [128, L], BF16,
                               kind="ExternalOutput").ap()
        tap_v = nc.dram_tensor("tap_v", [128, L], BF16,
                               kind="ExternalOutput").ap()
        tap_hm = nc.dram_tensor("tap_hm", [128, L], F32,
                                kind="ExternalOutput").ap()
    h1sp = nc.dram_tensor("h1sp", [HC, 128, L], BF16, kind="Internal").ap()
    h1qsp = nc.dram_tensor("h1qsp", [HC, 128, L], BF16, kind="Internal").ap()
    w2qsp = nc.dram_tensor("w2qsp", [HC, 128, KC * 128], BF16,
                           kind="Internal").ap()

    def sb(name, shape, dt):
        return nc.alloc_sbuf_tensor(name, shape, dt).ap()

    X = sb("X", [128, KC * L], BF16)
    Hh = sb("Hh", [128, KC * L], BF16)
    X2 = sb("X2", [128, KC * L], BF16)
    SC = [sb(f"SC{i}", [128, L], F32) for i in range(5)]
    MKm = sb("MKm", [128, L], U8)
    MB = sb("MB", [128, L], BF16)
    RB = sb("RB", [128, L], BF16)
    M1 = sb("M1", [128, L], BF16)
    WSTG = sb("WSTG", [128, 4096], F32)
    WSB = sb("WSB", [128, 4096], BF16)
    W1Q = sb("W1Q", [128, 2048], BF16)
    B127 = sb("B127", [128, L], BF16)
    BS1 = sb("BS1", [128, L], BF16)
    BS2 = sb("BS2", [128, L], BF16)
    ROW1 = sb("ROW1", [1, L], BF16)
    ROW2 = sb("ROW2", [1, L], BF16)
    SCf = sb("SCf", [128, KC], F32)
    TMPa = sb("TMPa", [128, KC], F32)
    MODC = sb("MODC", [128, 48], F32)
    GA = sb("GAt", [128, KC], F32)
    GB = sb("GBt", [128, KC], F32)
    CA1 = sb("CA1", [128, KC], F32)
    CB1 = sb("CB1", [128, KC], F32)
    NBF = sb("NBF", [128, KC], F32)
    NBI = sb("NBI", [128, KC], F32)
    B2G = sb("B2G", [128, KC], F32)
    IDNF = sb("IDNF", [128, 128], F32)
    STM = sb("STM", [128, 16], F32)
    STMH = sb("STMH", [128, 16], F32)
    VB127 = sb("VB127", [128, 16], F32)
    VBS1 = sb("VBS1", [128, 16], F32)
    VBS2 = sb("VBS2", [128, 16], F32)
    WS1i = sb("WS1i", [128, 1], F32)
    WS1n = sb("WS1n", [128, 1], F32)
    WS2i = sb("WS2i", [128, 1], F32)
    WS2n = sb("WS2n", [128, 1], F32)
    WS1v = sb("WS1v", [128, 1], F32)
    WS2v = sb("WS2v", [128, 1], F32)
    WSrow = sb("WSrow", [1, 8], F32)
    CPF = sb("CPF", [128, 84 + 128], F32)
    CPB = sb("CPB", [128, 256], BF16)

    XS = lambda a, b: X[:, a:b]
    QUU, QAU = XS(0, 2048), XS(2048, 4096)
    QY, QR0 = XS(4096, 6144), XS(6144, 8192)
    QRA, QRB = XS(8192, 10240), XS(10240, 12288)
    QT, QNA = XS(12288, 14336), XS(14336, 16384)
    QSG = RB[:, 0:2048]
    H1Tp = [XS(0, 2048), XS(2048, 4096)]
    TRN2A = XS(4096, 6144)
    TRN2B = XS(6144, 8192)
    HP = [XS(6144, 6656), XS(6656, 7168)]
    H1Qp = [XS(7168, 7680), XS(7680, 8192)]
    W2QS = [XS(8192, 9216), XS(9216, 10240)]
    R0t = [XS(10240, 10752), XS(10752, 11264)]
    Rta = [XS(11264, 11776), XS(11776, 12288)]
    Rtb = [XS(12288, 12800), XS(12800, 13312)]
    Tt = [XS(13312, 13824), XS(13824, 14336)]
    UUt = [XS(14336, 14848), XS(14848, 15360)]
    AUt = [XS(15360, 15872), XS(15872, 16384)]
    NAt = [RB[:, 0:512], RB[:, 512:1024]]
    SGt = [RB[:, 1024:1536], RB[:, 1536:2048]]
    OUTP = [SC[2][:, 0:512], SC[2][:, 512:1024]]

    CVt = CPF[:, 0:8]
    BFVt = CPF[:, 8:16]
    BIVt = CPF[:, 16:24]
    BHVt = CPF[:, 24:32]
    B1Vt = CPF[:, 32:64]
    B2Vt = CPF[:, 64:72]
    CONSTV = CPF[:, 72:84]
    ONESRf = CPF[0:1, 84:212]
    _cvals = [EPS_LN, 1.0, 0.5, -1.0, -0.5, 2.0, 128.0, -127.5, -64.0,
              0.0, 0.0, 0.0]
    for _j, _v in enumerate(_cvals):
        nc.const_aps.register(F32, _v, CONSTV[:, 72 - 72 + _j:_j + 1]
                              if False else CPF[:, 72 + _j:72 + _j + 1])
    ONES128 = CPB[:, 0:128]
    IDN = CPB[:, 128:256]
    ONESRb = CPB[0:1, 0:128]
    ONESCb = CPB[:, 0:1]
    ONE1b = CPB[0:1, 0:1]

    PB = [nc.alloc_psum_tensor(f"PB{i}", [128, NT], F32).ap() for i in range(8)]

    semnames = ["sBR", "sMD", "sM1", "sM2", "sM3", "sLXa", "sLXb", "sCN", "sWMa", "sWMb", "sLWa", "sLWb",
                "sMW", "sW1a", "sW1b", "sW2", "sW2b", "sW2c", "sW2d", "sLWc", "sLWd", "sSP", "sSPa", "sSPb", "sRDa", "sRDb",
                "sFINa", "sFINb", "sMM", "sDV", "sAC", "sPL",
                "sVq", "sSq", "sPq", "sTAP", "sH1a", "sH1b", "sW1c", "sW1d"]
    sems = {s: nc.alloc_semaphore(s) for s in semnames}
    p.sems = sems
    for s in semnames:
        p.sem(s)

    chunk = lambda t, k: t[:, k * L:(k + 1) * L]
    ntile = lambda t, k, j: t[:, k * L + j * NT:k * L + (j + 1) * NT]
    slot = lambda s: MODC[:, s * 8:(s + 1) * 8]

    p.on("scalar", lambda E: E.dma_start(out=CPF[:], in_=cpack_f),
         inc=("sCN", 16), dma=True)
    p.on("scalar", lambda E: E.dma_start(out=CPB[:], in_=cpack_b),
         inc=("sCN", 16), dma=True)
    p.on("scalar", lambda E: E.dma_start(out=IDNF[:], in_=idnf),
         inc=("sCN", 16), dma=True)
    cn_all = p.cnt["sCN"]

    xstg = [SC[0], SC[1]]
    for k in range(KC):
        w = [("sDV", p.cnt["sDV"])] if k >= 2 else []
        sx = "sLXa" if k % 2 == 0 else "sLXb"
        p.on("sync", lambda E, k=k: E.dma_start(out=xstg[k % 2], in_=xT[k]),
             wait=w, inc=(sx, 16))
        p.on("vector", lambda E, k=k: E.tensor_copy(chunk(X, k), xstg[k % 2]),
             wait=[(sx, p.cnt[sx])], inc=("sDV", 1))
    xcast_done = p.cnt["sDV"]

    p.on("scalar", lambda E: E.activation(SCf[:], CVt[:], AF.Silu),
         wait=[("sCN", cn_all)], inc=("sAC", 1))
    ac_silu = p.cnt["sAC"]

    mod_dv_base = p.cnt["sDV"]
    mod_ev = {}
    modstg = [WSTG[:, 0:4096], None]

    def mod_piece(j, k):
        if j % 2 == 0:
            return WSTG[:, k * 512:(k + 1) * 512]
        t = SC[2] if k < 4 else SC[3]
        return t[:, (k % 4) * 512:(k % 4 + 1) * 512]

    def emit_mod_block(j):
        pp = j % 2
        if j >= 2:
            p.on("sync", lambda E, v=j - 1: E.wait_ge(sems["sMD"], v))
        swm = "sWMa" if pp == 0 else "sWMb"
        if pp == 0:
            p.on("sync", lambda E, j=j: E.dma_start(
                out=WSTG[:, 0:4096], in_=wmodT[j]), inc=(swm, 16))
        else:
            p.on("sync", lambda E, j=j: E.dma_start(
                out=SC[2][:], in_=wmodT[j][:, 0:2048]), inc=(swm, 16))
            p.on("sync", lambda E, j=j: E.dma_start(
                out=SC[3][:], in_=wmodT[j][:, 2048:4096]), inc=(swm, 16))
        wm_now = p.cnt[swm]
        for k in range(KC):
            w = []
            if k == 0:
                w = [(swm, wm_now), ("sAC", ac_silu)]
                if j >= 2:
                    w.append(("sDV", mod_ev[j - 1]))
            p.on("tensor", lambda E, j=j, k=k: E.matmul(
                PB[j % 2][0:1, :], SCf[:, k:k + 1], mod_piece(j, k),
                start=(k == 0), stop=(k == KC - 1)),
                wait=w, inc=("sMD", 1) if k == KC - 1 else None)
        mm_now = p.cnt["sMD"]
        if j < 4:
            dstrow = MB[0:1, j * 512:(j + 1) * 512]
        elif j < 8:
            dstrow = RB[0:1, (j - 4) * 512:(j - 4 + 1) * 512]
        else:
            dstrow = M1[0:1, (j - 8) * 512:(j - 8 + 1) * 512]
        p.on("vector", lambda E, d=dstrow, j=j: E.tensor_copy(
            d, PB[j % 2][0:1, :]), wait=[("sMD", mm_now)], inc=("sDV", 1))
        mod_ev[j] = p.cnt["sDV"]


    wabs_stage = [SC[4][:, 0:1024], SC[4][:, 1024:2048],
                  SC[3][:, 0:1024], SC[3][:, 1024:2048]]
    wabs_b = [WSB[:, i * 1024:(i + 1) * 1024] for i in range(4)]
    w2abs_stage = [SC[0], SC[1], WSTG[:, 0:2048], WSTG[:, 2048:4096]]
    w2abs_b = [X2[:, 6144 + i * 1024:6144 + (i + 1) * 1024] for i in range(4)]
    w2actn = {}
    w1actn = {}
    w1mm = {}
    w2mm_abs = {}

    def emit_w1_slab(s):
        if s < 10:
            st = s % 2
            w = [("sAC", w1actn[s - 2])] if s >= 2 else []
        elif s < 12:
            st = s % 4
            w = [("sMD", 12)]
        else:
            st = s % 4
            w = [("sAC", w1actn[s - 4])]
        slw = ["sLWa", "sLWb", "sLWc", "sLWd"][s % 4]
        p.on("gpsimd", lambda E, s=s, st=st: E.dma_start(
            out=wabs_stage[st], in_=w1T[s]),
             wait=w, inc=(slw, 16), dma=True)
        ob = s % 4
        wsMM = [("sM1", w1mm[s - 4])] if s >= 4 else []
        p.on("scalar", lambda E, s=s, st=st, ob=ob: E.activation(
            wabs_b[ob], wabs_stage[st], AF.Abs),
            wait=[(slw, p.cnt[slw])] + wsMM, inc=("sAC", 1))
        acn = p.cnt["sAC"]
        w1actn[s] = acn
        for h in range(2):
            p.on("tensor", lambda E, s=s, h=h, ob=ob: E.matmul(
                PB[4 + h][0:1, :], ONESCb[:],
                wabs_b[ob][:, h * 512:(h + 1) * 512],
                start=(s == 0), stop=(s == HC - 1)),
                wait=[("sAC", acn)] if h == 0 else [],
                inc=("sM1", 1))
        w1mm[s] = p.cnt["sM1"]

    def emit_w2_unit(t):
        if t < 2:
            st = t % 2
            w2w = [("sDV", xcast_done)]
        elif t < 6:
            st = t % 2
            w2w = [("sAC", w2actn[t - 2])]
        elif t < 8:
            st = t % 4
            w2w = [("sMD", 11)]
        else:
            st = t % 4
            w2w = [("sAC", w2actn[t - 4])]
        sw2 = ["sW2", "sW2b", "sW2c", "sW2d"][t % 4]
        p.on("scalar", lambda E, t=t, st=st: E.dma_start(
            out=w2abs_stage[st],
            in_=w2T[t // 2][:, (t % 2) * 2048:(t % 2) * 2048 + 2048]),
            wait=w2w, inc=(sw2, 16), dma=True)
        w2n = p.cnt[sw2]
        for h in range(2):
            a = 2 * t + h
            ob = a % 4
            wv2 = [(sw2, w2n)] if h == 0 else []
            if a >= 4:
                wv2 += [("sM2", w2mm_abs[a - 4])]
            p.on("scalar", lambda E, t=t, h=h, st=st, ob=ob: E.activation(
                w2abs_b[ob], w2abs_stage[st][:, h * 1024:(h + 1) * 1024],
                AF.Abs), wait=wv2, inc=("sAC", 1))
            a2 = p.cnt["sAC"]
            for q in range(2):
                p.on("tensor", lambda E, t=t, h=h, q=q, ob=ob: E.matmul(
                    PB[6][0:1, :], ONESCb[:],
                    w2abs_b[ob][:, q * 512:(q + 1) * 512],
                    start=(t == 0 and h == 0 and q == 0),
                    stop=(t == 15 and h == 1 and q == 1)),
                    wait=[("sAC", a2)] if q == 0 else [],
                    inc=("sM2", 1))
            w2mm_abs[a] = p.cnt["sM2"]
        w2actn[t] = p.cnt["sAC"]

    for j in range(12):
        emit_mod_block(j)
        if j < 10:
            emit_w1_slab(j)
        if j < 6:
            emit_w2_unit(j)
    mod_rows_done = p.cnt["sDV"]

    def modr_piece(t):
        if t < 16:
            return MB[0:1, t * 128:(t + 1) * 128]
        if t < 32:
            return RB[0:1, (t - 16) * 128:(t - 16) * 128 + 128]
        return M1[0:1, (t - 32) * 128:(t - 32) * 128 + 128]

    for t in range(48):
        w = [("sDV", mod_rows_done), ("sCN", cn_all)] if t == 0 else []
        p.on("tensor", lambda E, t=t: E.matmul(
            PB[2][:, t:t + 1], modr_piece(t), ONE1b[:],
            start=True, stop=True), wait=w,
            inc=("sMD", 1) if t == 47 else None)
    mm_modc = p.cnt["sMD"]
    p.on("vector", lambda E: E.tensor_copy(MODC[:], PB[2][:, 0:48]),
         wait=[("sMD", mm_modc)], inc=("sDV", 1))
    modc_done = p.cnt["sDV"]

    def derive(E):
        E.tensor_scalar(CA1[:], slot(1), 1.0, None, ALU.add)
        E.tensor_scalar(CB1[:], slot(4), 1.0, None, ALU.add)
        E.tensor_scalar(GA[:], slot(2), 0.0, 2.0, ALU.is_ge, ALU.mult)
        E.tensor_scalar(GA[:], GA[:], -1.0, None, ALU.add)
        E.tensor_scalar(TMPa[:], slot(2), -1.0, None, ALU.mult)
        E.tensor_tensor(TMPa[:], TMPa[:], slot(2), ALU.max)
        E.tensor_scalar(TMPa[:], TMPa[:], 0.25, None, ALU.max)
        E.tensor_tensor(GA[:], GA[:], TMPa[:], ALU.mult)
        E.tensor_scalar(GB[:], slot(5), 0.0, 2.0, ALU.is_ge, ALU.mult)
        E.tensor_scalar(GB[:], GB[:], -1.0, None, ALU.add)
        E.tensor_scalar(TMPa[:], slot(5), -1.0, None, ALU.mult)
        E.tensor_tensor(TMPa[:], TMPa[:], slot(5), ALU.max)
        E.tensor_scalar(TMPa[:], TMPa[:], 0.25, None, ALU.max)
        E.tensor_tensor(GB[:], GB[:], TMPa[:], ALU.mult)
        E.tensor_scalar(NBF[:], BFVt[:], -1.0, None, ALU.mult)
        E.tensor_scalar(NBI[:], BIVt[:], -1.0, None, ALU.mult)
        return E.tensor_tensor(B2G[:], GB[:], B2Vt[:], ALU.mult)
    p.on("vector", derive, wait=[("sDV", modc_done)], inc=("sDV", 1))

    for r in range(11):
        emit_w1_slab(10 + 2 * r)
        emit_w1_slab(11 + 2 * r)
        if r < 10:
            emit_w2_unit(6 + r)
    mm_w1abs = ("sM1", p.cnt["sM1"])
    mm_w2abs = ("sM2", p.cnt["sM2"])


    def ws1_fin(E):
        E.tensor_reduce(WSrow[0:1, 0:1], PB[4][0:1, :], AX.X, ALU.add)
        E.tensor_reduce(WSrow[0:1, 4:5], PB[5][0:1, :], AX.X, ALU.add)
        E.tensor_tensor(WSrow[0:1, 0:1], WSrow[0:1, 0:1], WSrow[0:1, 4:5],
                        ALU.add)
        E.tensor_scalar(WSrow[0:1, 2:3], WSrow[0:1, 0:1],
                        1.0 / (HID * D), None, ALU.mult)
        return E.tensor_scalar(WSrow[0:1, 2:3], WSrow[0:1, 2:3], EPS_Q,
                               None, ALU.max)
    p.on("vector", ws1_fin, wait=[mm_w1abs], inc=("sDV", 1))
    ws1_done = p.cnt["sDV"]

    def ws2_fin(E):
        E.tensor_reduce(WSrow[0:1, 1:2], PB[6][0:1, :], AX.X, ALU.add)
        E.tensor_scalar(WSrow[0:1, 3:4], WSrow[0:1, 1:2],
                        1.0 / (HID * D), None, ALU.mult)
        return E.tensor_scalar(WSrow[0:1, 3:4], WSrow[0:1, 3:4], EPS_Q,
                               None, ALU.max)
    p.on("vector", ws2_fin, wait=[mm_w2abs], inc=("sDV", 1))
    ws2_done = p.cnt["sDV"]

    p.on("tensor", lambda E: E.matmul(PB[7][:, 0:1], ONESRf[:],
                                      WSrow[0:1, 2:3], start=True, stop=True),
         wait=[("sDV", ws2_done)])
    p.on("tensor", lambda E: E.matmul(PB[7][:, 1:2], ONESRf[:],
                                      WSrow[0:1, 3:4], start=True, stop=True),
         inc=("sMM", 1))
    mm_wsb = p.cnt["sMM"]

    def ws_vecs(E):
        E.tensor_copy(WS1v[:], PB[7][:, 0:1])
        E.tensor_copy(WS2v[:], PB[7][:, 1:2])
        E.reciprocal(WS1i[:], WS1v[:])
        E.reciprocal(WS2i[:], WS2v[:])
        E.tensor_scalar(WS1n[:], WS1i[:], -1.0, None, ALU.mult)
        return E.tensor_scalar(WS2n[:], WS2i[:], -1.0, None, ALU.mult)
    p.on("vector", ws_vecs, wait=[("sMM", mm_wsb)], inc=("sDV", 1))

    for k in range(KC):
        wv = [("sDV", modc_done)] if k == 0 else []
        for j in range(NNT):
            p.on("tensor", lambda E, k=k, j=j: E.matmul(
                PB[j][0:1, :], ONESCb[:], ntile(X, k, j),
                start=(k == 0), stop=(k == KC - 1)),
                wait=wv if j == 0 else [],
                inc=("sM3", 1) if (k == KC - 1 and j == NNT - 1) else None)
    mm_sum1 = p.cnt["sM3"]

    def ln1_mean(E):
        for j in range(NNT):
            E.tensor_scalar(ROW1[0:1, j * 512:(j + 1) * 512],
                            PB[j][0:1, :], 1.0 / D, None, ALU.mult)
        last = None
        for j in range(NNT):
            last = E.tensor_tensor(ROW2[0:1, j * 512:(j + 1) * 512],
                                   ROW1[0:1, j * 512:(j + 1) * 512],
                                   ROW1[0:1, j * 512:(j + 1) * 512],
                                   ALU.mult)
        return last
    p.on("vector", ln1_mean, wait=[("sM3", mm_sum1)], inc=("sDV", 1))

    sq1 = [X2[:, 2048:4096], X2[:, 4096:6144]]
    st1_mm = {}
    for k in range(KC):
        wv = [("sM3", st1_mm[k - 2])] if k >= 2 else []
        p.on("vector", lambda E, k=k: E.tensor_tensor(
            sq1[k % 2], chunk(X, k), chunk(X, k), ALU.mult),
            wait=wv, inc=("sDV", 1))
        sqdv = p.cnt["sDV"]
        for j in range(NNT):
            p.on("tensor", lambda E, k=k, j=j: E.matmul(
                PB[j][0:1, :], ONESCb[:],
                sq1[k % 2][:, j * 512:(j + 1) * 512],
                start=(k == 0), stop=(k == KC - 1)),
                wait=[("sDV", sqdv)] if j == 0 else [],
                inc=("sM3", 1) if j == NNT - 1 else None)
        st1_mm[k] = p.cnt["sM3"]
    mm_sq1 = p.cnt["sM3"]

    def ln1_var(E):
        for j in range(NNT):
            E.tensor_scalar(PB[j][0:1, :], PB[j][0:1, :], 1.0 / D,
                            None, ALU.mult)
        last = None
        for j in range(NNT):
            last = E.tensor_tensor(PB[j][0:1, :], PB[j][0:1, :],
                                   ROW2[0:1, j * 512:(j + 1) * 512],
                                   ALU.subtract)
        return last
    p.on("vector", ln1_var, wait=[("sM3", mm_sq1)], inc=("sDV", 1))

    def ln1_rstd(E):
        for j in range(NNT):
            E.activation(SC[2][0:1, j * 512:(j + 1) * 512], PB[j][0:1, :],
                         AF.Ln, bias=EPS_LN)
        last = None
        for j in range(NNT):
            last = E.activation(ROW2[0:1, j * 512:(j + 1) * 512],
                                SC[2][0:1, j * 512:(j + 1) * 512],
                                AF.Exp, scale=-0.5)
        return last
    p.on("scalar", ln1_rstd, wait=[("sDV", p.cnt["sDV"])], inc=("sAC", 1))
    rstd1_ac = p.cnt["sAC"]

    for j in range(NNT):
        p.on("tensor", lambda E, j=j: E.matmul(
            PB[j][:, :], ONESRb[:], ROW1[0:1, j * 512:(j + 1) * 512],
            start=True, stop=True),
            wait=[("sAC", rstd1_ac)] if j == 0 else [],
            inc=("sM3", 1) if j == NNT - 1 else None)
    bcw1 = p.cnt["sM3"]

    def bc1_evac(E):
        last = None
        for j in range(NNT):
            last = E.activation(MB[:, j * 512:(j + 1) * 512], PB[j][:, :],
                                AF.Identity)
        return last
    p.on("scalar", bc1_evac, wait=[("sM3", bcw1)], inc=("sAC", 1))
    bc1e_ac = p.cnt["sAC"]
    for j in range(NNT):
        p.on("tensor", lambda E, j=j: E.matmul(
            PB[j][:, :], ONESRb[:], ROW2[0:1, j * 512:(j + 1) * 512],
            start=True, stop=True),
            wait=[("sAC", bc1e_ac)] if j == 0 else [],
            inc=("sM3", 1) if j == NNT - 1 else None)
    bcw2 = p.cnt["sM3"]

    def bc2_evac(E):
        last = None
        for j in range(NNT):
            last = E.activation(RB[:, j * 512:(j + 1) * 512], PB[j][:, :],
                                AF.Identity)
        return last
    p.on("scalar", bc2_evac, wait=[("sM3", bcw2)], inc=("sAC", 1))
    p.on("vector", None, wait=[("sAC", p.cnt["sAC"])], inc=("sDV", 1))
    bc1_done = p.cnt["sDV"]

    for k in range(KC):
        wv = [("sDV", bc1_done)] if k == 0 else [("sAC", p.cnt["sAC"])]

        def ln_apply_v(E, k=k, src=X):
            E.tensor_tensor(M1[:], chunk(src, k), MB[:], ALU.subtract)
            return E.tensor_tensor(X2[:, 0:L], M1[:], RB[:], ALU.mult)
        p.on("vector", ln_apply_v, wait=wv, inc=("sDV", 1))
        p.on("scalar", lambda E, k=k: E.activation(
            chunk(Hh, k), X2[:, 0:L], AF.Identity,
            bias=slot(0)[:, k:k + 1], scale=CA1[:, k:k + 1]),
            wait=[("sDV", p.cnt["sDV"])], inc=("sAC", 1))

    sq_slab = [WSB[:, 0:2048], WSB[:, 2048:4096]]

    def ln_stats(src_tile):
        mm_chunk_vals = []
        for k in range(KC):
            wv = []
            if k >= 2:
                wv = [("sMM", mm_chunk_vals[k - 2])]
            p.on("scalar", lambda E, k=k, s=src_tile: E.activation(
                sq_slab[k % 2], chunk(s, k), AF.Square),
                wait=wv, inc=("sAC", 1))
            acn = p.cnt["sAC"]
            for j in range(NNT):
                p.on("tensor", lambda E, k=k, j=j, s=src_tile: E.matmul(
                    PB[j][0:1, :], ONESCb[:], ntile(s, k, j),
                    start=(k == 0), stop=(k == KC - 1)))
            for j in range(NNT):
                p.on("tensor", lambda E, k=k, j=j: E.matmul(
                    PB[4 + j][0:1, :], ONESCb[:],
                    sq_slab[k % 2][:, j * 512:(j + 1) * 512],
                    start=(k == 0), stop=(k == KC - 1)),
                    wait=[("sAC", acn)] if j == 0 else [],
                    inc=("sMM", 1) if j == NNT - 1 else None)
            mm_chunk_vals.append(p.cnt["sMM"])
        return p.cnt["sMM"]

    def ln_rows(E):
        for j in range(NNT):
            E.tensor_scalar(ROW1[0:1, j * 512:(j + 1) * 512],
                            PB[j][0:1, :], 1.0 / D, None, ALU.mult)
        for j in range(NNT):
            E.tensor_scalar(PB[4 + j][0:1, :], PB[4 + j][0:1, :],
                            1.0 / D, None, ALU.mult)
            E.tensor_tensor(ROW2[0:1, j * 512:(j + 1) * 512],
                            ROW1[0:1, j * 512:(j + 1) * 512],
                            ROW1[0:1, j * 512:(j + 1) * 512], ALU.mult)
        last = None
        for j in range(NNT):
            last = E.tensor_tensor(PB[4 + j][0:1, :], PB[4 + j][0:1, :],
                                   ROW2[0:1, j * 512:(j + 1) * 512],
                                   ALU.subtract)
        return last

    def ln_rstd(E):
        for j in range(NNT):
            E.activation(PB[j][0:1, :], PB[4 + j][0:1, :], AF.Ln, bias=EPS_LN)
        last = None
        for j in range(NNT):
            last = E.activation(ROW2[0:1, j * 512:(j + 1) * 512],
                                PB[j][0:1, :], AF.Exp, scale=-0.5)
        return last

    def ln_bcast():
        ac_r = p.cnt["sAC"]
        for j in range(NNT):
            p.on("tensor", lambda E, j=j: E.matmul(
                PB[j][:, :], ONESRb[:], ROW1[0:1, j * 512:(j + 1) * 512],
                start=True, stop=True),
                wait=[("sAC", ac_r)] if j == 0 else [])
            p.on("tensor", lambda E, j=j: E.matmul(
                PB[4 + j][:, :], ONESRb[:], ROW2[0:1, j * 512:(j + 1) * 512],
                start=True, stop=True),
                inc=("sMM", 1) if j == NNT - 1 else None)
        mm_bc = p.cnt["sMM"]

        def bc_evac(E):
            for j in range(NNT):
                E.activation(MB[:, j * 512:(j + 1) * 512], PB[j][:, :],
                             AF.Identity)
            last = None
            for j in range(NNT):
                last = E.activation(RB[:, j * 512:(j + 1) * 512],
                                    PB[4 + j][:, :], AF.Identity)
            return last
        p.on("scalar", bc_evac, wait=[("sMM", mm_bc)], inc=("sAC", 1))
        p.on("vector", None, wait=[("sAC", p.cnt["sAC"])], inc=("sDV", 1))
        return p.cnt["sDV"]

    if DEBUG_TAPS:
        p.on("gpsimd", lambda E: E.dma_start(out=tap_hln, in_=Hh[:]),
             wait=[("sAC", p.cnt["sAC"])], inc=("sTAP", 16), dma=True)
        p.on("gpsimd", lambda E, v=p.cnt["sTAP"]: E.wait_ge(sems["sTAP"], v))
        p.on("gpsimd", lambda E: E.dma_start(out=tap_mod, in_=MODC[:]),
             inc=("sTAP", 16), dma=True)
        p.on("gpsimd", lambda E, v=p.cnt["sTAP"]: E.wait_ge(sems["sTAP"], v))
    p.barrier()

    BH1p = [M1, B127]
    BH2p = [MB, BS1]
    RLp = [SC[3][:], BS2[:]]
    MNp = [SC[4][:], W1Q[:, 0:L]]
    evac_vals = {}
    cast_pls = {}

    chain_ends = {}

    def emit_mixer_mm_evac(e):
        par = e % 2
        for gi, wimg in enumerate((wfT, wiT, whT)):
            wv = []
            if gi == 0 and e >= 1:
                wv = [("sPL", cast_pls[e - 1])]
            p.on("sync", lambda E, gi=gi, w_=wimg, e=e: E.dma_start(
                out=WSTG[:, gi * 1024:(gi + 1) * 1024],
                in_=w_[:, e * 1024:(e + 1) * 1024]),
                wait=wv, inc=("sMW", 16))
        mw_now = p.cnt["sMW"]

        def slab_cast(E, e=e):
            E.tensor_copy(WSB[:, 0:1024], WSTG[:, 0:1024])
            E.tensor_copy(WSB[:, 1024:2048], WSTG[:, 1024:2048])
            return E.tensor_copy(WSB[:, 2048:3072], WSTG[:, 2048:3072])
        cw = [("sMW", mw_now)]
        cw += [("sMM", p.cnt["sMM"])] if e >= 1 else [("sAC", p.cnt["sAC"])]
        p.on("gpsimd", slab_cast, wait=cw, inc=("sPL", 1))
        cast_pls[e] = p.cnt["sPL"]
        cast_done = ("sPL", p.cnt["sPL"])

        for j in range(NNT):
            pp = 3 * (j % 2)
            wv = [cast_done]
            if j >= 2:
                wv += [("sDV", evac_vals[(e, j - 2, "dv")]),
                       ("sAC", evac_vals[(e, j - 2, "ac")])]
            elif e >= 1:
                wv += [("sDV", evac_vals[(e - 1, j + 2, "dv")]),
                       ("sAC", evac_vals[(e - 1, j + 2, "ac")])]
            for gi in range(3):
                for k in range(KC):
                    p.on("tensor", lambda E, e=e, j=j, gi=gi, k=k, pp=pp:
                         E.matmul(PB[pp + gi][:, :],
                                  WSB[:, gi * 1024 + k * 128:
                                      gi * 1024 + (k + 1) * 128],
                                  ntile(Hh, k, j),
                                  start=(k == 0), stop=(k == KC - 1)),
                         wait=wv if (gi == 0 and k == 0) else [],
                         inc=("sMM", 1) if k == KC - 1 else None)
            mm_gate = p.cnt["sMM"]

            def evac_act(E, e=e, j=j, pp=pp, par=par):
                E.activation(BH1p[par][:, j * 512:(j + 1) * 512],
                             PB[pp][:, :],
                             AF.Exp, scale=-1.0, bias=NBF[:, e:e + 1])
                return E.activation(BH2p[par][:, j * 512:(j + 1) * 512],
                                    PB[pp + 1][:, :],
                                    AF.Exp, scale=-1.0, bias=NBI[:, e:e + 1])
            ga_g = [("sDV", g_dvs[e - 2])] if (j == 0 and
                                               (e - 2) in g_dvs) else []
            p.on("scalar", evac_act,
                 wait=[("sMM", mm_gate)] + ga_g, inc=("sAC", 1))
            evac_vals[(e, j, "ac")] = p.cnt["sAC"]

            def evac_dve(E, e=e, j=j, pp=pp, par=par):
                E.tensor_scalar(RLp[par][:, j * 512:(j + 1) * 512],
                                PB[pp + 2][:, :], BHVt[:, e:e + 1], 0.0,
                                ALU.add, ALU.max)
                return E.tensor_scalar(MNp[par][:, j * 512:(j + 1) * 512],
                                       PB[pp + 2][:, :], BHVt[:, e:e + 1],
                                       0.0, ALU.add, ALU.min)
            p.on("vector", evac_dve,
                 wait=[("sMM", mm_gate)], inc=("sDV", 1))
            evac_vals[(e, j, "dv")] = p.cnt["sDV"]

    Ap = [SC[1][:], SC[0][:]]
    Bp = [SC[2][:], RB[:]]

    g_dvs = {}

    def emit_chain_pair(e0):
        es = (e0, e0 + 1)

        def arrs(e):
            par = e % 2
            return (Ap[par], Bp[par], BH1p[par][:], BH2p[par][:],
                    RLp[par], MNp[par])

        t_ac = {}
        t_dv = {}
        for e in es:
            A, B, BH1, BH2, RL, MN = arrs(e)
            w0 = [("sAC", evac_vals[(e, NNT - 1, "ac")]),
                  ("sDV", evac_vals[(e, NNT - 1, "dv")])]
            if (e - 2) in chain_ends:
                w0 += [("sDV", chain_ends[e - 2])]
            p.on("scalar", lambda E, A=A, BH2=BH2: E.activation(
                A, BH2, AF.Ln, bias=1.0), wait=w0, inc=("sAC", 1))
        for e in es:
            A, B, BH1, BH2, RL, MN = arrs(e)
            p.on("scalar", lambda E, B=B, BH1=BH1: E.activation(
                B, BH1, AF.Ln, bias=1.0), inc=("sAC", 1))
            t_ac[e] = p.cnt["sAC"]
        for e in es:
            A, B, BH1, BH2, RL, MN = arrs(e)
            p.on("vector", lambda E, A=A, B=B: E.tensor_tensor(
                A, B, A, ALU.subtract),
                wait=[("sAC", t_ac[e])], inc=("sDV", 1))
            t_dv[e] = p.cnt["sDV"]
        for e in es:
            A, B, BH1, BH2, RL, MN = arrs(e)
            p.on("scalar", lambda E, BH1=BH1, MN=MN: E.activation(
                BH1, MN, AF.Sigmoid), inc=("sAC", 1))
            t_ac[e] = p.cnt["sAC"]
        for e in es:
            A, B, BH1, BH2, RL, MN = arrs(e)
            p.on("scalar", lambda E, A=A, B=B: E.activation(
                B, A, AF.Sigmoid, scale=-1.0),
                wait=[("sDV", t_dv[e])], inc=("sAC", 1))
            t_dv[e] = p.cnt["sAC"]
        for e in es:
            A, B, BH1, BH2, RL, MN = arrs(e)
            p.on("vector", lambda E, BH1=BH1, RL=RL: E.tensor_tensor(
                RL, RL, BH1, ALU.add),
                wait=[("sAC", t_ac[e])], inc=("sDV", 1))
            g_dvs[e] = p.cnt["sDV"]
        for e in es:
            A, B, BH1, BH2, RL, MN = arrs(e)
            p.on("vector", lambda E, A=A, B=B: E.tensor_scalar(
                A, B, -1.0, 1.0, ALU.mult, ALU.add),
                wait=[("sAC", t_dv[e])], inc=("sDV", 1))
        for e in es:
            A, B, BH1, BH2, RL, MN = arrs(e)
            p.on("vector", lambda E, A=A, MN=MN, RL=RL: E.tensor_tensor(
                MN, RL, A, ALU.mult), inc=("sDV", 1))
        for e in es:
            A, B, BH1, BH2, RL, MN = arrs(e)

            def scan_epi(E, e=e, B=B, MN=MN, RL=RL):
                E.tensor_tensor_scan(RL, B, MN, 0.0, ALU.mult, ALU.add)
                return E.scalar_tensor_tensor(
                    chunk(X2, e), RL, GA[:, e:e + 1], chunk(X, e),
                    ALU.mult, ALU.add)
            p.on("vector", scan_epi, inc=("sDV", 1))
            chain_ends[e] = p.cnt["sDV"]

    for ep in range(KC // 2):
        if ep >= 1:
            emit_chain_pair(2 * ep - 2)
        emit_mixer_mm_evac(2 * ep)
        emit_mixer_mm_evac(2 * ep + 1)
    emit_chain_pair(KC - 2)
    p.barrier()

    mm_st2 = ln_stats(X2)
    p.on("vector", ln_rows, wait=[("sMM", mm_st2)], inc=("sDV", 1))
    p.on("scalar", ln_rstd, wait=[("sDV", p.cnt["sDV"])], inc=("sAC", 1))
    bc2_done = ln_bcast()

    for k in range(KC):
        wv = [("sDV", bc2_done)] if k == 0 else []

        def h2_v(E, k=k):
            E.tensor_tensor(QUU, chunk(X2, k), MB[:], ALU.subtract)
            return E.tensor_tensor(QAU, QUU, RB[:], ALU.mult)
        p.on("vector", h2_v, wait=wv, inc=("sDV", 1))
        p.on("scalar", lambda E, k=k: E.activation(
            chunk(Hh, k), QAU, AF.Identity,
            bias=slot(3)[:, k:k + 1], scale=CB1[:, k:k + 1]),
            wait=[("sDV", p.cnt["sDV"])], inc=("sAC", 1))
        acn = p.cnt["sAC"]
        M1n = XS(4096, 6144)
        if k == 0:
            p.on("vector", lambda E, M1n=M1n: (
                E.tensor_copy(M1[:], chunk(Hh, 0)),
                E.tensor_copy(M1n, chunk(Hh, 0)))[-1],
                wait=[("sAC", acn)], inc=("sDV", 1))
        else:
            p.on("vector", lambda E, k=k, M1n=M1n: (
                E.tensor_tensor(M1[:], M1[:], chunk(Hh, k), ALU.max),
                E.tensor_tensor(M1n, M1n, chunk(Hh, k), ALU.min))[-1],
                wait=[("sAC", acn)], inc=("sDV", 1))
    M1n = XS(4096, 6144)
    p.on("vector", lambda E, M1n=M1n: (
        E.tensor_scalar(M1n, M1n, -1.0, None, ALU.mult),
        E.tensor_tensor(M1[:], M1[:], M1n, ALU.max))[-1],
        inc=("sDV", 1))
    m1_done = p.cnt["sDV"]

    def make_scales(stm_tile, first_waits):
        p.on("vector", lambda E: E.tensor_copy(SC[0][:], M1[:]),
             wait=list(first_waits), inc=("sDV", 1))
        m1f_done = p.cnt["sDV"]
        for g4 in range(4):
            tgt = PB[6] if g4 % 2 == 0 else PB[7]
            wv = [("sDV", m1f_done)] if g4 == 0 else (
                [("sDV", p.cnt["sDV"])] if g4 >= 2 else [])
            for b4 in range(4):
                blk = g4 * 4 + b4
                p.on("tensor", lambda E, blk=blk, b4=b4, tgt=tgt: E.transpose(
                    tgt[:, b4 * 128:(b4 + 1) * 128],
                    SC[0][:, blk * 128:(blk + 1) * 128], IDNF[:]),
                    wait=wv if b4 == 0 else [],
                    inc=("sMM", 1) if b4 == 3 else None)
            mmn = p.cnt["sMM"]
            p.on("vector", lambda E, g4=g4, tgt=tgt, stm=stm_tile:
                 E.tensor_reduce(
                     stm[:, g4 * 4:(g4 + 1) * 4],
                     tgt[:, 0:512].rearrange("p (b c) -> p b c", b=4),
                     AX.X, ALU.max, apply_absolute_value=True),
                 wait=[("sMM", mmn)], inc=("sDV", 1))
        p.on("vector", lambda E, stm=stm_tile: E.tensor_scalar(
            stm[:], stm[:], EPS_Q, None, ALU.max), inc=("sDV", 1))
        return p.cnt["sDV"]

    stm_done = make_scales(STM, [("sDV", m1_done)])

    def xscale_vecs(E):
        E.reciprocal(STMH[:], STM[:])
        E.tensor_scalar(VB127[:], STMH[:], QB, None, ALU.mult)
        E.tensor_scalar(STMH[:], STM[:], 1.0 / QB, None, ALU.mult)
        return E.tensor_scalar(VBS1[:], STMH[:], WS1v[:, 0:1], None, ALU.mult)
    p.on("vector", xscale_vecs, wait=[("sDV", stm_done)], inc=("sDV", 1))
    xs_done = p.cnt["sDV"]

    def build_bcast(vec_tile, out_tile, after_dv):
        for g4 in range(4):
            pp = g4 % 2
            wv = [("sDV", after_dv)] if g4 == 0 else []
            if g4 >= 2:
                wv += [("sAC", p.cnt["sAC"])]
            for b4 in range(4):
                blk = g4 * 4 + b4
                p.on("vector", lambda E, blk=blk, b4=b4, v=vec_tile, pp=pp:
                     E.tensor_scalar(
                         WSB[:, (pp * 4 + b4) * 128:(pp * 4 + b4) * 128 + 128],
                         IDN[:], v[:, blk:blk + 1], None, ALU.mult),
                     wait=wv if b4 == 0 else [], inc=("sDV", 1))
                dvn = p.cnt["sDV"]
                p.on("tensor", lambda E, blk=blk, b4=b4, pp=pp: E.matmul(
                    PB[pp][:, b4 * 128:(b4 + 1) * 128], ONES128[:],
                    WSB[:, (pp * 4 + b4) * 128:(pp * 4 + b4) * 128 + 128],
                    start=True, stop=True),
                    wait=[("sDV", dvn)],
                    inc=("sMM", 1) if b4 == 3 else None)
            mmn = p.cnt["sMM"]
            p.on("scalar", lambda E, g4=g4, o=out_tile, pp=pp:
                 E.activation(o[:, g4 * 512:(g4 + 1) * 512],
                              PB[pp][:, 0:512], AF.Identity),
                 wait=[("sMM", mmn)], inc=("sAC", 1))
        p.on("vector", None, wait=[("sAC", p.cnt["sAC"])], inc=("sDV", 1))
        return p.cnt["sDV"]

    b127_done = build_bcast(VB127, B127, xs_done)
    bs1_done = build_bcast(VBS1, BS1, b127_done)

    QSET = [(QUU, QY, QRA, QT, QSG), (QAU, QR0, QRB, QNA, MB[:])]
    xact = {}

    def x_u(k):
        U, Y, RA, T, SG = QSET[k % 2]
        p.on("vector", lambda E, k=k, U=U: E.tensor_tensor(
            U, chunk(Hh, k), B127[:], ALU.mult),
            wait=([("sDV", bs1_done)] if k == 0 else []), inc=("sDV", 1))
        p.on("scalar", lambda E, U=U, Y=Y, SG=SG: (
            E.activation(Y, U, AF.Abs),
            E.activation(SG, U, AF.Sign))[-1],
            wait=[("sDV", p.cnt["sDV"])], inc=("sAC", 1))
        xact[k] = p.cnt["sAC"]

    x_u(0)
    for k in range(KC):
        if k + 1 < KC:
            x_u(k + 1)
        U, Y, RA, T, SG = QSET[k % 2]

        def xdve(E, k=k, Y=Y, RA=RA, T=T, SG=SG):
            E.tensor_scalar(Y, Y, 128.0, None, ALU.add)
            E.tensor_scalar(RA, Y, -127.5, None, ALU.add)
            for bit in (64.0, 32.0, 16.0, 8.0, 4.0, 2.0, 1.0):
                E.tensor_scalar(T, RA, bit, bit, ALU.is_ge, ALU.mult)
                E.tensor_tensor(RA, RA, T, ALU.subtract)
            E.scalar_tensor_tensor(T, RA, -1.0, Y, ALU.mult, ALU.add)
            return E.scalar_tensor_tensor(chunk(Hh, k), T, -127.5, SG,
                                          ALU.add, ALU.mult)
        p.on("vector", xdve, wait=[("sAC", xact[k])], inc=("sDV", 1))
    xq_done = p.cnt["sDV"]
    if DEBUG_TAPS:
        p.on("gpsimd", lambda E: E.dma_start(out=tap_xq, in_=Hh[:]),
             wait=[("sDV", p.cnt["sDV"])], inc=("sTAP", 16), dma=True)
        def tapstm(E):
            return E.dma_start(out=tap_stm[:, 0:16], in_=STM[:])
        p.on("gpsimd", tapstm, inc=("sTAP", 16), dma=True)
        p.on("gpsimd", lambda E, v=p.cnt["sTAP"]: E.wait_ge(sems["sTAP"], v))
        p.on("gpsimd", lambda E: E.dma_start(out=tap_bs[:, 0:L], in_=B127[:]),
             inc=("sTAP", 16), dma=True)
        p.on("gpsimd", lambda E, v=p.cnt["sTAP"]: E.wait_ge(sems["sTAP"], v))
        p.on("gpsimd", lambda E: E.dma_start(out=tap_bs[:, L:2 * L], in_=BS1[:]),
             inc=("sTAP", 16), dma=True)
        p.on("gpsimd", lambda E, v=p.cnt["sTAP"]: E.wait_ge(sems["sTAP"], v))

    w1stage = [WSTG[:, i * 1024:(i + 1) * 1024] for i in range(4)]
    w1q = [W1Q[:, 0:1024], W1Q[:, 1024:2048], RB[:, 0:1024], RB[:, 1024:2048]]
    tern_st = [(SC[1][:, 0:1024], SC[1][:, 1024:2048]),
               (SC[3][:, 0:1024], SC[3][:, 1024:2048])]
    tern_dvs = {}
    tern_pl = {}
    hc_mm = {}
    gelu_acs = []
    ffn1_evacs = []
    spill_sps = []

    def emit_tern(hc):
        sp4 = hc % 4
        wv = []
        if hc >= 4:
            wv = [tern_dvs[hc - 4]]
        sw1 = ["sW1a", "sW1b", "sW1c", "sW1d"][sp4]
        p.on("sync", lambda E, hc=hc, sp4=sp4: E.dma_start(
            out=w1stage[sp4], in_=w1T[hc]), wait=wv, inc=(sw1, 16))
        w1n = p.cnt[sw1]
        tp_, tn_ = tern_st[hc % 2]

        def tern_ts(E, sp4=sp4, tp_=tp_, tn_=tn_):
            E.tensor_scalar(tp_, w1stage[sp4], WS1i[:, 0:1], 0.5,
                            ALU.mult, ALU.is_ge)
            return E.tensor_scalar(tn_, w1stage[sp4], WS1n[:, 0:1], 0.5,
                                   ALU.mult, ALU.is_ge)
        tw = [(sw1, w1n)]
        if hc >= 2:
            tw += [("sPL", tern_pl[hc - 2])]
        p.on("vector", tern_ts, wait=tw, inc=("sDV", 1))
        subw = [("sDV", p.cnt["sDV"])]
        if hc >= 4:
            subw += [("sMM", hc_mm[hc - 4])]
        elif hc >= 2:
            subw += [("sDV", xq_done)]
        p.on("gpsimd", lambda E, hc=hc, tp_=tp_, tn_=tn_: E.tensor_tensor(
            w1q[hc % 4], tp_, tn_, ALU.subtract),
            wait=subw, inc=("sPL", 1))
        tern_pl[hc] = p.cnt["sPL"]
        tern_dvs[hc] = ("sPL", p.cnt["sPL"])

    emit_tern(0)
    emit_tern(1)
    for hc in range(HC):
        if hc + 2 < HC:
            emit_tern(hc + 2)
        for j in range(NNT):
            jp = (hc * NNT + j) % 4
            gidx = hc * NNT + j
            for k in range(KC):
                wv = []
                if k == 0:
                    wv = [tern_dvs[hc]]
                    if gidx == 0:
                        wv += [("sDV", xq_done)]
                    if gidx >= 4:
                        wv += [("sDV", ffn1_evacs[gidx - 4])]
                p.on("tensor", lambda E, hc=hc, j=j, k=k, jp=jp:
                     E.matmul(PB[jp][:, :],
                              w1q[hc % 4][:, k * 128:(k + 1) * 128],
                              ntile(Hh, k, j),
                              start=(k == 0), stop=(k == KC - 1)),
                     wait=wv, inc=("sMM", 1) if k == KC - 1 else None)
            mmn = p.cnt["sMM"]
            hc_mm[hc] = mmn
            p.on("vector", lambda E, j=j, jp=jp: E.tensor_tensor(
                SC[2][:, j * 512:(j + 1) * 512], PB[jp][:, :],
                BS1[:, j * 512:(j + 1) * 512], ALU.mult),
                wait=[("sMM", mmn)], inc=("sDV", 1))
            dvn = p.cnt["sDV"]
            ffn1_evacs.append(dvn)
            p.on("scalar", lambda E, hc=hc, j=j: E.activation(
                H1Tp[hc % 2][:, j * 512:(j + 1) * 512],
                SC[2][:, j * 512:(j + 1) * 512], AF.Gelu,
                bias=B1Vt[:, hc:hc + 1]),
                wait=[("sDV", dvn)], inc=("sAC", 1))
        gelu_acs.append(p.cnt["sAC"])
        H1T = H1Tp[hc % 2]
        if hc == 0:
            p.on("vector", lambda E, H1T=H1T: (
                E.tensor_copy(M1[:], H1T),
                E.tensor_copy(MB[:], H1T))[-1],
                 wait=[("sAC", gelu_acs[-1])], inc=("sDV", 1))
        else:
            p.on("vector", lambda E, H1T=H1T: (
                E.tensor_tensor(M1[:], M1[:], H1T, ALU.max),
                E.tensor_tensor(MB[:], MB[:], H1T, ALU.min))[-1],
                 wait=[("sAC", gelu_acs[-1])], inc=("sDV", 1))
        ssp = "sSPa" if hc % 2 == 0 else "sSPb"
        p.on("gpsimd", lambda E, hc=hc, H1T=H1T: E.dma_start(
            out=h1sp[hc], in_=H1T),
             wait=[("sAC", gelu_acs[-1])], inc=(ssp, 16), dma=True)
        spill_sps.append((ssp, p.cnt[ssp]))
        if DEBUG_TAPS and hc == 0:
            p.on("gpsimd", lambda E: E.dma_start(out=tap_h1, in_=H1Tp[0]),
                 inc=("sTAP", 16), dma=True)
            p.on("gpsimd", lambda E, v=p.cnt["sTAP"]: E.wait_ge(sems["sTAP"], v))
        if hc >= 1:
            sn, v = spill_sps[hc - 1]
            p.on("scalar", lambda E, sn=sn, v=v: E.wait_ge(sems[sn], v))
        if DEBUG_TAPS and hc == 0:
            p.on("scalar", lambda E, v=p.cnt["sTAP"]: E.wait_ge(sems["sTAP"], v))
    p.on("vector", lambda E: (
        E.tensor_scalar(MB[:], MB[:], -1.0, None, ALU.mult),
        E.tensor_tensor(M1[:], M1[:], MB[:], ALU.max))[-1],
        inc=("sDV", 1))
    pl_m1h = ("sDV", p.cnt["sDV"])

    w2stg4 = [SC[1][:], SC[2][:], SC[3][:], WSTG[:, 0:2048]]
    w2sem4 = ["sW2", "sW2b", "sW2c", "sW2d"]
    w2u_dv = {}
    w2spl = {}
    for u in range(2 * KC):
        s, h = u // 2, u % 2
        st = u % 4
        if u < 4:
            wv = [("sSPa", p.cnt["sSPa"]), ("sSPb", p.cnt["sSPb"])]
        else:
            wv = [("sDV", w2u_dv[u - 4])]
        sw = w2sem4[st]
        p.on("sync", lambda E, s=s, h=h, st=st: E.dma_start(
            out=w2stg4[st], in_=w2T[s][:, h * 2048:(h + 1) * 2048]),
            wait=wv, inc=(sw, 16))
        w2n = p.cnt[sw]

        ioff = 4096 + (s % 2) * 4096

        def tern2(E, st=st, h=h, ioff=ioff):
            E.tensor_scalar(W1Q[:], w2stg4[st], WS2i[:, 0:1], 0.5,
                            ALU.mult, ALU.is_ge)
            E.tensor_scalar(RB[:], w2stg4[st], WS2n[:, 0:1], 0.5,
                            ALU.mult, ALU.is_ge)
            return E.tensor_tensor(
                X[:, ioff + h * 2048:ioff + (h + 1) * 2048],
                W1Q[:], RB[:], ALU.subtract)
        tv = [(sw, w2n)]
        if h == 0 and s >= 2:
            tv += [w2spl[s - 2]]
        p.on("vector", tern2, wait=tv, inc=("sDV", 1))
        w2u_dv[u] = p.cnt["sDV"]
        if h == 1:
            ssp_s = "sSP" if s % 2 == 0 else "sSPa"
            p.on("gpsimd", lambda E, s=s, ioff=ioff: E.dma_start(
                out=w2qsp[:, :, s * 128:(s + 1) * 128].rearrange(
                    "k p c -> p k c"),
                in_=X[:, ioff:ioff + 4096].rearrange("p (k c) -> p k c",
                                                     k=HC)),
                wait=[("sDV", w2u_dv[u])], inc=(ssp_s, 16), dma=True)
            w2spl[s] = (ssp_s, p.cnt[ssp_s])
    sp_all = p.cnt["sSP"]

    stmh_done = make_scales(STMH, [pl_m1h])

    def hscale_vecs(E):
        E.reciprocal(STM[:], STMH[:])
        E.tensor_scalar(VB127[:], STM[:], QB, None, ALU.mult)
        E.tensor_scalar(STM[:], STMH[:], 1.0 / QB, None, ALU.mult)
        return E.tensor_scalar(VBS2[:], STM[:], WS2v[:, 0:1], None, ALU.mult)
    p.on("vector", hscale_vecs, wait=[("sDV", stmh_done)], inc=("sDV", 1))
    hs_done = p.cnt["sDV"]
    b127h_done = build_bcast(VB127, B127, hs_done)
    bs2_done = build_bcast(VBS2, BS2, b127h_done)
    if DEBUG_TAPS:
        p.on("gpsimd", lambda E: E.dma_start(out=tap_bs[:, 2 * L:3 * L],
                                             in_=BS2[:]),
             wait=[("sDV", bs2_done)], inc=("sTAP", 16), dma=True)
        p.on("gpsimd", lambda E, v=p.cnt["sTAP"]: E.wait_ge(sems["sTAP"], v))
        p.on("gpsimd", lambda E: E.dma_start(out=tap_stm[:, 16:32],
                                             in_=STMH[:]),
             inc=("sTAP", 16), dma=True)
        p.on("gpsimd", lambda E, v=p.cnt["sTAP"]: E.wait_ge(sems["sTAP"], v))
    p.barrier()

    HHS = lambda a, b: Hh[:, a:b]
    FIN = [HHS(0, 4096), XS(0, 4096)]
    FUU = [HHS(4096, 8192), XS(4096, 8192)]
    FRHO = [HHS(8192, 12288), XS(8192, 12288)]
    FSG = [HHS(12288, 16384), XS(12288, 16384)]
    H1Q2 = [XS(i * 2048, (i + 1) * 2048) for i in range(8)]
    W2R = [M1[:, 0:1024], M1[:, 1024:2048], RB[:, 0:1024], RB[:, 1024:2048],
           MB[:, 0:1024], MB[:, 1024:2048], W1Q[:, 0:1024],
           W1Q[:, 1024:2048]]

    def lad_dve2(E, pf):
        E.tensor_scalar(FIN[pf], FIN[pf], 128.0, None, ALU.add)
        E.tensor_scalar(FRHO[pf], FIN[pf], -127.5, None, ALU.add)
        for bit in (64.0, 32.0, 16.0, 8.0, 4.0, 2.0, 1.0):
            E.tensor_scalar(FUU[pf], FRHO[pf], bit, bit,
                            ALU.is_ge, ALU.mult)
            E.tensor_tensor(FRHO[pf], FRHO[pf], FUU[pf], ALU.subtract)
        E.scalar_tensor_tensor(FUU[pf], FRHO[pf], -1.0, FIN[pf],
                               ALU.mult, ALU.add)
        return E.scalar_tensor_tensor(FIN[pf], FUU[pf], -127.5,
                                      FSG[pf], ALU.add, ALU.mult)

    prev_evac_dv = 0
    bank_free_dv = 0
    last_mm = 0
    mm_vals = {}
    lad_vals = {}
    spl_vals = {}
    LBITS = (32.0, 16.0, 8.0, 4.0, 2.0, 1.0)

    for gi, (d0, d1) in enumerate([(0, 1), (2, 3), (4, 5), (6, 7)]):
        if gi == 0:
            uacts = {}
            udma = {}
            NPAIR = HC // 2

            def emit_uact(pr):
                pf = pr % 2
                srd = "sRDa" if pf == 0 else "sRDb"
                if pr <= 1:
                    wv = [("sSP", sp_all),
                          ("sSPa", p.cnt["sSPa"]), ("sSPb", p.cnt["sSPb"])]
                    if pr == 0:
                        wv += [("sDV", bs2_done)]
                else:
                    wv = [("sMM", mm_vals[(0, pr - 2)]),
                          ("sDV", lad_vals[(0, pr - 2)]),
                          (("sH1a" if pf == 0 else "sH1b"),
                           spl_vals[(0, pr - 2)])]
                p.on("sync", lambda E, pr=pr, pf=pf: E.dma_start(
                    out=FIN[pf].rearrange("p (k c) -> p k c", k=2),
                    in_=h1sp[2 * pr:2 * pr + 2].rearrange("k p c -> p k c")),
                    wait=wv, inc=(srd, 16))
                p.on("sync", lambda E, pr=pr, pf=pf: E.dma_start(
                    out=W2R[2 * pf], in_=w2qsp[2 * pr]), inc=(srd, 16))
                p.on("sync", lambda E, pr=pr, pf=pf: E.dma_start(
                    out=W2R[2 * pf + 1], in_=w2qsp[2 * pr + 1]),
                    inc=(srd, 16))
                udma[pr] = (srd, p.cnt[srd])

            def emit_ulad(pr):
                pf = pr % 2
                p.on("vector", lambda E, pf=pf: (
                    E.tensor_tensor(FUU[pf][:, 0:2048], FIN[pf][:, 0:2048],
                                    B127[:], ALU.mult),
                    E.tensor_tensor(FUU[pf][:, 2048:4096],
                                    FIN[pf][:, 2048:4096],
                                    B127[:], ALU.mult))[-1],
                     wait=[udma[pr]], inc=("sDV", 1))
                p.on("scalar", lambda E, pf=pf: (
                    E.activation(FIN[pf], FUU[pf], AF.Abs),
                    E.activation(FSG[pf], FUU[pf], AF.Sign))[-1],
                     wait=[("sDV", p.cnt["sDV"])], inc=("sAC", 1))
                uacts[pr] = p.cnt["sAC"]

            emit_uact(0)
            emit_ulad(0)
            for pr in range(NPAIR):
                pf = pr % 2
                if pr + 1 < NPAIR:
                    emit_uact(pr + 1)
                p.on("vector", lambda E, pf=pf: (
                    E.tensor_scalar(FIN[pf], FIN[pf], 128.0, None, ALU.add),
                    E.tensor_scalar(FRHO[pf], FIN[pf], -127.5, None,
                                    ALU.add),
                    E.tensor_scalar(FUU[pf], FRHO[pf], 64.0, 64.0,
                                    ALU.is_ge, ALU.mult),
                    E.tensor_tensor(FRHO[pf], FRHO[pf], FUU[pf],
                                    ALU.subtract),
                    E.tensor_scalar(FUU[pf], FRHO[pf], 32.0, 32.0,
                                    ALU.is_ge, ALU.mult),
                    E.tensor_tensor(FRHO[pf], FRHO[pf], FUU[pf],
                                    ALU.subtract),
                    E.tensor_scalar(FUU[pf], FRHO[pf], 16.0, 16.0,
                                    ALU.is_ge, ALU.mult),
                    E.tensor_tensor(FRHO[pf], FRHO[pf], FUU[pf],
                                    ALU.subtract))[-1],
                     wait=[("sAC", uacts[pr])], inc=("sDV", 1))
                if pr + 1 < NPAIR:
                    emit_ulad(pr + 1)

                def dve_tail(E, pf=pf):
                    for bit in (8.0, 4.0, 2.0, 1.0):
                        E.tensor_scalar(FUU[pf], FRHO[pf], bit, bit,
                                        ALU.is_ge, ALU.mult)
                        E.tensor_tensor(FRHO[pf], FRHO[pf], FUU[pf],
                                        ALU.subtract)
                    E.scalar_tensor_tensor(FUU[pf], FRHO[pf], -1.0, FIN[pf],
                                           ALU.mult, ALU.add)
                    return E.scalar_tensor_tensor(FIN[pf], FUU[pf], -127.5,
                                                  FSG[pf], ALU.add, ALU.mult)
                p.on("vector", dve_tail, inc=("sDV", 1))
                lad_vals[(0, pr)] = p.cnt["sDV"]
                sh1 = "sH1a" if pf == 0 else "sH1b"
                p.on("gpsimd", lambda E, pr=pr, pf=pf: E.dma_start(
                    out=h1qsp[2 * pr:2 * pr + 2].rearrange("k p c -> p k c"),
                    in_=FIN[pf].rearrange("p (k c) -> p k c", k=2)),
                    wait=[("sDV", lad_vals[(0, pr)])], inc=(sh1, 16),
                    dma=True)
                spl_vals[(0, pr)] = p.cnt[sh1]
                mmw = [("sDV", lad_vals[(0, pr)])]
                for kch in range(2):
                    for nt in range(NNT):
                        for di, d in enumerate((d0, d1)):
                            p.on("tensor",
                                 lambda E, pf=pf, kch=kch, nt=nt, di=di, d=d,
                                 pr=pr:
                                 E.matmul(
                                     PB[4 * di + nt][:, :],
                                     W2R[2 * pf + kch][:,
                                                       d * 128:(d + 1) * 128],
                                     FIN[pf][:, kch * 2048 + nt * 512:
                                             kch * 2048 + (nt + 1) * 512],
                                     start=(pr == 0 and kch == 0),
                                     stop=(pr == NPAIR - 1 and kch == 1)),
                                 wait=mmw
                                 if (kch == 0 and nt == 0 and di == 0)
                                 else [],
                                 inc=("sMM", 1)
                                 if (kch == 1 and nt == NNT - 1 and di == 1)
                                 else None)
                mm_vals[(0, pr)] = p.cnt["sMM"]
        else:
            for kc in range(HC):
                pp = kc % 8
                srd = ["sRDa", "sRDb", "sW1a", "sW1b",
                       "sW2", "sW2b", "sW2c", "sW2d"][pp]
                wv = []
                if kc < 8:
                    wv = [("sMM", last_mm)]
                    if gi == 1:
                        wv += [("sH1a", p.cnt["sH1a"]),
                               ("sH1b", p.cnt["sH1b"])]
                elif (gi, kc - 8) in mm_vals:
                    wv = [("sMM", mm_vals[(gi, kc - 8)])]
                p.on("sync", lambda E, kc=kc, pp=pp: E.dma_start(
                    out=H1Q2[pp], in_=h1qsp[kc]), wait=wv, inc=(srd, 16))
                p.on("scalar", lambda E, kc=kc, pp=pp: E.dma_start(
                    out=W2R[pp], in_=w2qsp[kc]), wait=wv, inc=(srd, 16),
                    dma=True)
                rdn = p.cnt[srd]
                mmw = [(srd, rdn)]
                if kc == 0:
                    mmw += [("sDV", bank_free_dv)]
                for nt in range(NNT):
                    for di, d in enumerate((d0, d1)):
                        p.on("tensor",
                             lambda E, kc=kc, pp=pp, nt=nt, di=di, d=d:
                             E.matmul(PB[4 * di + nt][:, :],
                                      W2R[pp][:, d * 128:(d + 1) * 128],
                                      H1Q2[pp][:, nt * 512:(nt + 1) * 512],
                                      start=(kc == 0), stop=(kc == HC - 1)),
                             wait=mmw if (nt == 0 and di == 0) else [],
                             inc=("sMM", 1) if (nt == NNT - 1 and di == 1)
                             else None)
                mm_vals[(gi, kc)] = p.cnt["sMM"]
        mm_g = p.cnt["sMM"]
        last_mm = mm_g
        bank_free_trk = None
        for ei in range(8):
            di, nt = ei // 4, ei % 4
            d = (d0, d1)[di]
            pp = ei % 2
            sfin = "sFINa" if pp == 0 else "sFINb"
            wv = [("sMM", mm_g)] if ei == 0 else []
            if ei >= 2 or gi >= 1:
                wv += [(sfin, p.cnt[sfin])]
            p.on("vector", lambda E, di=di, nt=nt, pp=pp: E.tensor_tensor(
                OUTP[pp], PB[4 * di + nt][:, :],
                BS2[:, nt * 512:(nt + 1) * 512], ALU.mult),
                wait=wv, inc=("sDV", 1))
            dvn = p.cnt["sDV"]
            if ei == 7:
                bank_free_trk = dvn
            p.on("scalar", lambda E, d=d, pp=pp: E.activation(
                OUTP[pp], OUTP[pp], AF.Identity,
                bias=B2G[:, d:d + 1], scale=GB[:, d:d + 1]),
                wait=[("sDV", dvn)], inc=("sAC", 1))
            acn = p.cnt["sAC"]
            p.on("vector", lambda E, d=d, nt=nt, pp=pp: E.tensor_tensor(
                OUTP[pp], OUTP[pp],
                X2[:, d * L + nt * 512:d * L + (nt + 1) * 512], ALU.add),
                wait=[("sAC", acn)], inc=("sDV", 1))
            dvn2 = p.cnt["sDV"]
            p.on("gpsimd", lambda E, d=d, nt=nt, pp=pp: E.dma_start(
                out=out_img[d][:, nt * 512:(nt + 1) * 512], in_=OUTP[pp]),
                wait=[("sDV", dvn2)], inc=(sfin, 16), dma=True)
        prev_evac_dv = p.cnt["sDV"]
        bank_free_dv = bank_free_trk

    fa, fb = p.cnt["sFINa"], p.cnt["sFINb"]
    p.on("sync", lambda E: E.wait_ge(sems["sFINa"], fa))
    p.on("sync", lambda E: E.wait_ge(sems["sFINb"], fb))

    with nc.Block() as block:
        @block.sync
        def _(E):
            for f in p.ops["sync"]:
                f(E)

        @block.vector
        def _(E):
            E2 = Ser(E, sems["sVq"])
            for f in p.ops["vector"]:
                f(E2)

        @block.scalar
        def _(E):
            E2 = Ser(E, sems["sSq"])
            for f in p.ops["scalar"]:
                f(E2)

        @block.tensor
        def _(E):
            for f in p.ops["tensor"]:
                f(E)

        @block.gpsimd
        def _(E):
            E2 = Ser(E, sems["sPq"])
            for f in p.ops["gpsimd"]:
                f(E2)

    return nc


_NC_CACHE = None


def kernel(L0_attn_x, L0_attn_W_f, L0_attn_b_f, L0_attn_W_i, L0_attn_b_i,
           L0_attn_W_h, L0_attn_b_h, L0_ffn_W1, L0_ffn_b1, L0_ffn_W2,
           L0_ffn_b2, L0_c, L0_wmod):
    global _NC_CACHE
    import ml_dtypes
    x = np.asarray(L0_attn_x, dtype=np.float32)
    W_f = np.asarray(L0_attn_W_f, dtype=np.float32)
    W_i = np.asarray(L0_attn_W_i, dtype=np.float32)
    W_h = np.asarray(L0_attn_W_h, dtype=np.float32)
    b_f = np.asarray(L0_attn_b_f, dtype=np.float32)
    b_i = np.asarray(L0_attn_b_i, dtype=np.float32)
    b_h = np.asarray(L0_attn_b_h, dtype=np.float32)
    W1 = np.asarray(L0_ffn_W1, dtype=np.float32)
    b1 = np.asarray(L0_ffn_b1, dtype=np.float32)
    W2 = np.asarray(L0_ffn_W2, dtype=np.float32)
    b2 = np.asarray(L0_ffn_b2, dtype=np.float32)
    cmat = np.asarray(L0_c, dtype=np.float32)
    wmod = np.asarray(L0_wmod, dtype=np.float32)

    def lay_lhsT(W, mout, kin):
        return np.ascontiguousarray(
            W.reshape(mout, 128, kin, 128).transpose(3, 0, 2, 1)
            .reshape(128, mout * kin * 128))

    shared = {
        "wfT": lay_lhsT(W_f, KC, KC),
        "wiT": lay_lhsT(W_i, KC, KC),
        "whT": lay_lhsT(W_h, KC, KC),
        "w1T": np.ascontiguousarray(
            W1.reshape(HC, 128, KC, 128).transpose(0, 3, 2, 1)
            .reshape(HC, 128, KC * 128)),
        "w2T": np.ascontiguousarray(
            W2.reshape(KC, 128, HC, 128).transpose(0, 3, 2, 1)
            .reshape(KC, 128, HC * 128)),
        "wmodT": np.ascontiguousarray(
            wmod.reshape(KC, 128, 12, 512).transpose(2, 1, 0, 3)
            .reshape(12, 128, KC * 512)),
        "idnf": np.eye(128, dtype=np.float32),
        "cpack_b": np.concatenate([
            np.ones((128, 128), ml_dtypes.bfloat16),
            np.eye(128, dtype=ml_dtypes.bfloat16)], axis=1),

    }
    constv = np.broadcast_to(np.array(
        [1e-5, 1.0, 0.5, -1.0, -0.5, 2.0, 128.0, -127.5, -64.0,
         0.0, 0.0, 0.0], np.float32), (128, 12))
    base_pack = np.concatenate([
        np.zeros((128, 8), np.float32),
        np.ascontiguousarray(b_f.reshape(KC, 128).T),
        np.ascontiguousarray(b_i.reshape(KC, 128).T),
        np.ascontiguousarray(b_h.reshape(KC, 128).T),
        np.ascontiguousarray(b1.reshape(HC, 128).T),
        np.ascontiguousarray(b2.reshape(KC, 128).T),
        constv,
        np.ones((128, 128), np.float32)], axis=1)
    in_maps = []
    for b in range(B):
        m = dict(shared)
        m["xT"] = np.ascontiguousarray(x[b].T).reshape(KC, 128, L)
        pk = base_pack.copy()
        pk[:, 0:8] = cmat[b].reshape(KC, 128).T
        m["cpack_f"] = pk
        in_maps.append(m)

    if _NC_CACHE is None:
        _NC_CACHE = build_nc()
    res = run_bass_kernel_spmd(_NC_CACHE, in_maps, core_ids=list(range(B)))
    outs = []
    for b in range(B):
        oi = np.asarray(res.results[b]["out_img"], dtype=np.float32)
        outs.append(np.ascontiguousarray(oi.reshape(D, L).T))
    return np.stack(outs, axis=0)
